# version 07
# baseline (speedup 1.0000x reference)
import numpy as np
import ml_dtypes
import concourse.bass as bass
import concourse.mybir as mybir
from concourse.bass_utils import run_bass_kernel_spmd

F32 = mybir.dt.float32
BF16 = mybir.dt.bfloat16
I32 = mybir.dt.int32
AF = mybir.ActivationFunctionType
ALU = mybir.AluOpType
AX = mybir.AxisListType

NB = 4
S = 4096
D = 2048
R = 4
HID = 5632
TQ = 2048
NORM = 1.0 / np.sqrt(128.0)
EPS = 1e-5
NSTRIP = 16
QB = 256
NQB = TQ // QB
NKC = S // 128
HC = HID // 128
INV_D = 1.0 / D
SB_BASE = 18432
SBUF_LIMIT = 229120


class Sem:
    def __init__(self, nc, es, name):
        self.name = name
        self.h = es.enter_context(nc.semaphore(name))
        self.v = 0

    def bump(self, n):
        self.v += n
        return self.v


class Prog:
    ENGS = ("sync", "tensor", "vector", "scalar", "gpsimd")

    def __init__(self):
        self.streams = {e: [] for e in self.ENGS}
        self.hwm = {e: {} for e in self.ENGS}

    def emit(self, eng, fn, waits=(), incs=()):
        ws = []
        for w in waits:
            if w is None:
                continue
            s, v = w
            if v is None:
                continue
            if self.hwm[eng].get(s.name, -1) >= v:
                continue
            self.hwm[eng][s.name] = v
            ws.append((s, v))
        self.streams[eng].append((tuple(ws), fn, tuple(incs)))

    def run(self, eng_name, E):
        for ws, fn, incs in self.streams[eng_name]:
            for s, v in ws:
                E.wait_ge(s.h, v)
            if fn is None:
                continue
            ins = fn(E)
            for s, n in incs:
                ins = ins.then_inc(s.h, n)


def build_nc(phase="full"):
    nc = bass.Bass("TRN2", target_bir_lowering=False)
    from contextlib import ExitStack
    es = ExitStack()

    x_all = nc.dram_tensor("x_all", [NB * S, D], F32, kind="ExternalInput").ap()
    x_own = nc.dram_tensor("x_own", [TQ, D], F32, kind="ExternalInput").ap()
    wqkv_s = nc.dram_tensor("wqkv_s", [R * 384, D], F32, kind="ExternalInput").ap()
    wproj = nc.dram_tensor("wproj", [D, D], F32, kind="ExternalInput").ap()
    bproj = nc.dram_tensor("bproj", [D], F32, kind="ExternalInput").ap()
    wq_gate = nc.dram_tensor("wq_gate", [HID, D], I32, kind="ExternalInput").ap()
    wq_up = nc.dram_tensor("wq_up", [HID, D], I32, kind="ExternalInput").ap()
    wq_down = nc.dram_tensor("wq_down", [D, HID], I32, kind="ExternalInput").ap()
    s_gate2 = nc.dram_tensor("s_gate2", [128, HC], F32, kind="ExternalInput").ap()
    s_up2 = nc.dram_tensor("s_up2", [128, HC], F32, kind="ExternalInput").ap()
    s_down2 = nc.dram_tensor("s_down2", [128, 16], F32, kind="ExternalInput").ap()
    ident_in = nc.dram_tensor("ident", [128, 128], BF16, kind="ExternalInput").ap()
    out_d = nc.dram_tensor("out", [TQ, D], F32, kind="ExternalOutput").ap()

    _k = lambda p: "ExternalOutput" if phase == p else "Internal"
    ctx_d = nc.dram_tensor("ctx_d", [TQ, D], BF16, kind=_k("A")).ap()
    h_d = nc.dram_tensor("h_d", [HID, TQ], BF16, kind=_k("B2")).ap()
    y1_d = nc.dram_tensor("y1_d", [TQ, D], F32, kind=_k("B1")).ap()
    fo_d = nc.dram_tensor("fo_d", [TQ, D], BF16, kind=_k("B3")).ap()

    OFF = [SB_BASE]

    def sb(name, shape, dtype):
        nbytes = int(np.prod(shape[1:])) * mybir.dt.size(dtype)
        o = (OFF[0] + 31) // 32 * 32
        h = nc.alloc_sbuf_tensor_at(name, shape, dtype, offset=o)
        OFF[0] = o + nbytes
        assert OFF[0] <= SBUF_LIMIT, f"sbuf overflow at {name}: {OFF[0]}"
        return h

    identS = sb("identS", [128, 128], BF16)
    ones128 = sb("ones128", [128, 128], BF16)
    sg2 = sb("sg2", [128, HC], F32)
    su2 = sb("su2", [128, HC], F32)
    sd2 = sb("sd2", [128, 16], F32)
    nbg = sb("nbg", [128, HC], F32)
    nbu = sb("nbu", [128, HC], F32)
    nbd = sb("nbd", [128, 16], F32)
    stats = sb("stats", [128, 16], F32)
    ssq2 = sb("ssq2", [128, 2], F32)
    maxK = sb("maxK", [128, 64], F32)
    maxQ = sb("maxQ", [128, 32], F32)
    gtmp = sb("gtmp", [128, 8], F32)
    dcol = sb("dcol", [128, 2], F32)
    rdcol = sb("rdcol", [128, 2], F32)
    sff = sb("sff", [128, 16], F32)
    sff2 = sb("sff2", [128, 16], F32)
    PERS_END = OFF[0]

    wqkvT = sb("wqkvT", [128, 16 * 1536], BF16)
    Kst = sb("Kst", [128, R * S], BF16)
    Vst = sb("Vst", [128, 32 * 520], BF16)
    Qst = sb("Qst", [128, R * TQ], BF16)
    Pt = [sb("Pt0", [128, NKC * QB], BF16), sb("Pt1", [128, NKC * QB], BF16)]
    xin = [sb("xin0", [128, 2048], F32), sb("xin1", [128, 2048], F32)]
    xln = [sb("xln0", [128, 2048], BF16), sb("xln1", [128, 2048], BF16)]
    xT = [sb("xT0", [128, 16 * 256], BF16), sb("xT1", [128, 16 * 256], BF16)]
    Ksqb = sb("Ksqb", [128, 512], BF16)
    ctxs = [sb("ctxs0", [128, 128], BF16), sb("ctxs1", [128, 128], BF16)]
    assert OFF[0] <= SBUF_LIMIT

    OFF[0] = PERS_END
    junk = sb("junk", [128, 2048], BF16)
    BJ_END = OFF[0]
    WprojT = sb("WprojT", [128, 16 * 2048], BF16)
    zT = sb("zT", [128, 16 * 2048], BF16)
    bpb = sb("bpb", [128, 2048], F32)
    B_COMMON_END = OFF[0]

    B1W_START = OFF[0]
    wpl_ = sb("wpl0", [128, 2048], F32)
    wpl = [wpl_, wpl_]
    ctxin = [sb("ctxin0", [128, 2048], BF16), sb("ctxin1", [128, 2048], BF16)]
    ctxTb = [sb("ctxT0", [128, 2048], BF16), sb("ctxT1", [128, 2048], BF16)]
    attnout = sb("attnout", [128, 2048], F32)
    xo_ = sb("xo0", [128, 2048], F32)
    xo = [xo_, xo_]
    y1t = sb("y1t", [128, 2048], F32)
    zt = [sb("zt0", [128, 2048], BF16), sb("zt1", [128, 2048], BF16)]
    ln3t = sb("ln3t", [128, 2048], BF16)
    wplb = ln3t
    assert OFF[0] <= SBUF_LIMIT

    OFF[0] = BJ_END
    wqg = [sb("wqg0", [128, 2048], I32), sb("wqg1", [128, 2048], I32)]
    wgd = sb("wgd", [128, 2048], BF16)
    wud = sb("wud", [128, 2048], BF16)
    WgT = sb("WgT", [128, 2048], BF16)
    WuT = sb("WuT", [128, 2048], BF16)
    gbuf = sb("gbuf", [128, 2048], BF16)
    tbuf = sb("tbuf", [128, 2048], BF16)
    ubuf = sb("ubuf", [128, 2048], BF16)
    abuf = sb("abuf", [128, 2048], BF16)
    hbuf = [sb("hbuf0", [128, 2048], BF16), sb("hbuf1", [128, 2048], BF16)]
    assert OFF[0] <= BJ_END + 16 * 2048 * 2, "B2 must fit inside WprojT region"
    OFF[0] = B1W_START
    wqgu = [sb("wqgu0", [128, 2048], I32), sb("wqgu1", [128, 2048], I32)]

    OFF[0] = BJ_END
    WdT = sb("WdT", [128, HC * 1024], BF16)
    wqd = [sb("wqd0", [128, 2816], I32), sb("wqd1", [128, 2816], I32)]
    wdq = sb("wdq", [128, HID], BF16)
    hld = [sb("hld0", [128, HC * 128], BF16), sb("hld1", [128, HC * 128], BF16)]
    fob = [sb("fob0", [128, 512], BF16), sb("fob1", [128, 512], BF16)]
    fin = [sb("fin0", [128, 2048], BF16), sb("fin1", [128, 2048], BF16)]
    y1in = [sb("y1in0", [128, 2048], F32), sb("y1in1", [128, 2048], F32)]
    yout = [sb("yout0", [128, 2048], F32), sb("yout1", [128, 2048], F32)]
    assert OFF[0] <= SBUF_LIMIT

    pbT = [nc.alloc_psum_tensor(f"pbT{i}", [128, 1024], F32) for i in range(2)]
    pb4 = nc.alloc_psum_tensor("pb4", [128, 512], F32)
    pb5 = nc.alloc_psum_tensor("pb5", [128, 512], F32)
    pbS = nc.alloc_psum_tensor("pbS", [128, 1024], F32)

    def PB(slot, a=0, b=512):
        if slot == 4:
            return pb4[:, a:b]
        if slot == 5:
            return pb5[:, a:b]
        if slot == 6:
            return pbS[:, a:b]
        return pbS[:, 512 + a:512 + b]

    sWA = Sem(nc, es, "sWA")
    sX = [Sem(nc, es, "sX0"), Sem(nc, es, "sX1")]
    sWPL = [Sem(nc, es, "sWPL0"), Sem(nc, es, "sWPL1")]
    sXO = Sem(nc, es, "sXO")
    sBPB = Sem(nc, es, "sBPB")
    sCI = [Sem(nc, es, "sCI0"), Sem(nc, es, "sCI1")]
    sCW = [Sem(nc, es, "sCW0"), Sem(nc, es, "sCW1")]
    sWQ = [Sem(nc, es, "sWQ0"), Sem(nc, es, "sWQ1")]
    sWU = [Sem(nc, es, "sWU0"), Sem(nc, es, "sWU1")]
    sWD = [Sem(nc, es, "sWD0"), Sem(nc, es, "sWD1")]
    sHW = [Sem(nc, es, "sHW0"), Sem(nc, es, "sHW1")]
    sHL = [Sem(nc, es, "sHL0"), Sem(nc, es, "sHL1")]
    sFW = [Sem(nc, es, "sFW0"), Sem(nc, es, "sFW1")]
    sFI = [Sem(nc, es, "sFI0"), Sem(nc, es, "sFI1")]
    sYI = [Sem(nc, es, "sYI0"), Sem(nc, es, "sYI1")]
    sY1W = Sem(nc, es, "sY1W")
    sOW = [Sem(nc, es, "sOW0"), Sem(nc, es, "sOW1")]
    sTE = Sem(nc, es, "sTE")
    sVE = Sem(nc, es, "sVE")
    sSE = Sem(nc, es, "sSE")
    sGE = Sem(nc, es, "sGE")

    P = Prog()

    def dma(eng, out, in_, sem, waits=()):
        P.emit(eng, lambda E, o=out, i=in_: E.dma_start(out=o, in_=i),
               waits=waits, incs=((sem, 16),))
        return sem.bump(16)

    def te(fn, waits=(), inc=False):
        P.emit("tensor", fn, waits=waits, incs=((sTE, 1),))
        return sTE.bump(1)

    def ve(fn, waits=(), inc=False):
        P.emit("vector", fn, waits=waits, incs=((sVE, 1),) if inc else ())
        return sVE.bump(1) if inc else None

    def se(fn, waits=(), inc=False):
        P.emit("scalar", fn, waits=waits, incs=((sSE, 1),) if inc else ())
        return sSE.bump(1) if inc else None

    def ge(fn, waits=(), inc=False):
        P.emit("gpsimd", fn, waits=waits, incs=((sGE, 1),) if inc else ())
        return sGE.bump(1) if inc else None

    def vdr():
        P.emit("vector", lambda E: E.drain())

    def sdr():
        P.emit("scalar", lambda E: E.drain())

    bank_free = [None] * 8

    trot = [0]

    def transpose_tile_set(srcbuf, dstfn, banks_unused, waits_first,
                           extra_evac_wait=None, ndc=16, koff=0):
        vt = None
        done = 0
        while done < ndc:
            g = min(8, ndc - done)
            bank = trot[0] % 2
            trot[0] += 1
            for q in range(g):
                dc = done + q
                w = []
                if done == 0 and q == 0:
                    w = list(waits_first)
                if q == 0:
                    w = w + [bank_free[bank]]
                vt = te(lambda E, dc=dc, bank=bank, q=q:
                        E.matmul(out=pbT[bank][:, q * 128:(q + 1) * 128],
                                 lhsT=srcbuf[:, koff + dc * 128: koff + (dc + 1) * 128],
                                 rhs=identS[:], start=True, stop=True),
                        waits=w)
            ew = [(sTE, vt)]
            if extra_evac_wait is not None:
                ew.append(extra_evac_wait)
            vv = ve(lambda E, dst=dstfn(done, g), bank=bank, g=g:
                    E.tensor_copy(out=dst,
                                  in_=pbT[bank][:].rearrange(
                                      "p (c t) -> p c t", t=128)[:, 0:g, :]),
                    waits=ew, inc=True)
            bank_free[bank] = (sVE, vv)
            done += g
        return vt, sVE.v

    def norm_mm(rhs_ap, waits, width=256):
        bank = trot[0] % 2
        trot[0] += 1
        vt = te(lambda E, bank=bank, width=width:
                E.matmul(out=pbT[bank][:, 0:width], lhsT=ones128[:], rhs=rhs_ap,
                         start=True, stop=True),
                waits=list(waits) + [bank_free[bank]])
        return bank, vt

    v_ident = dma("sync", identS[:], ident_in, sWA)
    v_sg = dma("sync", sg2[:], s_gate2, sWA)
    v_su = dma("sync", su2[:], s_up2, sWA)
    v_sd = dma("sync", sd2[:], s_down2, sWA)

    ve(lambda E: E.memset(ones128[:], 1.0))
    ve(lambda E: E.tensor_scalar_mul(out=nbg[:], in0=sg2[:], scalar1=-128.0),
       waits=[(sWA, 64)])
    ve(lambda E: E.tensor_scalar_mul(out=nbu[:], in0=su2[:], scalar1=-128.0))
    v_nb = ve(lambda E: E.tensor_scalar_mul(out=nbd[:], in0=sd2[:], scalar1=-128.0),
              inc=True)

    xin_free = [None, None]
    xln_free = [None, None]
    for ob in range(12):
        p = ob % 2
        vl = dma("sync", xin[p][:], wqkv_s[ob * 128:(ob + 1) * 128, :], sX[p],
                 waits=[xin_free[p]])
        vc = ve(lambda E, p=p: E.tensor_copy(out=xln[p][:], in_=xin[p][:]),
                waits=[(sX[p], vl), xln_free[p], (sWA, 64)], inc=True)
        xin_free[p] = (sVE, vc)

        def dst4(dc0, g, ob=ob):
            return wqkvT[:].rearrange("p (dc o) -> p dc o", o=1536)[
                :, dc0:dc0 + g, ob * 128:(ob + 1) * 128]
        vt, _ = transpose_tile_set(xln[p], dst4, None,
                                   waits_first=[(sVE, vc)])
        xln_free[p] = (sTE, vt)
    v_wqkvT = sVE.v

    kst_free = None
    vst_free = None
    pt_free = [None, None]
    ctxs_free = [None, None]
    stats_free = [None, None]
    xT_free = [None, None]
    exp_reg_rel = {0: None, 1: None}
    ctx_bank_rel = {4: None, 5: None}
    gbias_free = None
    ksqb_free = [None, None]

    def ln_chain(tile_idx, src_ap, dst_ap, data_ready, dst_free=None, scr=None):
        if scr is None:
            scr = junk[:]
        st = tile_idx % 2
        c = st * 8
        ssum = stats[:, c + 0:c + 1]
        ssq = stats[:, c + 1:c + 2]
        smu = stats[:, c + 2:c + 3]
        smu2 = stats[:, c + 3:c + 4]
        snm2e = stats[:, c + 4:c + 5]
        srstd = stats[:, c + 5:c + 6]
        snegr = stats[:, c + 6:c + 7]
        snmr = stats[:, c + 7:c + 8]
        wfree = [stats_free[st]] + ([dst_free] if dst_free else [])
        vq = se(lambda E: E.activation(out=scr, in_=src_ap, func=AF.Square,
                                       accum_out=ssq),
                waits=list(data_ready) + wfree, inc=True)
        vs = ve(lambda E: E.reduce_sum(out=ssum, in_=src_ap, axis=AX.X),
                waits=list(data_ready) + wfree, inc=True)
        v1 = se(lambda E: E.mul(out=smu, in_=ssum, mul=INV_D),
                waits=[(sVE, vs), (sSE, vq)], inc=True)
        v2 = se(lambda E: E.activation(out=smu2, in_=smu, func=AF.Square),
                waits=[(sSE, v1)], inc=True)
        v3 = se(lambda E: E.activation(out=snm2e, in_=smu2, func=AF.Identity,
                                       bias=np.float32(EPS),
                                       scale=np.float32(-1.0)),
                waits=[(sSE, v2)], inc=True)
        v4 = se(lambda E: E.activation(out=srstd, in_=ssq, func=AF.Ln,
                                       bias=snm2e, scale=np.float32(INV_D)),
                waits=[(sSE, v3)], inc=True)
        v5 = se(lambda E: E.activation(out=srstd, in_=srstd, func=AF.Exp,
                                       bias=np.float32(0.0),
                                       scale=np.float32(-0.5)),
                waits=[(sSE, v4)], inc=True)
        v6 = se(lambda E: E.mul(out=snegr, in_=srstd, mul=-1.0),
                waits=[(sSE, v5)], inc=True)
        v7 = se(lambda E: E.activation(out=snmr, in_=smu, func=AF.Identity,
                                       bias=np.float32(0.0), scale=snegr),
                waits=[(sSE, v6)], inc=True)
        v8 = se(lambda E: E.activation(out=dst_ap, in_=src_ap, func=AF.Identity,
                                       bias=snmr, scale=srstd),
                waits=[(sSE, v7)], inc=True)
        stats_free[st] = (sSE, v8)
        return v8

    for a in range(NB):
        v_vmset = ve(lambda E: E.memset(Vst[:], 1.0), waits=[vst_free], inc=True)
        for n in range(NSTRIP):
            p = n % 2
            row0 = a * S + n * 256
            for k in range(2):
                tp = (n * 2 + k) % 2
                vl = dma("sync", xin[tp][:],
                         x_all[row0 + k * 128: row0 + (k + 1) * 128, :], sX[tp],
                         waits=[xin_free[tp]])
                v_ln = ln_chain(n * 2 + k, xin[tp][:], xln[tp][:],
                                [(sX[tp], vl)], dst_free=xln_free[tp],
                                scr=xln[tp][:])
                xin_free[tp] = (sSE, v_ln)

                def dst4(dc0, g, p=p, k=k):
                    return xT[p][:].rearrange("p (dc t) -> p dc t", t=256)[
                        :, dc0:dc0 + g, k * 128:(k + 1) * 128]
                vt, _ = transpose_tile_set(xln[tp], dst4, None,
                                           waits_first=[(sSE, v_ln)],
                                           extra_evac_wait=xT_free[p])
                xln_free[tp] = (sTE, vt)
            xT_ready = sVE.v

            for r in range(R):
                bank = 4 + r
                for dc in range(16):
                    w = []
                    if dc == 0:
                        w = [(sVE, xT_ready), bank_free[bank]]
                    vt = te(lambda E, r=r, dc=dc, bank=bank, p=p:
                            E.matmul(out=PB(bank, 0, 256),
                                     lhsT=wqkvT[:, dc * 1536 + r * 384 + 128:
                                                dc * 1536 + r * 384 + 256],
                                     rhs=xT[p][:, dc * 256:(dc + 1) * 256],
                                     start=(dc == 0), stop=(dc == 15)),
                            waits=w, inc=(dc == 15))
                vek = ve(lambda E, r=r, n=n, bank=bank:
                         E.tensor_copy(out=Kst[:, r * S + n * 256: r * S + (n + 1) * 256],
                                       in_=PB(bank, 0, 256)),
                         waits=[(sTE, vt), kst_free], inc=True)
                bank_free[bank] = (sVE, vek)
                if n % 2 == 1:
                    vsq = ve(lambda E, r=r, n=n:
                             E.tensor_mul(out=Ksqb[:],
                                          in0=Kst[:, r * S + (n - 1) * 256:
                                                 r * S + (n + 1) * 256],
                                          in1=Kst[:, r * S + (n - 1) * 256:
                                                 r * S + (n + 1) * 256]),
                             waits=[(sVE, vek), ksqb_free[0]], inc=True)
                    nbank, vn = norm_mm(Ksqb[:], [(sVE, vsq)], 512)
                    vm = ve(lambda E, r=r, n=n, nbank=nbank:
                            E.reduce_max(out=maxK[:, r * 8 + n // 2:
                                               r * 8 + n // 2 + 1],
                                         in_=pbT[nbank][:, 0:512], axis=AX.X),
                            waits=[(sTE, vn)], inc=True)
                    bank_free[nbank] = (sVE, vm)
                    ksqb_free[0] = (sTE, vn)

            if n < 8:
                for r in range(R):
                    bank = 4 + r
                    for dc in range(16):
                        w = []
                        if dc == 0:
                            w = [bank_free[bank]]
                        vt = te(lambda E, r=r, dc=dc, bank=bank, p=p:
                                E.matmul(out=PB(bank, 0, 256),
                                         lhsT=wqkvT[:, dc * 1536 + r * 384:
                                                    dc * 1536 + r * 384 + 128],
                                         rhs=xT[p][:, dc * 256:(dc + 1) * 256],
                                         start=(dc == 0), stop=(dc == 15)),
                                waits=w, inc=(dc == 15))
                    veq = ve(lambda E, r=r, n=n, bank=bank:
                             E.tensor_copy(out=Qst[:, r * TQ + n * 256:
                                                  r * TQ + (n + 1) * 256],
                                           in_=PB(bank, 0, 256)),
                             waits=[(sTE, vt), kst_free], inc=True)
                    bank_free[bank] = (sVE, veq)
                    if n % 2 == 1:
                        vsq = ve(lambda E, r=r, n=n:
                                 E.tensor_mul(out=Ksqb[:],
                                              in0=Qst[:, r * TQ + (n - 1) * 256:
                                                     r * TQ + (n + 1) * 256],
                                              in1=Qst[:, r * TQ + (n - 1) * 256:
                                                     r * TQ + (n + 1) * 256]),
                                 waits=[(sVE, veq), ksqb_free[1]], inc=True)
                        nbank, vn = norm_mm(Ksqb[:], [(sVE, vsq)], 512)
                        vm = ve(lambda E, r=r, n=n, nbank=nbank:
                                E.reduce_max(out=maxQ[:, r * 4 + n // 2:
                                                   r * 4 + n // 2 + 1],
                                             in_=pbT[nbank][:, 0:512], axis=AX.X),
                                waits=[(sTE, vn)], inc=True)
                        bank_free[nbank] = (sVE, vm)
                        ksqb_free[1] = (sTE, vn)

            for k in range(2):
                bank = 4 + k
                for r in range(R):
                    for dc in range(16):
                        w = []
                        if r == 0 and dc == 0:
                            w = [(sVE, v_vmset), bank_free[bank]]
                        vt = te(lambda E, k=k, r=r, dc=dc, bank=bank, p=p:
                                E.matmul(out=PB(bank, r * 128, (r + 1) * 128),
                                         lhsT=xT[p][:, dc * 256 + k * 128:
                                                   dc * 256 + k * 128 + 128],
                                         rhs=wqkvT[:, dc * 1536 + r * 384 + 256:
                                                   dc * 1536 + r * 384 + 384],
                                         start=(dc == 0), stop=(dc == 15)),
                                waits=w, inc=(r == 3 and dc == 15))
                tc = n * 2 + k
                dstv = Vst[:].rearrange("p (tc r x) -> p tc r x", r=4, x=130)[
                    :, tc, :, 0:128]
                vev = ve(lambda E, dstv=dstv, bank=bank:
                         E.tensor_copy(out=dstv,
                                       in_=PB(bank).rearrange(
                                           "p (r x) -> p r x", x=128)),
                         waits=[(sTE, vt)], inc=True)
                bank_free[bank] = (sVE, vev)
            xT_free[p] = (sTE, sTE.v)

        vst_ready = sVE.v
        v_norm_done = sVE.v
        for r in range(R):
            i_block = 4 * a + r
            vk1 = ve(lambda E, r=r: E.reduce_max(out=gtmp[:, 0:1],
                                                 in_=maxK[:, r * 8:(r + 1) * 8],
                                                 axis=AX.X),
                     waits=[gbias_free, (sVE, v_norm_done)], inc=True)
            vk2 = ve(lambda E, r=r: E.reduce_max(out=gtmp[:, 1:2],
                                                 in_=maxQ[:, r * 4:(r + 1) * 4],
                                                 axis=AX.X), inc=True)
            v1 = ve(lambda E: E.tensor_mul(out=gtmp[:, 2:3], in0=gtmp[:, 0:1],
                                           in1=gtmp[:, 1:2]),
                    waits=[(sVE, vk2)], inc=True)
            vln_g = se(lambda E: E.activation(out=gtmp[:, 3:4], in_=gtmp[:, 2:3],
                                              func=AF.Ln),
                       waits=[(sVE, v1)], inc=True)
            v2 = se(lambda E: E.activation(out=gtmp[:, 4:5], in_=gtmp[:, 3:4],
                                           func=AF.Exp, bias=np.float32(0.0),
                                           scale=np.float32(0.5)),
                    waits=[(sSE, vln_g)], inc=True)
            vbias = ve(lambda E: E.tensor_scalar_mul(out=gtmp[:, 5:6],
                                                     in0=gtmp[:, 4:5],
                                                     scalar1=-float(NORM)),
                       waits=[(sSE, v2)], inc=True)

            for qb in range(NQB):
                ptp = qb % 2
                for m2 in range(8):
                    reg = m2 % 2
                    rten = pbS if reg == 0 else pbT[1]
                    rslot = 6 if reg == 0 else 1
                    for kco in range(4):
                        kc = 4 * m2 + kco
                        w = [exp_reg_rel[reg]]
                        if m2 in (0, 1) and kco == 0:
                            w += [bank_free[rslot]]
                            if reg == 0:
                                w += [bank_free[7]]
                        if kc == 0:
                            w += [(sVE, vst_ready), (sVE, vbias)]
                        vt = te(lambda E, r=r, kc=kc, qb=qb, rten=rten:
                                E.matmul(out=rten[:, (kc % 4) * 256:
                                                  (kc % 4) * 256 + 256],
                                         lhsT=Kst[:, r * S + kc * 128:
                                                r * S + (kc + 1) * 128],
                                         rhs=Qst[:, r * TQ + qb * 256:
                                                r * TQ + (qb + 1) * 256],
                                         start=True, stop=True),
                                waits=w)
                    wx = [(sTE, vt)]
                    if m2 == 0:
                        wx += [pt_free[ptp]]
                    vxp = se(lambda E, m2=m2, rten=rten, ptp=ptp:
                             E.activation(out=Pt[ptp][:, m2 * 1024:
                                                     (m2 + 1) * 1024],
                                          in_=rten[:], func=AF.Exp,
                                          bias=gtmp[:, 5:6],
                                          scale=np.float32(NORM)),
                             waits=wx, inc=True)
                    exp_reg_rel[reg] = (sSE, vxp)
                exp_done = sSE.v
                for qt in range(2):
                    bank = 4 + qt
                    for kc in range(NKC):
                        w = []
                        if kc == 0:
                            w = [(sSE, exp_done), ctx_bank_rel[bank]]
                        vt = te(lambda E, r=r, qt=qt, kc=kc, bank=bank, ptp=ptp:
                                E.matmul(out=PB(bank, 0, 129),
                                         lhsT=Pt[ptp][:, kc * 256 + qt * 128:
                                                     kc * 256 + qt * 128 + 128],
                                         rhs=Vst[:, kc * 520 + r * 130:
                                                 kc * 520 + r * 130 + 129],
                                         start=(kc == 0), stop=(kc == NKC - 1)),
                                waits=w, inc=(kc == NKC - 1))
                    vd = ve(lambda E, bank=bank, qt=qt:
                            E.tensor_copy(out=dcol[:, qt:qt + 1],
                                          in_=PB(bank, 128, 129)),
                            waits=[(sTE, vt)], inc=True)
                    vr = ve(lambda E, qt=qt:
                            E.reciprocal(out=rdcol[:, qt:qt + 1],
                                         in_=dcol[:, qt:qt + 1]),
                            waits=[(sVE, vd)], inc=True)
                    tile = qb * 2 + qt
                    vnorm = se(lambda E, bank=bank, qt=qt:
                               E.activation(out=ctxs[qt][:],
                                            in_=PB(bank, 0, 128),
                                            func=AF.Identity,
                                            bias=np.float32(0.0),
                                            scale=rdcol[:, qt:qt + 1]),
                               waits=[(sVE, vr), ctxs_free[qt]], inc=True)
                    ctx_bank_rel[bank] = (sSE, vnorm)
                    vctx = dma("scalar",
                               ctx_d[tile * 128:(tile + 1) * 128,
                                     i_block * 128:(i_block + 1) * 128],
                               ctxs[qt][:], sCW[qt], waits=[(sSE, vnorm)])
                    ctxs_free[qt] = (sCW[qt], vctx)
                pt_free[ptp] = (sTE, sTE.v)
            gbias_free = (sSE, sSE.v)
        kst_free = (sTE, sTE.v)
        vst_free = (sTE, sTE.v)
        bank_free[4] = ctx_bank_rel[4]
        bank_free[5] = ctx_bank_rel[5]
        bank_free[6] = exp_reg_rel[0]
        bank_free[7] = exp_reg_rel[0]
        bank_free[1] = exp_reg_rel[1]

    if phase == "A":
        P.emit("sync", None, waits=[(sCW[0], sCW[0].v), (sCW[1], sCW[1].v)])
        return _finish(nc, P, es)
    barrier = [(sTE, sTE.v), (sVE, sVE.v), (sSE, sSE.v), (sGE, sGE.v),
               (sCW[0], sCW[0].v), (sCW[1], sCW[1].v)]

    rep_src = bass.AP(bproj.tensor, 0, [[0, 128], [1, 2048]])
    v_bpb = dma("sync", bpb[:], rep_src, sBPB, waits=barrier)

    zoff = zT.manual_sbuf_range[0]
    wplA = [nc.alloc_sbuf_tensor_at("wplA0", [128, 4 * 2048], F32, offset=zoff),
            nc.alloc_sbuf_tensor_at("wplA1", [128, 4 * 2048], F32,
                                    offset=zoff + 4 * 2048 * 4)]
    wplA_free = [None, None]
    wplb_free = None
    for blk in range(4):
        bp = blk % 2
        wsrc = bass.AP(wproj.tensor, blk * 512 * 2048,
                       [[2048, 128], [128 * 2048, 4], [1, 2048]])
        wdst = bass.AP(wplA[bp], 0, [[8192, 128], [2048, 4], [1, 2048]])
        vl = dma("sync", wdst, wsrc, sWPL[bp],
                 waits=barrier + [wplA_free[bp]])
        for t in range(4):
            ob = blk * 4 + t
            vc = ve(lambda E, bp=bp, t=t:
                    E.tensor_copy(out=wplb[:],
                                  in_=wplA[bp][:, t * 2048:(t + 1) * 2048]),
                    waits=barrier + [(sWPL[bp], vl), wplb_free], inc=True)

            def dst4(dc0, g, ob=ob):
                return WprojT[:].rearrange("p (dc o) -> p dc o", o=2048)[
                    :, dc0:dc0 + g, ob * 128:(ob + 1) * 128]
            vt, _ = transpose_tile_set(wplb, dst4, None,
                                       waits_first=barrier + [(sVE, vc)])
            wplb_free = (sTE, vt)
        wplA_free[bp] = (sVE, sVE.v)
    v_wprojT = sVE.v
    b0_ln3_guard = wplb_free

    all_ctx0 = sCW[0].v
    all_ctx1 = sCW[1].v
    ctxin_free = [None, None]
    ctxT_free = [None, None]
    xo_free = None
    attnout_free = None
    y1t_free = []
    zt_free = [None, None]
    zt_pending = []
    ln3_free = b0_ln3_guard
    vT_ready = {}

    def b1_stage1(tt, extra):
        p = tt % 2
        vl = dma("sync", ctxin[p][:], ctx_d[tt * 128:(tt + 1) * 128, :],
                 sCI[p], waits=extra + [ctxin_free[p]])

        def dst4(dc0, g, p=p):
            return ctxTb[p][:].rearrange("p (dc t) -> p dc t", t=128)[
                :, dc0:dc0 + g, :]
        vt, _ = transpose_tile_set(ctxin[p], dst4, None,
                                   waits_first=[(sCI[p], vl)],
                                   extra_evac_wait=ctxT_free[p])
        ctxin_free[p] = (sTE, vt)
        vT_ready[tt] = sVE.v

    b1_stage1(0, barrier + [(sCW[0], all_ctx0), (sCW[1], all_ctx1)])
    for tt in range(16):
        p = tt % 2
        if tt + 1 < 16:
            b1_stage1(tt + 1, [])
        vxo = dma("sync", xo[p][:], x_own[tt * 128:(tt + 1) * 128, :], sXO,
                  waits=barrier + [xo_free])
        for dob in range(4):
            bank = 4 + dob
            for dc in range(16):
                w = []
                if dc == 0:
                    w = [(sVE, vT_ready[tt]), (sVE, v_wprojT), bank_free[bank]]
                vt = te(lambda E, p=p, dob=dob, dc=dc, bank=bank:
                        E.matmul(out=PB(bank),
                                 lhsT=ctxTb[p][:, dc * 128:(dc + 1) * 128],
                                 rhs=WprojT[:, dc * 2048 + dob * 512:
                                           dc * 2048 + (dob + 1) * 512],
                                 start=(dc == 0), stop=(dc == 15)),
                        waits=w, inc=(dc == 15))
            vev = ve(lambda E, dob=dob, bank=bank:
                     E.tensor_tensor(out=attnout[:, dob * 512:(dob + 1) * 512],
                                     in0=PB(bank),
                                     in1=bpb[:, dob * 512:(dob + 1) * 512],
                                     op=ALU.add),
                     waits=[(sTE, vt), (sBPB, v_bpb), attnout_free], inc=True)
            bank_free[bank] = (sVE, vev)
        v_ao = sVE.v
        ctxT_free[p] = (sTE, sTE.v)
        v_ln3 = ln_chain(tt, attnout[:], ln3t[:], [(sVE, v_ao)],
                         dst_free=ln3_free)
        attnout_free = (sSE, v_ln3)
        vy1 = ve(lambda E, p=p: E.tensor_add(out=y1t[:], in0=xo[p][:],
                                             in1=ln3t[:]),
                 waits=[(sSE, v_ln3), (sXO, vxo)] + y1t_free, inc=True)
        ln3_free = (sVE, vy1)
        xo_free = (sVE, vy1)
        vw1 = dma("scalar", y1_d[tt * 128:(tt + 1) * 128, :], y1t[:], sY1W,
                  waits=[(sVE, vy1)])
        v_ln2 = ln_chain(tt, y1t[:], zt[tt % 2][:], [(sVE, vy1)],
                         dst_free=zt_free[tt % 2])
        y1t_free = [(sY1W, vw1), (sSE, v_ln2)]
        zt_pending.append((tt, v_ln2))
        if len(zt_pending) == 2 or tt == 15:
            for ptt, pv in (zt_pending if tt == 15 else zt_pending[:1]):
                def dst4(dc0, g, ptt=ptt):
                    return zT[:].rearrange("p (dc t) -> p dc t", t=2048)[
                        :, dc0:dc0 + g, ptt * 128:(ptt + 1) * 128]
                vt, _ = transpose_tile_set(zt[ptt % 2], dst4, None,
                                           waits_first=[(sSE, pv)])
                zt_free[ptt % 2] = (sTE, vt)
            zt_pending = zt_pending[1:] if tt != 15 else []
    v_zT = sVE.v
    v_y1_written = sY1W.v
    if phase == "B1":
        P.emit("sync", None, waits=[(sY1W, v_y1_written)])
        return _finish(nc, P, es)
    b2barrier = [(sTE, sTE.v), (sVE, sVE.v), (sSE, sSE.v), (sGE, sGE.v),
                 (sY1W, v_y1_written)]

    wqg_free = [None, None]
    wgd_free = None
    wud_free = None
    WgT_free = None
    WuT_free = None
    gbuf_free = None
    ubuf_free = None
    tbuf_free = None
    abuf_free = None
    hbuf_free = [None, None]
    wqgu_free = [None, None]
    dqv = {}

    def b2_dq(hc, extra):
        nonlocal wgd_free, wud_free
        p = hc % 2
        vg = dma("sync", wqg[p][:], wq_gate[hc * 128:(hc + 1) * 128, :],
                 sWQ[p], waits=extra + [wqg_free[p]])
        vu = dma("sync", wqgu[p][:], wq_up[hc * 128:(hc + 1) * 128, :],
                 sWU[p], waits=extra + [wqgu_free[p]])
        vdq_g = se(lambda E, p=p, hc=hc:
                   E.activation(out=wgd[:], in_=wqg[p][:], func=AF.Identity,
                                bias=nbg[:, hc:hc + 1], scale=sg2[:, hc:hc + 1]),
                   waits=extra + [(sWQ[p], vg), (sVE, v_nb), wgd_free],
                   inc=True)
        wqg_free[p] = (sSE, vdq_g)
        vdq_u = se(lambda E, p=p, hc=hc:
                   E.activation(out=wud[:], in_=wqgu[p][:], func=AF.Identity,
                                bias=nbu[:, hc:hc + 1], scale=su2[:, hc:hc + 1]),
                   waits=[(sWU[p], vu), wud_free], inc=True)
        wqgu_free[p] = (sSE, vdq_u)
        dqv[hc] = (vdq_g, vdq_u)

    b2_dq(0, b2barrier)
    for hc in range(HC):
        vdq_g, vdq_u = dqv[hc]

        def dstg(dc0, g):
            return WgT[:].rearrange("p (dc t) -> p dc t", t=128)[
                :, dc0:dc0 + g, :]
        vtg, _ = transpose_tile_set(wgd, dstg, None,
                                    waits_first=[(sSE, vdq_g), WgT_free])
        wgd_free = (sTE, vtg)
        vWgT = sVE.v

        def dstu(dc0, g):
            return WuT[:].rearrange("p (dc t) -> p dc t", t=128)[
                :, dc0:dc0 + g, :]
        vtu, _ = transpose_tile_set(wud, dstu, None,
                                    waits_first=[(sSE, vdq_u), WuT_free])
        wud_free = (sTE, vtu)
        vWuT = sVE.v
        if hc + 1 < HC:
            b2_dq(hc + 1, [])

        for tb in range(4):
            bank = 4 + tb
            for dc in range(16):
                w = []
                if dc == 0:
                    w = [(sVE, vWgT), (sVE, v_zT), bank_free[bank]]
                vt = te(lambda E, dc=dc, tb=tb, bank=bank:
                        E.matmul(out=PB(bank),
                                 lhsT=WgT[:, dc * 128:(dc + 1) * 128],
                                 rhs=zT[:, dc * 2048 + tb * 512:
                                        dc * 2048 + (tb + 1) * 512],
                                 start=(dc == 0), stop=(dc == 15)),
                        waits=w, inc=(dc == 15))
            vev = ve(lambda E, tb=tb, bank=bank:
                     E.tensor_copy(out=gbuf[:, tb * 512:(tb + 1) * 512],
                                   in_=PB(bank)),
                     waits=[(sTE, vt), gbuf_free], inc=True)
            bank_free[bank] = (sVE, vev)
        WgT_free = (sTE, vt)
        v_g = sVE.v
        for tb in range(4):
            bank = 4 + tb
            for dc in range(16):
                w = []
                if dc == 0:
                    w = [(sVE, vWuT), bank_free[bank]]
                vt = te(lambda E, dc=dc, tb=tb, bank=bank:
                        E.matmul(out=PB(bank),
                                 lhsT=WuT[:, dc * 128:(dc + 1) * 128],
                                 rhs=zT[:, dc * 2048 + tb * 512:
                                        dc * 2048 + (tb + 1) * 512],
                                 start=(dc == 0), stop=(dc == 15)),
                        waits=w, inc=(dc == 15))
            vev = ve(lambda E, tb=tb, bank=bank:
                     E.tensor_copy(out=ubuf[:, tb * 512:(tb + 1) * 512],
                                   in_=PB(bank)),
                     waits=[(sTE, vt), ubuf_free], inc=True)
            bank_free[bank] = (sVE, vev)
        WuT_free = (sTE, vt)
        v_u = sVE.v
        v_t = se(lambda E: E.activation(out=tbuf[:], in_=gbuf[:], func=AF.Tanh,
                                        bias=np.float32(0.0),
                                        scale=np.float32(0.5)),
                 waits=[(sVE, v_g), tbuf_free], inc=True)
        va = ve(lambda E: E.tensor_mul(out=abuf[:], in0=gbuf[:], in1=tbuf[:]),
                waits=[(sSE, v_t), abuf_free], inc=True)
        vhp = ve(lambda E: E.tensor_add(out=abuf[:], in0=abuf[:], in1=gbuf[:]),
                 waits=[(sVE, va)], inc=True)
        gbuf_free = (sVE, vhp)
        tbuf_free = (sVE, vhp)
        vh = ve(lambda E, p=hc % 2: E.scalar_tensor_tensor(out=hbuf[p][:],
                                                      in0=abuf[:], scalar=0.5,
                                                      in1=ubuf[:],
                                                      op0=ALU.mult,
                                                      op1=ALU.mult),
                waits=[(sVE, v_u), (sVE, vhp), hbuf_free[hc % 2]], inc=True)
        abuf_free = (sVE, vh)
        ubuf_free = (sVE, vh)
        vwh = dma("scalar", h_d[hc * 128:(hc + 1) * 128, :], hbuf[hc % 2][:],
                  sHW[hc % 2], waits=[(sVE, vh)])
        hbuf_free[hc % 2] = (sHW[hc % 2], vwh)
    v_hw = [sHW[0].v, sHW[1].v]
    if phase == "B2":
        P.emit("sync", None, waits=[(sHW[0], v_hw[0]), (sHW[1], v_hw[1]),
                                    (sY1W, v_y1_written)])
        return _finish(nc, P, es)
    b3barrier = [(sTE, sTE.v), (sVE, sVE.v), (sSE, sSE.v),
                 (sHW[0], v_hw[0]), (sHW[1], v_hw[1]), (sY1W, v_y1_written)]

    ve(lambda E: E.memset(sff[:], 0.0), waits=b3barrier)
    v_sffz = ve(lambda E: E.memset(sff2[:], 0.0), inc=True)
    wqd_free = [None, None]
    wdq_free_box = [None]
    WdT_half_free = [None, None]
    sff_add_rel = None
    hld_free = [None, None]
    fob_free = [None, None]
    acc_free = None
    wdt_ready = {}

    def b3_build(u, extra):
        half = u % 2
        for dt in range(4):
            row0 = u * 512 + dt * 128
            sidx = u * 4 + dt
            vl0 = dma("sync", wqd[0][:], wq_down[row0:row0 + 128, 0:2816],
                      sWD[0], waits=extra + [wqd_free[0]])
            vl1 = dma("sync", wqd[1][:], wq_down[row0:row0 + 128, 2816:5632],
                      sWD[1], waits=extra + [wqd_free[1]])
            vdq0 = se(lambda E, sidx=sidx:
                      E.activation(out=wdq[:, 0:2816], in_=wqd[0][:],
                                   func=AF.Identity, bias=nbd[:, sidx:sidx + 1],
                                   scale=sd2[:, sidx:sidx + 1]),
                      waits=extra + [(sWD[0], vl0), wdq_free_box[0]],
                      inc=True)
            vdq1 = se(lambda E, sidx=sidx:
                      E.activation(out=wdq[:, 2816:5632], in_=wqd[1][:],
                                   func=AF.Identity, bias=nbd[:, sidx:sidx + 1],
                                   scale=sd2[:, sidx:sidx + 1]),
                      waits=[(sWD[1], vl1)], inc=True)
            wqd_free[0] = (sSE, vdq0)
            wqd_free[1] = (sSE, vdq1)

            def dstw(dc0, g, dt=dt, half=half):
                return WdT[:].rearrange("p (hcx x) -> p hcx x", x=1024)[
                    :, dc0:dc0 + g, half * 512 + dt * 128:
                    half * 512 + (dt + 1) * 128]
            vt, _ = transpose_tile_set(wdq, dstw, None,
                                       waits_first=[(sSE, vdq1),
                                                    WdT_half_free[half]],
                                       ndc=HC)
            wdq_free_box[0] = (sTE, vt)
        wdt_ready[u] = sVE.v

    b3_build(0, b3barrier)
    for u in range(4):
        half = u % 2
        if u + 1 < 4:
            b3_build(u + 1, [])
        for tt in range(16):
            p = tt % 2
            hsrc = bass.AP(h_d.tensor, tt * 128,
                           [[TQ, 128], [128 * TQ, HC], [1, 128]])
            hdst = bass.AP(hld[p], 0, [[HC * 128, 128], [128, HC], [1, 128]])
            vlh = dma("sync", hdst, hsrc, sHL[p],
                      waits=[(sHW[0], v_hw[0]), (sHW[1], v_hw[1]),
                             hld_free[p]])
            bank = 4 + ((u * 16 + tt) % 4)
            dcol0 = u * 512
            for hcc in range(HC):
                w = []
                if hcc == 0:
                    w = [(sHL[p], vlh), (sVE, wdt_ready[u]), bank_free[bank]]
                vt = te(lambda E, p=p, hcc=hcc, half=half, bank=bank:
                        E.matmul(out=PB(bank),
                                 lhsT=hld[p][:, hcc * 128:(hcc + 1) * 128],
                                 rhs=WdT[:, hcc * 1024 + half * 512:
                                        hcc * 1024 + (half + 1) * 512],
                                 start=(hcc == 0), stop=(hcc == HC - 1)),
                        waits=w, inc=(hcc == HC - 1))
            vps = ve(lambda E, bank=bank:
                     E.reduce_sum(out=gtmp[:, 6:7], in_=PB(bank), axis=AX.X),
                     waits=[(sTE, vt), (sVE, v_sffz), sff_add_rel], inc=True)
            sff_add_v = ve(lambda E, tt=tt:
                           E.tensor_add(out=sff[:, tt:tt + 1],
                                        in0=sff[:, tt:tt + 1],
                                        in1=gtmp[:, 6:7]),
                           waits=[(sVE, vps)], inc=True)
            sff_add_rel = (sVE, sff_add_v)
            fp = tt % 2
            vev = ve(lambda E, bank=bank, fp=fp:
                     E.tensor_copy(out=fob[fp][:], in_=PB(bank)),
                     waits=[fob_free[fp]], inc=True)
            vsq = se(lambda E, bank=bank:
                     E.activation(out=junk[:, 0:512], in_=PB(bank),
                                  func=AF.Square, accum_out=gtmp[:, 0:1]),
                     waits=[(sVE, vev), acc_free], inc=True)
            vacc = ve(lambda E, tt=tt:
                      E.tensor_add(out=sff2[:, tt:tt + 1],
                                   in0=sff2[:, tt:tt + 1],
                                   in1=gtmp[:, 0:1]),
                      waits=[(sSE, vsq)], inc=True)
            acc_free = (sVE, vacc)
            bank_free[bank] = (sSE, vsq)
            vwf = dma("scalar", fo_d[tt * 128:(tt + 1) * 128,
                                     dcol0:dcol0 + 512], fob[fp][:],
                      sFW[fp], waits=[(sVE, vev)])
            fob_free[fp] = (sFW[fp], vwf)
            hld_free[p] = (sTE, sTE.v)
        WdT_half_free[half] = (sTE, sTE.v)
    v_fw = [sFW[0].v, sFW[1].v]
    v_b3_v = sVE.v
    if phase == "B3":
        P.emit("sync", None, waits=[(sFW[0], v_fw[0]), (sFW[1], v_fw[1]),
                                    (sY1W, v_y1_written)])
        return _finish(nc, P, es)

    fin_free = [None, None]
    y1in_free = [None, None]
    yout_free = [None, None]
    junk_free = None
    for tt in range(16):
        p = tt % 2
        vlf = dma("sync", fin[p][:], fo_d[tt * 128:(tt + 1) * 128, :], sFI[p],
                  waits=[(sFW[0], v_fw[0]), (sFW[1], v_fw[1]), fin_free[p]])
        vly = dma("sync", y1in[p][:], y1_d[tt * 128:(tt + 1) * 128, :], sYI[p],
                  waits=[(sY1W, v_y1_written), y1in_free[p]])
        st = tt % 2
        c = st * 8
        smu = stats[:, c + 2:c + 3]
        svar = stats[:, c + 3:c + 4]
        slnv = stats[:, c + 4:c + 5]
        srstd = stats[:, c + 5:c + 6]
        srstdv = stats[:, c + 6:c + 7]
        snmr = stats[:, c + 7:c + 8]
        wstat = [(sVE, v_b3_v), stats_free[st]]
        vsm = ve(lambda E, tt=tt, smu=smu:
                 E.tensor_scalar_mul(out=smu, in0=sff[:, tt:tt + 1],
                                     scalar1=INV_D),
                 waits=wstat, inc=True)
        vsv = ve(lambda E, tt=tt, svar=svar:
                 E.tensor_scalar_mul(out=svar, in0=sff2[:, tt:tt + 1],
                                     scalar1=INV_D), inc=True)
        vmm = ve(lambda E, smu=smu: E.tensor_mul(out=gtmp[:, 1:2], in0=smu,
                                                 in1=smu),
                 waits=[(sVE, vsm)], inc=True)
        v1 = ve(lambda E, svar=svar:
                E.scalar_tensor_tensor(out=svar, in0=svar, scalar=EPS,
                                       in1=gtmp[:, 1:2], op0=ALU.add,
                                       op1=ALU.subtract),
                waits=[(sVE, vmm)], inc=True)
        vln4 = se(lambda E, svar=svar, slnv=slnv:
                  E.activation(out=slnv, in_=svar, func=AF.Ln),
                  waits=[(sVE, v1)], inc=True)
        v2 = se(lambda E, slnv=slnv, srstd=srstd:
                E.activation(out=srstd, in_=slnv, func=AF.Exp,
                             bias=np.float32(0.0), scale=np.float32(-0.5)),
                waits=[(sSE, vln4)], inc=True)
        ve(lambda E, srstd=srstd, srstdv=srstdv:
           E.tensor_copy(out=srstdv, in_=srstd), waits=[(sSE, v2)])
        v3 = ve(lambda E, smu=smu, srstd=srstd, snmr=snmr:
                E.scalar_tensor_tensor(out=snmr, in0=smu, scalar=-1.0,
                                       in1=srstd, op0=ALU.mult, op1=ALU.mult),
                inc=True)
        v4 = se(lambda E, p=p, snmr=snmr, srstdv=srstdv:
                E.activation(out=junk[:], in_=fin[p][:], func=AF.Identity,
                             bias=snmr, scale=srstdv),
                waits=[(sVE, v3), (sFI[p], vlf), junk_free], inc=True)
        stats_free[st] = (sSE, v4)
        fin_free[p] = (sSE, v4)
        vy2 = ve(lambda E, p=p: E.tensor_add(out=yout[p][:], in0=y1in[p][:],
                                             in1=junk[:]),
                 waits=[(sSE, v4), (sYI[p], vly), yout_free[p]], inc=True)
        junk_free = (sVE, vy2)
        y1in_free[p] = (sVE, vy2)
        vst = dma("scalar", out_d[tt * 128:(tt + 1) * 128, :], yout[p][:],
                  sOW[p], waits=[(sVE, vy2)])
        yout_free[p] = (sOW[p], vst)

    P.emit("sync", None, waits=[(sOW[0], sOW[0].v), (sOW[1], sOW[1].v)])
    return _finish(nc, P, es)


def _finish(nc, P, es):
    with nc.Block() as block:
        @block.sync
        def _(E):
            P.run("sync", E)

        @block.tensor
        def _(E):
            P.run("tensor", E)

        @block.vector
        def _(E):
            P.run("vector", E)

        @block.scalar
        def _(E):
            P.run("scalar", E)

        @block.gpsimd
        def _(E):
            P.run("gpsimd", E)

    es.close()
    return nc


def kernel(L0_attn_query, L0_attn_key, L0_attn_value, L0_attn_Wqkv,
           L0_attn_Wproj, L0_attn_bproj, L0_ffn_wq_gate, L0_ffn_s_gate,
           L0_ffn_wq_up, L0_ffn_s_up, L0_ffn_wq_down, L0_ffn_s_down):
    xq = np.ascontiguousarray(np.asarray(L0_attn_query, dtype=np.float32))
    Wqkv = np.ascontiguousarray(np.asarray(L0_attn_Wqkv, dtype=np.float32))
    Wproj = np.ascontiguousarray(np.asarray(L0_attn_Wproj, dtype=np.float32))
    bprojv = np.ascontiguousarray(np.asarray(L0_attn_bproj, dtype=np.float32))
    wqg = np.ascontiguousarray(np.asarray(L0_ffn_wq_gate, dtype=np.int32))
    wqu = np.ascontiguousarray(np.asarray(L0_ffn_wq_up, dtype=np.int32))
    wqdn = np.ascontiguousarray(np.asarray(L0_ffn_wq_down, dtype=np.int32))
    sg = np.asarray(L0_ffn_s_gate, dtype=np.float32)
    su = np.asarray(L0_ffn_s_up, dtype=np.float32)
    sd = np.asarray(L0_ffn_s_down, dtype=np.float32)

    sg2 = np.ascontiguousarray(sg.reshape(HC, 128).T)
    su2 = np.ascontiguousarray(su.reshape(HC, 128).T)
    sd2 = np.ascontiguousarray(sd.reshape(16, 128).T)
    ident = np.eye(128, dtype=ml_dtypes.bfloat16)

    nc = build_nc()
    in_maps = []
    for c in range(8):
        j, half = c // 2, c % 2
        xs = []
        for a in range(NB):
            xs.append(xq[a, half * TQ:(half + 1) * TQ])
            xs.append(xq[a, (1 - half) * TQ:(2 - half) * TQ])
        x_all = np.ascontiguousarray(np.concatenate(xs, axis=0))
        x_own_c = np.ascontiguousarray(xq[j, half * TQ:(half + 1) * TQ])
        wqkv_sl = np.ascontiguousarray(np.concatenate(
            [Wqkv[(4 * r + j) * 384:(4 * r + j + 1) * 384] for r in range(R)],
            axis=0))
        in_maps.append({
            "x_all": x_all, "x_own": x_own_c, "wqkv_s": wqkv_sl,
            "wproj": Wproj, "bproj": bprojv,
            "wq_gate": wqg, "wq_up": wqu, "wq_down": wqdn,
            "s_gate2": sg2, "s_up2": su2, "s_down2": sd2,
            "ident": ident,
        })
    res = run_bass_kernel_spmd(nc, in_maps, core_ids=list(range(8)))
    out = np.empty((NB, S, D), dtype=np.float32)
    for c in range(8):
        j, half = c // 2, c % 2
        out[j, half * TQ:(half + 1) * TQ] = res.results[c]["out"]
    return out


# version 08
# speedup vs baseline: 1.2186x; 1.2186x over previous
import numpy as np
import ml_dtypes
import concourse.bass as bass
import concourse.mybir as mybir
from concourse.bass_utils import run_bass_kernel_spmd

F32 = mybir.dt.float32
BF16 = mybir.dt.bfloat16
I32 = mybir.dt.int32
AF = mybir.ActivationFunctionType
ALU = mybir.AluOpType
AX = mybir.AxisListType

NB = 4
S = 4096
D = 2048
R = 4
HID = 5632
TQ = 2048
NORM = 1.0 / np.sqrt(128.0)
EPS = 1e-5
NSTRIP = 16
QB = 256
NQB = TQ // QB
NKC = S // 128
HC = HID // 128
INV_D = 1.0 / D
SB_BASE = 18432
SBUF_LIMIT = 229120


class Sem:
    def __init__(self, nc, es, name):
        self.name = name
        self.h = es.enter_context(nc.semaphore(name))
        self.v = 0

    def bump(self, n):
        self.v += n
        return self.v


class Prog:
    ENGS = ("sync", "tensor", "vector", "scalar", "gpsimd")

    def __init__(self):
        self.streams = {e: [] for e in self.ENGS}
        self.hwm = {e: {} for e in self.ENGS}

    def emit(self, eng, fn, waits=(), incs=()):
        ws = []
        for w in waits:
            if w is None:
                continue
            s, v = w
            if v is None:
                continue
            if self.hwm[eng].get(s.name, -1) >= v:
                continue
            self.hwm[eng][s.name] = v
            ws.append((s, v))
        self.streams[eng].append((tuple(ws), fn, tuple(incs)))

    def run(self, eng_name, E):
        for ws, fn, incs in self.streams[eng_name]:
            for s, v in ws:
                E.wait_ge(s.h, v)
            if fn is None:
                continue
            ins = fn(E)
            for s, n in incs:
                ins = ins.then_inc(s.h, n)


def build_nc(phase="full"):
    nc = bass.Bass("TRN2", target_bir_lowering=False)
    from contextlib import ExitStack
    es = ExitStack()

    x_all = nc.dram_tensor("x_all", [NB * S, D], F32, kind="ExternalInput").ap()
    x_own = nc.dram_tensor("x_own", [TQ, D], F32, kind="ExternalInput").ap()
    wqkv_s = nc.dram_tensor("wqkv_s", [R * 384, D], F32, kind="ExternalInput").ap()
    wproj = nc.dram_tensor("wproj", [D, D], F32, kind="ExternalInput").ap()
    bproj = nc.dram_tensor("bproj", [D], F32, kind="ExternalInput").ap()
    wq_gate = nc.dram_tensor("wq_gate", [HID, D], I32, kind="ExternalInput").ap()
    wq_up = nc.dram_tensor("wq_up", [HID, D], I32, kind="ExternalInput").ap()
    wq_down = nc.dram_tensor("wq_down", [D, HID], I32, kind="ExternalInput").ap()
    s_gate2 = nc.dram_tensor("s_gate2", [128, HC], F32, kind="ExternalInput").ap()
    s_up2 = nc.dram_tensor("s_up2", [128, HC], F32, kind="ExternalInput").ap()
    s_down2 = nc.dram_tensor("s_down2", [128, 16], F32, kind="ExternalInput").ap()
    ident_in = nc.dram_tensor("ident", [128, 128], BF16, kind="ExternalInput").ap()
    out_d = nc.dram_tensor("out", [TQ, D], F32, kind="ExternalOutput").ap()

    _k = lambda p: "ExternalOutput" if phase == p else "Internal"
    ctx_d = nc.dram_tensor("ctx_d", [TQ, D], BF16, kind=_k("A")).ap()
    h_d = nc.dram_tensor("h_d", [HID, TQ], BF16, kind=_k("B2")).ap()
    y1_d = nc.dram_tensor("y1_d", [TQ, D], F32, kind=_k("B1")).ap()
    fo_d = nc.dram_tensor("fo_d", [TQ, D], BF16, kind=_k("B3")).ap()

    OFF = [SB_BASE]

    def sb(name, shape, dtype):
        nbytes = int(np.prod(shape[1:])) * mybir.dt.size(dtype)
        o = (OFF[0] + 31) // 32 * 32
        h = nc.alloc_sbuf_tensor_at(name, shape, dtype, offset=o)
        OFF[0] = o + nbytes
        assert OFF[0] <= SBUF_LIMIT, f"sbuf overflow at {name}: {OFF[0]}"
        return h

    identS = sb("identS", [128, 128], BF16)
    ones128 = sb("ones128", [128, 128], BF16)
    sg2 = sb("sg2", [128, HC], F32)
    su2 = sb("su2", [128, HC], F32)
    sd2 = sb("sd2", [128, 16], F32)
    nbg = sb("nbg", [128, HC], F32)
    nbu = sb("nbu", [128, HC], F32)
    nbd = sb("nbd", [128, 16], F32)
    stats = sb("stats", [128, 16], F32)
    ssq2 = sb("ssq2", [128, 2], F32)
    maxK = sb("maxK", [128, 64], F32)
    maxQ = sb("maxQ", [128, 32], F32)
    gtmp = sb("gtmp", [128, 8], F32)
    dcol = sb("dcol", [128, 2], F32)
    rdcol = sb("rdcol", [128, 2], F32)
    sff = sb("sff", [128, 16], F32)
    sff2 = sb("sff2", [128, 16], F32)
    PERS_END = OFF[0]

    wqkvT = sb("wqkvT", [128, 16 * 1536], BF16)
    Kst = sb("Kst", [128, R * S], BF16)
    Vst = sb("Vst", [128, 32 * 520], BF16)
    Qst = sb("Qst", [128, R * TQ], BF16)
    Pt = [sb("Pt0", [128, NKC * QB], BF16), sb("Pt1", [128, NKC * QB], BF16)]
    xin = [sb("xin0", [128, 2048], F32), sb("xin1", [128, 2048], F32)]
    xln = [sb("xln0", [128, 2048], BF16), sb("xln1", [128, 2048], BF16)]
    xT = [sb("xT0", [128, 16 * 256], BF16), sb("xT1", [128, 16 * 256], BF16)]
    Ksqb = sb("Ksqb", [128, 512], BF16)
    ctxs = [sb("ctxs0", [128, 128], BF16), sb("ctxs1", [128, 128], BF16)]
    assert OFF[0] <= SBUF_LIMIT

    OFF[0] = PERS_END
    junk = sb("junk", [128, 2048], BF16)
    BJ_END = OFF[0]
    WprojT = sb("WprojT", [128, 16 * 2048], BF16)
    zT = sb("zT", [128, 16 * 2048], BF16)
    bpb = sb("bpb", [128, 2048], F32)
    B_COMMON_END = OFF[0]

    B1W_START = OFF[0]
    wpl_ = sb("wpl0", [128, 2048], F32)
    wpl = [wpl_, wpl_]
    ctxin = [sb("ctxin0", [128, 2048], BF16), sb("ctxin1", [128, 2048], BF16)]
    ctxTb = [sb("ctxT0", [128, 2048], BF16), sb("ctxT1", [128, 2048], BF16)]
    attnout = sb("attnout", [128, 2048], F32)
    xo_ = sb("xo0", [128, 2048], F32)
    xo = [xo_, xo_]
    y1t = sb("y1t", [128, 2048], F32)
    zt = [sb("zt0", [128, 2048], BF16), sb("zt1", [128, 2048], BF16)]
    ln3t = sb("ln3t", [128, 2048], BF16)
    wplb = ln3t
    assert OFF[0] <= SBUF_LIMIT

    OFF[0] = BJ_END
    wqg = [sb("wqg0", [128, 2048], I32), sb("wqg1", [128, 2048], I32)]
    wgd = sb("wgd", [128, 2048], BF16)
    wud = sb("wud", [128, 2048], BF16)
    WgT = sb("WgT", [128, 2048], BF16)
    WuT = sb("WuT", [128, 2048], BF16)
    gbuf = sb("gbuf", [128, 2048], BF16)
    tbuf = sb("tbuf", [128, 2048], BF16)
    ubuf = sb("ubuf", [128, 2048], BF16)
    abuf = sb("abuf", [128, 2048], BF16)
    hbuf = [sb("hbuf0", [128, 2048], BF16), sb("hbuf1", [128, 2048], BF16)]
    assert OFF[0] <= BJ_END + 16 * 2048 * 2, "B2 must fit inside WprojT region"
    OFF[0] = B1W_START
    wqgu = [sb("wqgu0", [128, 2048], I32), sb("wqgu1", [128, 2048], I32)]

    OFF[0] = BJ_END
    WdT = sb("WdT", [128, HC * 1024], BF16)
    wqd = [sb("wqd0", [128, 2816], I32), sb("wqd1", [128, 2816], I32)]
    wdq = sb("wdq", [128, HID], BF16)
    hld = [sb("hld0", [128, HC * 128], BF16), sb("hld1", [128, HC * 128], BF16)]
    fob = [sb("fob0", [128, 512], BF16), sb("fob1", [128, 512], BF16)]
    fin = [sb("fin0", [128, 2048], BF16), sb("fin1", [128, 2048], BF16)]
    y1in = [sb("y1in0", [128, 2048], F32), sb("y1in1", [128, 2048], F32)]
    yout = [sb("yout0", [128, 2048], F32), sb("yout1", [128, 2048], F32)]
    assert OFF[0] <= SBUF_LIMIT

    pbT = [nc.alloc_psum_tensor(f"pbT{i}", [128, 1024], F32) for i in range(2)]
    pb4 = nc.alloc_psum_tensor("pb4", [128, 512], F32)
    pb5 = nc.alloc_psum_tensor("pb5", [128, 512], F32)
    pbS = nc.alloc_psum_tensor("pbS", [128, 1024], F32)

    def PB(slot, a=0, b=512):
        if slot == 4:
            return pb4[:, a:b]
        if slot == 5:
            return pb5[:, a:b]
        if slot == 6:
            return pbS[:, a:b]
        return pbS[:, 512 + a:512 + b]

    sWA = Sem(nc, es, "sWA")
    sX = [Sem(nc, es, "sX0"), Sem(nc, es, "sX1")]
    sWPL = [Sem(nc, es, "sWPL0"), Sem(nc, es, "sWPL1")]
    sXO = Sem(nc, es, "sXO")
    sBPB = Sem(nc, es, "sBPB")
    sCI = [Sem(nc, es, "sCI0"), Sem(nc, es, "sCI1")]
    sCW = [Sem(nc, es, "sCW0"), Sem(nc, es, "sCW1")]
    sWQ = [Sem(nc, es, "sWQ0"), Sem(nc, es, "sWQ1")]
    sWU = [Sem(nc, es, "sWU0"), Sem(nc, es, "sWU1")]
    sWD = [Sem(nc, es, "sWD0"), Sem(nc, es, "sWD1")]
    sHW = [Sem(nc, es, "sHW0"), Sem(nc, es, "sHW1")]
    sHL = [Sem(nc, es, "sHL0"), Sem(nc, es, "sHL1")]
    sFW = [Sem(nc, es, "sFW0"), Sem(nc, es, "sFW1")]
    sFI = [Sem(nc, es, "sFI0"), Sem(nc, es, "sFI1")]
    sYI = [Sem(nc, es, "sYI0"), Sem(nc, es, "sYI1")]
    sY1W = Sem(nc, es, "sY1W")
    sOW = [Sem(nc, es, "sOW0"), Sem(nc, es, "sOW1")]
    sTE = Sem(nc, es, "sTE")
    sVE = Sem(nc, es, "sVE")
    sSE = Sem(nc, es, "sSE")
    sGE = Sem(nc, es, "sGE")

    P = Prog()

    def dma(eng, out, in_, sem, waits=()):
        P.emit(eng, lambda E, o=out, i=in_: E.dma_start(out=o, in_=i),
               waits=waits, incs=((sem, 16),))
        return sem.bump(16)

    def te(fn, waits=(), inc=False):
        P.emit("tensor", fn, waits=waits, incs=((sTE, 1),))
        return sTE.bump(1)

    def ve(fn, waits=(), inc=False):
        P.emit("vector", fn, waits=waits, incs=((sVE, 1),) if inc else ())
        return sVE.bump(1) if inc else None

    def se(fn, waits=(), inc=False):
        P.emit("scalar", fn, waits=waits, incs=((sSE, 1),) if inc else ())
        return sSE.bump(1) if inc else None

    def ge(fn, waits=(), inc=False):
        P.emit("gpsimd", fn, waits=waits, incs=((sGE, 1),) if inc else ())
        return sGE.bump(1) if inc else None

    def vdr():
        P.emit("vector", lambda E: E.drain())

    def sdr():
        P.emit("scalar", lambda E: E.drain())

    bank_free = [None] * 8

    trot = [0]

    def transpose_tile_set(srcbuf, dstfn, banks_unused, waits_first,
                           extra_evac_wait=None, ndc=16, koff=0):
        vt = None
        done = 0
        while done < ndc:
            g = min(8, ndc - done)
            bank = trot[0] % 2
            trot[0] += 1
            for q in range(g):
                dc = done + q
                w = []
                if done == 0 and q == 0:
                    w = list(waits_first)
                if q == 0:
                    w = w + [bank_free[bank]]
                vt = te(lambda E, dc=dc, bank=bank, q=q:
                        E.matmul(out=pbT[bank][:, q * 128:(q + 1) * 128],
                                 lhsT=srcbuf[:, koff + dc * 128: koff + (dc + 1) * 128],
                                 rhs=identS[:], start=True, stop=True),
                        waits=w)
            ew = [(sTE, vt)]
            if extra_evac_wait is not None:
                ew.append(extra_evac_wait)
            vv = ve(lambda E, dst=dstfn(done, g), bank=bank, g=g:
                    E.tensor_copy(out=dst,
                                  in_=pbT[bank][:].rearrange(
                                      "p (c t) -> p c t", t=128)[:, 0:g, :]),
                    waits=ew, inc=True)
            bank_free[bank] = (sVE, vv)
            done += g
        return vt, sVE.v

    def norm_mm(rhs_ap, waits, width=256):
        bank = trot[0] % 2
        trot[0] += 1
        vt = te(lambda E, bank=bank, width=width:
                E.matmul(out=pbT[bank][:, 0:width], lhsT=ones128[:], rhs=rhs_ap,
                         start=True, stop=True),
                waits=list(waits) + [bank_free[bank]])
        return bank, vt

    v_ident = dma("sync", identS[:], ident_in, sWA)
    v_sg = dma("sync", sg2[:], s_gate2, sWA)
    v_su = dma("sync", su2[:], s_up2, sWA)
    v_sd = dma("sync", sd2[:], s_down2, sWA)

    ve(lambda E: E.memset(ones128[:], 1.0))
    ve(lambda E: E.tensor_scalar_mul(out=nbg[:], in0=sg2[:], scalar1=-128.0),
       waits=[(sWA, 64)])
    ve(lambda E: E.tensor_scalar_mul(out=nbu[:], in0=su2[:], scalar1=-128.0))
    v_nb = ve(lambda E: E.tensor_scalar_mul(out=nbd[:], in0=sd2[:], scalar1=-128.0),
              inc=True)

    xin_free = [None, None]
    xln_free = [None, None]
    for ob in range(12):
        p = ob % 2
        vl = dma("sync", xin[p][:], wqkv_s[ob * 128:(ob + 1) * 128, :], sX[p],
                 waits=[xin_free[p]])
        vc = ve(lambda E, p=p: E.tensor_copy(out=xln[p][:], in_=xin[p][:]),
                waits=[(sX[p], vl), xln_free[p], (sWA, 64)], inc=True)
        xin_free[p] = (sVE, vc)

        def dst4(dc0, g, ob=ob):
            return wqkvT[:].rearrange("p (dc o) -> p dc o", o=1536)[
                :, dc0:dc0 + g, ob * 128:(ob + 1) * 128]
        vt, _ = transpose_tile_set(xln[p], dst4, None,
                                   waits_first=[(sVE, vc)])
        xln_free[p] = (sTE, vt)
    v_wqkvT = sVE.v

    kst_free = None
    vst_free = None
    pt_free = [None, None]
    ctxs_free = [None, None]
    stats_free = [None, None]
    xT_free = [None, None]
    exp_reg_rel = {0: None, 1: None}
    ctx_bank_rel = {4: None, 5: None}
    gbias_free = None
    ksqb_free = [None, None]

    def ln_chain(tile_idx, src_ap, dst_ap, data_ready, dst_free=None, scr=None):
        if scr is None:
            scr = junk[:]
        st = tile_idx % 2
        c = st * 8
        ssum = stats[:, c + 0:c + 1]
        ssq = stats[:, c + 1:c + 2]
        smu = stats[:, c + 2:c + 3]
        smu2 = stats[:, c + 3:c + 4]
        snm2e = stats[:, c + 4:c + 5]
        srstd = stats[:, c + 5:c + 6]
        snegr = stats[:, c + 6:c + 7]
        snmr = stats[:, c + 7:c + 8]
        wfree = [stats_free[st]] + ([dst_free] if dst_free else [])
        vq = se(lambda E: E.activation(out=scr, in_=src_ap, func=AF.Square,
                                       accum_out=ssq),
                waits=list(data_ready) + wfree, inc=True)
        vs = se(lambda E: E.activation(out=scr, in_=src_ap, func=AF.Identity,
                                       accum_out=ssum),
                waits=[(sSE, vq)], inc=True)
        v1 = se(lambda E: E.mul(out=smu, in_=ssum, mul=INV_D),
                waits=[(sSE, vs)], inc=True)
        v2 = se(lambda E: E.activation(out=smu2, in_=smu, func=AF.Square),
                waits=[(sSE, v1)], inc=True)
        v3 = se(lambda E: E.activation(out=snm2e, in_=smu2, func=AF.Identity,
                                       bias=np.float32(EPS),
                                       scale=np.float32(-1.0)),
                waits=[(sSE, v2)], inc=True)
        v4 = se(lambda E: E.activation(out=srstd, in_=ssq, func=AF.Ln,
                                       bias=snm2e, scale=np.float32(INV_D)),
                waits=[(sSE, v3)], inc=True)
        v5 = se(lambda E: E.activation(out=srstd, in_=srstd, func=AF.Exp,
                                       bias=np.float32(0.0),
                                       scale=np.float32(-0.5)),
                waits=[(sSE, v4)], inc=True)
        v6 = se(lambda E: E.mul(out=snegr, in_=srstd, mul=-1.0),
                waits=[(sSE, v5)], inc=True)
        v7 = se(lambda E: E.activation(out=snmr, in_=smu, func=AF.Identity,
                                       bias=np.float32(0.0), scale=snegr),
                waits=[(sSE, v6)], inc=True)
        v8 = se(lambda E: E.activation(out=dst_ap, in_=src_ap, func=AF.Identity,
                                       bias=snmr, scale=srstd),
                waits=[(sSE, v7)], inc=True)
        stats_free[st] = (sSE, v8)
        return v8

    for a in range(NB):
        v_vmset = ve(lambda E: E.memset(Vst[:], 1.0), waits=[vst_free], inc=True)
        for n in range(NSTRIP):
            p = n % 2
            row0 = a * S + n * 256
            for k in range(2):
                tp = (n * 2 + k) % 2
                vl = dma("sync", xin[tp][:],
                         x_all[row0 + k * 128: row0 + (k + 1) * 128, :], sX[tp],
                         waits=[xin_free[tp]])
                v_ln = ln_chain(n * 2 + k, xin[tp][:], xln[tp][:],
                                [(sX[tp], vl)], dst_free=xln_free[tp],
                                scr=xln[tp][:])
                xin_free[tp] = (sSE, v_ln)

                def dst4(dc0, g, p=p, k=k):
                    return xT[p][:].rearrange("p (dc t) -> p dc t", t=256)[
                        :, dc0:dc0 + g, k * 128:(k + 1) * 128]
                vt, _ = transpose_tile_set(xln[tp], dst4, None,
                                           waits_first=[(sSE, v_ln)],
                                           extra_evac_wait=xT_free[p])
                xln_free[tp] = (sTE, vt)
            xT_ready = sVE.v

            for r in range(R):
                bank = 4 + r
                for dc in range(16):
                    w = []
                    if dc == 0:
                        w = [(sVE, xT_ready), bank_free[bank]]
                    vt = te(lambda E, r=r, dc=dc, bank=bank, p=p:
                            E.matmul(out=PB(bank, 0, 256),
                                     lhsT=wqkvT[:, dc * 1536 + r * 384 + 128:
                                                dc * 1536 + r * 384 + 256],
                                     rhs=xT[p][:, dc * 256:(dc + 1) * 256],
                                     start=(dc == 0), stop=(dc == 15)),
                            waits=w, inc=(dc == 15))
                vek = ve(lambda E, r=r, n=n, bank=bank:
                         E.tensor_copy(out=Kst[:, r * S + n * 256: r * S + (n + 1) * 256],
                                       in_=PB(bank, 0, 256)),
                         waits=[(sTE, vt), kst_free], inc=True)
                bank_free[bank] = (sVE, vek)
                if n % 2 == 1:
                    vsq = ve(lambda E, r=r, n=n:
                             E.tensor_mul(out=Ksqb[:],
                                          in0=Kst[:, r * S + (n - 1) * 256:
                                                 r * S + (n + 1) * 256],
                                          in1=Kst[:, r * S + (n - 1) * 256:
                                                 r * S + (n + 1) * 256]),
                             waits=[(sVE, vek), ksqb_free[0]], inc=True)
                    nbank, vn = norm_mm(Ksqb[:], [(sVE, vsq)], 512)
                    vm = ve(lambda E, r=r, n=n, nbank=nbank:
                            E.reduce_max(out=maxK[:, r * 8 + n // 2:
                                               r * 8 + n // 2 + 1],
                                         in_=pbT[nbank][:, 0:512], axis=AX.X),
                            waits=[(sTE, vn)], inc=True)
                    bank_free[nbank] = (sVE, vm)
                    ksqb_free[0] = (sTE, vn)

            if n < 8:
                for r in range(R):
                    bank = 4 + r
                    for dc in range(16):
                        w = []
                        if dc == 0:
                            w = [bank_free[bank]]
                        vt = te(lambda E, r=r, dc=dc, bank=bank, p=p:
                                E.matmul(out=PB(bank, 0, 256),
                                         lhsT=wqkvT[:, dc * 1536 + r * 384:
                                                    dc * 1536 + r * 384 + 128],
                                         rhs=xT[p][:, dc * 256:(dc + 1) * 256],
                                         start=(dc == 0), stop=(dc == 15)),
                                waits=w, inc=(dc == 15))
                    veq = ve(lambda E, r=r, n=n, bank=bank:
                             E.tensor_copy(out=Qst[:, r * TQ + n * 256:
                                                  r * TQ + (n + 1) * 256],
                                           in_=PB(bank, 0, 256)),
                             waits=[(sTE, vt), kst_free], inc=True)
                    bank_free[bank] = (sVE, veq)
                    if n % 2 == 1:
                        vsq = ve(lambda E, r=r, n=n:
                                 E.tensor_mul(out=Ksqb[:],
                                              in0=Qst[:, r * TQ + (n - 1) * 256:
                                                     r * TQ + (n + 1) * 256],
                                              in1=Qst[:, r * TQ + (n - 1) * 256:
                                                     r * TQ + (n + 1) * 256]),
                                 waits=[(sVE, veq), ksqb_free[1]], inc=True)
                        nbank, vn = norm_mm(Ksqb[:], [(sVE, vsq)], 512)
                        vm = ve(lambda E, r=r, n=n, nbank=nbank:
                                E.reduce_max(out=maxQ[:, r * 4 + n // 2:
                                                   r * 4 + n // 2 + 1],
                                             in_=pbT[nbank][:, 0:512], axis=AX.X),
                                waits=[(sTE, vn)], inc=True)
                        bank_free[nbank] = (sVE, vm)
                        ksqb_free[1] = (sTE, vn)

            for k in range(2):
                bank = 4 + k
                for r in range(R):
                    for dc in range(16):
                        w = []
                        if r == 0 and dc == 0:
                            w = [(sVE, v_vmset), bank_free[bank]]
                        vt = te(lambda E, k=k, r=r, dc=dc, bank=bank, p=p:
                                E.matmul(out=PB(bank, r * 128, (r + 1) * 128),
                                         lhsT=xT[p][:, dc * 256 + k * 128:
                                                   dc * 256 + k * 128 + 128],
                                         rhs=wqkvT[:, dc * 1536 + r * 384 + 256:
                                                   dc * 1536 + r * 384 + 384],
                                         start=(dc == 0), stop=(dc == 15)),
                                waits=w, inc=(r == 3 and dc == 15))
                tc = n * 2 + k
                dstv = Vst[:].rearrange("p (tc r x) -> p tc r x", r=4, x=130)[
                    :, tc, :, 0:128]
                vev = ve(lambda E, dstv=dstv, bank=bank:
                         E.tensor_copy(out=dstv,
                                       in_=PB(bank).rearrange(
                                           "p (r x) -> p r x", x=128)),
                         waits=[(sTE, vt)], inc=True)
                bank_free[bank] = (sVE, vev)
            xT_free[p] = (sTE, sTE.v)

        vst_ready = sVE.v
        v_norm_done = sVE.v
        for r in range(R):
            i_block = 4 * a + r
            vk1 = ve(lambda E, r=r: E.reduce_max(out=gtmp[:, 0:1],
                                                 in_=maxK[:, r * 8:(r + 1) * 8],
                                                 axis=AX.X),
                     waits=[gbias_free, (sVE, v_norm_done)], inc=True)
            vk2 = ve(lambda E, r=r: E.reduce_max(out=gtmp[:, 1:2],
                                                 in_=maxQ[:, r * 4:(r + 1) * 4],
                                                 axis=AX.X), inc=True)
            v1 = ve(lambda E: E.tensor_mul(out=gtmp[:, 2:3], in0=gtmp[:, 0:1],
                                           in1=gtmp[:, 1:2]),
                    waits=[(sVE, vk2)], inc=True)
            vln_g = se(lambda E: E.activation(out=gtmp[:, 3:4], in_=gtmp[:, 2:3],
                                              func=AF.Ln),
                       waits=[(sVE, v1)], inc=True)
            v2 = se(lambda E: E.activation(out=gtmp[:, 4:5], in_=gtmp[:, 3:4],
                                           func=AF.Exp, bias=np.float32(0.0),
                                           scale=np.float32(0.5)),
                    waits=[(sSE, vln_g)], inc=True)
            vbias = ve(lambda E: E.tensor_scalar_mul(out=gtmp[:, 5:6],
                                                     in0=gtmp[:, 4:5],
                                                     scalar1=-float(NORM)),
                       waits=[(sSE, v2)], inc=True)

            for qb in range(NQB):
                ptp = qb % 2
                for m2 in range(8):
                    reg = m2 % 2
                    rten = pbS if reg == 0 else pbT[1]
                    rslot = 6 if reg == 0 else 1
                    for kco in range(4):
                        kc = 4 * m2 + kco
                        w = [exp_reg_rel[reg]]
                        if m2 in (0, 1) and kco == 0:
                            w += [bank_free[rslot]]
                            if reg == 0:
                                w += [bank_free[7]]
                        if kc == 0:
                            w += [(sVE, vst_ready), (sVE, vbias)]
                        vt = te(lambda E, r=r, kc=kc, qb=qb, rten=rten:
                                E.matmul(out=rten[:, (kc % 4) * 256:
                                                  (kc % 4) * 256 + 256],
                                         lhsT=Kst[:, r * S + kc * 128:
                                                r * S + (kc + 1) * 128],
                                         rhs=Qst[:, r * TQ + qb * 256:
                                                r * TQ + (qb + 1) * 256],
                                         start=True, stop=True),
                                waits=w)
                    wx = [(sTE, vt)]
                    if m2 == 0:
                        wx += [pt_free[ptp]]
                    vxp = se(lambda E, m2=m2, rten=rten, ptp=ptp:
                             E.activation(out=Pt[ptp][:, m2 * 1024:
                                                     (m2 + 1) * 1024],
                                          in_=rten[:], func=AF.Exp,
                                          bias=gtmp[:, 5:6],
                                          scale=np.float32(NORM)),
                             waits=wx, inc=True)
                    exp_reg_rel[reg] = (sSE, vxp)
                exp_done = sSE.v
                for qt in range(2):
                    bank = 4 + qt
                    for kc in range(NKC):
                        w = []
                        if kc == 0:
                            w = [(sSE, exp_done), ctx_bank_rel[bank]]
                        vt = te(lambda E, r=r, qt=qt, kc=kc, bank=bank, ptp=ptp:
                                E.matmul(out=PB(bank, 0, 129),
                                         lhsT=Pt[ptp][:, kc * 256 + qt * 128:
                                                     kc * 256 + qt * 128 + 128],
                                         rhs=Vst[:, kc * 520 + r * 130:
                                                 kc * 520 + r * 130 + 129],
                                         start=(kc == 0), stop=(kc == NKC - 1)),
                                waits=w, inc=(kc == NKC - 1))
                    vd = ve(lambda E, bank=bank, qt=qt:
                            E.tensor_copy(out=dcol[:, qt:qt + 1],
                                          in_=PB(bank, 128, 129)),
                            waits=[(sTE, vt)], inc=True)
                    vr = ve(lambda E, qt=qt:
                            E.reciprocal(out=rdcol[:, qt:qt + 1],
                                         in_=dcol[:, qt:qt + 1]),
                            waits=[(sVE, vd)], inc=True)
                    tile = qb * 2 + qt
                    vnorm = se(lambda E, bank=bank, qt=qt:
                               E.activation(out=ctxs[qt][:],
                                            in_=PB(bank, 0, 128),
                                            func=AF.Identity,
                                            bias=np.float32(0.0),
                                            scale=rdcol[:, qt:qt + 1]),
                               waits=[(sVE, vr), ctxs_free[qt]], inc=True)
                    ctx_bank_rel[bank] = (sSE, vnorm)
                    vctx = dma("scalar",
                               ctx_d[tile * 128:(tile + 1) * 128,
                                     i_block * 128:(i_block + 1) * 128],
                               ctxs[qt][:], sCW[qt], waits=[(sSE, vnorm)])
                    ctxs_free[qt] = (sCW[qt], vctx)
                pt_free[ptp] = (sTE, sTE.v)
            gbias_free = (sSE, sSE.v)
        kst_free = (sTE, sTE.v)
        vst_free = (sTE, sTE.v)
        bank_free[4] = ctx_bank_rel[4]
        bank_free[5] = ctx_bank_rel[5]
        bank_free[6] = exp_reg_rel[0]
        bank_free[7] = exp_reg_rel[0]
        bank_free[1] = exp_reg_rel[1]

    if phase == "A":
        P.emit("sync", None, waits=[(sCW[0], sCW[0].v), (sCW[1], sCW[1].v)])
        return _finish(nc, P, es)
    barrier = [(sTE, sTE.v), (sVE, sVE.v), (sSE, sSE.v), (sGE, sGE.v),
               (sCW[0], sCW[0].v), (sCW[1], sCW[1].v)]

    rep_src = bass.AP(bproj.tensor, 0, [[0, 128], [1, 2048]])
    v_bpb = dma("sync", bpb[:], rep_src, sBPB, waits=barrier)

    zoff = zT.manual_sbuf_range[0]
    wplA = [nc.alloc_sbuf_tensor_at("wplA0", [128, 4 * 2048], F32, offset=zoff),
            nc.alloc_sbuf_tensor_at("wplA1", [128, 4 * 2048], F32,
                                    offset=zoff + 4 * 2048 * 4)]
    wplA_free = [None, None]
    wplb_free = None
    for blk in range(4):
        bp = blk % 2
        wsrc = bass.AP(wproj.tensor, blk * 512 * 2048,
                       [[2048, 128], [128 * 2048, 4], [1, 2048]])
        wdst = bass.AP(wplA[bp], 0, [[8192, 128], [2048, 4], [1, 2048]])
        vl = dma("sync", wdst, wsrc, sWPL[bp],
                 waits=barrier + [wplA_free[bp]])
        for t in range(4):
            ob = blk * 4 + t
            vc = ve(lambda E, bp=bp, t=t:
                    E.tensor_copy(out=wplb[:],
                                  in_=wplA[bp][:, t * 2048:(t + 1) * 2048]),
                    waits=barrier + [(sWPL[bp], vl), wplb_free], inc=True)

            def dst4(dc0, g, ob=ob):
                return WprojT[:].rearrange("p (dc o) -> p dc o", o=2048)[
                    :, dc0:dc0 + g, ob * 128:(ob + 1) * 128]
            vt, _ = transpose_tile_set(wplb, dst4, None,
                                       waits_first=barrier + [(sVE, vc)])
            wplb_free = (sTE, vt)
        wplA_free[bp] = (sVE, sVE.v)
    v_wprojT = sVE.v
    b0_ln3_guard = wplb_free

    all_ctx0 = sCW[0].v
    all_ctx1 = sCW[1].v
    ctxin_free = [None, None]
    ctxT_free = [None, None]
    xo_free = None
    attnout_free = None
    y1t_free = []
    zt_free = [None, None]
    zt_pending = []
    ln3_free = b0_ln3_guard
    vT_ready = {}

    def b1_stage1(tt, extra):
        p = tt % 2
        vl = dma("sync", ctxin[p][:], ctx_d[tt * 128:(tt + 1) * 128, :],
                 sCI[p], waits=extra + [ctxin_free[p]])

        def dst4(dc0, g, p=p):
            return ctxTb[p][:].rearrange("p (dc t) -> p dc t", t=128)[
                :, dc0:dc0 + g, :]
        vt, _ = transpose_tile_set(ctxin[p], dst4, None,
                                   waits_first=[(sCI[p], vl)],
                                   extra_evac_wait=ctxT_free[p])
        ctxin_free[p] = (sTE, vt)
        vT_ready[tt] = sVE.v

    b1_stage1(0, barrier + [(sCW[0], all_ctx0), (sCW[1], all_ctx1)])
    for tt in range(16):
        p = tt % 2
        if tt + 1 < 16:
            b1_stage1(tt + 1, [])
        vxo = dma("sync", xo[p][:], x_own[tt * 128:(tt + 1) * 128, :], sXO,
                  waits=barrier + [xo_free])
        for dob in range(4):
            bank = 4 + dob
            for dc in range(16):
                w = []
                if dc == 0:
                    w = [(sVE, vT_ready[tt]), (sVE, v_wprojT), bank_free[bank]]
                vt = te(lambda E, p=p, dob=dob, dc=dc, bank=bank:
                        E.matmul(out=PB(bank),
                                 lhsT=ctxTb[p][:, dc * 128:(dc + 1) * 128],
                                 rhs=WprojT[:, dc * 2048 + dob * 512:
                                           dc * 2048 + (dob + 1) * 512],
                                 start=(dc == 0), stop=(dc == 15)),
                        waits=w, inc=(dc == 15))
            vev = ve(lambda E, dob=dob, bank=bank:
                     E.tensor_tensor(out=attnout[:, dob * 512:(dob + 1) * 512],
                                     in0=PB(bank),
                                     in1=bpb[:, dob * 512:(dob + 1) * 512],
                                     op=ALU.add),
                     waits=[(sTE, vt), (sBPB, v_bpb), attnout_free], inc=True)
            bank_free[bank] = (sVE, vev)
        v_ao = sVE.v
        ctxT_free[p] = (sTE, sTE.v)
        v_ln3 = ln_chain(tt, attnout[:], ln3t[:], [(sVE, v_ao)],
                         dst_free=ln3_free)
        attnout_free = (sSE, v_ln3)
        vy1 = ve(lambda E, p=p: E.tensor_add(out=y1t[:], in0=xo[p][:],
                                             in1=ln3t[:]),
                 waits=[(sSE, v_ln3), (sXO, vxo)] + y1t_free, inc=True)
        ln3_free = (sVE, vy1)
        xo_free = (sVE, vy1)
        vw1 = dma("scalar", y1_d[tt * 128:(tt + 1) * 128, :], y1t[:], sY1W,
                  waits=[(sVE, vy1)])
        v_ln2 = ln_chain(tt, y1t[:], zt[tt % 2][:], [(sVE, vy1)],
                         dst_free=zt_free[tt % 2])
        y1t_free = [(sY1W, vw1), (sSE, v_ln2)]
        zt_pending.append((tt, v_ln2))
        if len(zt_pending) == 2 or tt == 15:
            for ptt, pv in (zt_pending if tt == 15 else zt_pending[:1]):
                def dst4(dc0, g, ptt=ptt):
                    return zT[:].rearrange("p (dc t) -> p dc t", t=2048)[
                        :, dc0:dc0 + g, ptt * 128:(ptt + 1) * 128]
                vt, _ = transpose_tile_set(zt[ptt % 2], dst4, None,
                                           waits_first=[(sSE, pv)])
                zt_free[ptt % 2] = (sTE, vt)
            zt_pending = zt_pending[1:] if tt != 15 else []
    v_zT = sVE.v
    v_y1_written = sY1W.v
    if phase == "B1":
        P.emit("sync", None, waits=[(sY1W, v_y1_written)])
        return _finish(nc, P, es)
    b2barrier = [(sTE, sTE.v), (sVE, sVE.v), (sSE, sSE.v), (sGE, sGE.v),
                 (sY1W, v_y1_written)]

    wqg_free = [None, None]
    wgd_free = None
    wud_free = None
    WgT_free = None
    WuT_free = None
    gbuf_free = None
    ubuf_free = None
    tbuf_free = None
    abuf_free = None
    hbuf_free = [None, None]
    wqgu_free = [None, None]
    dqv = {}

    def b2_dq(hc, extra):
        nonlocal wgd_free, wud_free
        p = hc % 2
        vg = dma("sync", wqg[p][:], wq_gate[hc * 128:(hc + 1) * 128, :],
                 sWQ[p], waits=extra + [wqg_free[p]])
        vu = dma("sync", wqgu[p][:], wq_up[hc * 128:(hc + 1) * 128, :],
                 sWU[p], waits=extra + [wqgu_free[p]])
        vdq_g = se(lambda E, p=p, hc=hc:
                   E.activation(out=wgd[:], in_=wqg[p][:], func=AF.Identity,
                                bias=nbg[:, hc:hc + 1], scale=sg2[:, hc:hc + 1]),
                   waits=extra + [(sWQ[p], vg), (sVE, v_nb), wgd_free],
                   inc=True)
        wqg_free[p] = (sSE, vdq_g)
        vdq_u = se(lambda E, p=p, hc=hc:
                   E.activation(out=wud[:], in_=wqgu[p][:], func=AF.Identity,
                                bias=nbu[:, hc:hc + 1], scale=su2[:, hc:hc + 1]),
                   waits=[(sWU[p], vu), wud_free], inc=True)
        wqgu_free[p] = (sSE, vdq_u)
        dqv[hc] = (vdq_g, vdq_u)

    b2_dq(0, b2barrier)
    for hc in range(HC):
        vdq_g, vdq_u = dqv[hc]

        def dstg(dc0, g):
            return WgT[:].rearrange("p (dc t) -> p dc t", t=128)[
                :, dc0:dc0 + g, :]
        vtg, _ = transpose_tile_set(wgd, dstg, None,
                                    waits_first=[(sSE, vdq_g), WgT_free])
        wgd_free = (sTE, vtg)
        vWgT = sVE.v

        def dstu(dc0, g):
            return WuT[:].rearrange("p (dc t) -> p dc t", t=128)[
                :, dc0:dc0 + g, :]
        vtu, _ = transpose_tile_set(wud, dstu, None,
                                    waits_first=[(sSE, vdq_u), WuT_free])
        wud_free = (sTE, vtu)
        vWuT = sVE.v
        if hc + 1 < HC:
            b2_dq(hc + 1, [])

        for tb in range(4):
            bank = 4 + tb
            for dc in range(16):
                w = []
                if dc == 0:
                    w = [(sVE, vWgT), (sVE, v_zT), bank_free[bank]]
                vt = te(lambda E, dc=dc, tb=tb, bank=bank:
                        E.matmul(out=PB(bank),
                                 lhsT=WgT[:, dc * 128:(dc + 1) * 128],
                                 rhs=zT[:, dc * 2048 + tb * 512:
                                        dc * 2048 + (tb + 1) * 512],
                                 start=(dc == 0), stop=(dc == 15)),
                        waits=w, inc=(dc == 15))
            vev = ve(lambda E, tb=tb, bank=bank:
                     E.tensor_copy(out=gbuf[:, tb * 512:(tb + 1) * 512],
                                   in_=PB(bank)),
                     waits=[(sTE, vt), gbuf_free], inc=True)
            bank_free[bank] = (sVE, vev)
        WgT_free = (sTE, vt)
        v_g = sVE.v
        for tb in range(4):
            bank = 4 + tb
            for dc in range(16):
                w = []
                if dc == 0:
                    w = [(sVE, vWuT), bank_free[bank]]
                vt = te(lambda E, dc=dc, tb=tb, bank=bank:
                        E.matmul(out=PB(bank),
                                 lhsT=WuT[:, dc * 128:(dc + 1) * 128],
                                 rhs=zT[:, dc * 2048 + tb * 512:
                                        dc * 2048 + (tb + 1) * 512],
                                 start=(dc == 0), stop=(dc == 15)),
                        waits=w, inc=(dc == 15))
            vev = ve(lambda E, tb=tb, bank=bank:
                     E.tensor_copy(out=ubuf[:, tb * 512:(tb + 1) * 512],
                                   in_=PB(bank)),
                     waits=[(sTE, vt), ubuf_free], inc=True)
            bank_free[bank] = (sVE, vev)
        WuT_free = (sTE, vt)
        v_u = sVE.v
        v_t = se(lambda E: E.activation(out=tbuf[:], in_=gbuf[:], func=AF.Tanh,
                                        bias=np.float32(0.0),
                                        scale=np.float32(0.5)),
                 waits=[(sVE, v_g), tbuf_free], inc=True)
        va = ve(lambda E: E.tensor_mul(out=abuf[:], in0=gbuf[:], in1=tbuf[:]),
                waits=[(sSE, v_t), abuf_free], inc=True)
        vhp = ve(lambda E: E.tensor_add(out=abuf[:], in0=abuf[:], in1=gbuf[:]),
                 waits=[(sVE, va)], inc=True)
        gbuf_free = (sVE, vhp)
        tbuf_free = (sVE, vhp)
        vh = ve(lambda E, p=hc % 2: E.scalar_tensor_tensor(out=hbuf[p][:],
                                                      in0=abuf[:], scalar=0.5,
                                                      in1=ubuf[:],
                                                      op0=ALU.mult,
                                                      op1=ALU.mult),
                waits=[(sVE, v_u), (sVE, vhp), hbuf_free[hc % 2]], inc=True)
        abuf_free = (sVE, vh)
        ubuf_free = (sVE, vh)
        vwh = dma("scalar", h_d[hc * 128:(hc + 1) * 128, :], hbuf[hc % 2][:],
                  sHW[hc % 2], waits=[(sVE, vh)])
        hbuf_free[hc % 2] = (sHW[hc % 2], vwh)
    v_hw = [sHW[0].v, sHW[1].v]
    if phase == "B2":
        P.emit("sync", None, waits=[(sHW[0], v_hw[0]), (sHW[1], v_hw[1]),
                                    (sY1W, v_y1_written)])
        return _finish(nc, P, es)
    b3barrier = [(sTE, sTE.v), (sVE, sVE.v), (sSE, sSE.v),
                 (sHW[0], v_hw[0]), (sHW[1], v_hw[1]), (sY1W, v_y1_written)]

    ve(lambda E: E.memset(sff[:], 0.0), waits=b3barrier)
    v_sffz = ve(lambda E: E.memset(sff2[:], 0.0), inc=True)
    wqd_free = [None, None]
    wdq_free_box = [None]
    WdT_half_free = [None, None]
    sff_add_rel = None
    hld_free = [None, None]
    fob_free = [None, None]
    acc_free = None
    wdt_ready = {}

    def b3_build(u, extra):
        half = u % 2
        for dt in range(4):
            row0 = u * 512 + dt * 128
            sidx = u * 4 + dt
            vl0 = dma("sync", wqd[0][:], wq_down[row0:row0 + 128, 0:2816],
                      sWD[0], waits=extra + [wqd_free[0]])
            vl1 = dma("sync", wqd[1][:], wq_down[row0:row0 + 128, 2816:5632],
                      sWD[1], waits=extra + [wqd_free[1]])
            vdq0 = se(lambda E, sidx=sidx:
                      E.activation(out=wdq[:, 0:2816], in_=wqd[0][:],
                                   func=AF.Identity, bias=nbd[:, sidx:sidx + 1],
                                   scale=sd2[:, sidx:sidx + 1]),
                      waits=extra + [(sWD[0], vl0), wdq_free_box[0]],
                      inc=True)
            vdq1 = se(lambda E, sidx=sidx:
                      E.activation(out=wdq[:, 2816:5632], in_=wqd[1][:],
                                   func=AF.Identity, bias=nbd[:, sidx:sidx + 1],
                                   scale=sd2[:, sidx:sidx + 1]),
                      waits=[(sWD[1], vl1)], inc=True)
            wqd_free[0] = (sSE, vdq0)
            wqd_free[1] = (sSE, vdq1)

            def dstw(dc0, g, dt=dt, half=half):
                return WdT[:].rearrange("p (hcx x) -> p hcx x", x=1024)[
                    :, dc0:dc0 + g, half * 512 + dt * 128:
                    half * 512 + (dt + 1) * 128]
            vt, _ = transpose_tile_set(wdq, dstw, None,
                                       waits_first=[(sSE, vdq1),
                                                    WdT_half_free[half]],
                                       ndc=HC)
            wdq_free_box[0] = (sTE, vt)
        wdt_ready[u] = sVE.v

    b3_build(0, b3barrier)
    for u in range(4):
        half = u % 2
        if u + 1 < 4:
            b3_build(u + 1, [])
        for tt in range(16):
            p = tt % 2
            hsrc = bass.AP(h_d.tensor, tt * 128,
                           [[TQ, 128], [128 * TQ, HC], [1, 128]])
            hdst = bass.AP(hld[p], 0, [[HC * 128, 128], [128, HC], [1, 128]])
            vlh = dma("sync", hdst, hsrc, sHL[p],
                      waits=[(sHW[0], v_hw[0]), (sHW[1], v_hw[1]),
                             hld_free[p]])
            bank = 4 + ((u * 16 + tt) % 4)
            dcol0 = u * 512
            for hcc in range(HC):
                w = []
                if hcc == 0:
                    w = [(sHL[p], vlh), (sVE, wdt_ready[u]), bank_free[bank]]
                vt = te(lambda E, p=p, hcc=hcc, half=half, bank=bank:
                        E.matmul(out=PB(bank),
                                 lhsT=hld[p][:, hcc * 128:(hcc + 1) * 128],
                                 rhs=WdT[:, hcc * 1024 + half * 512:
                                        hcc * 1024 + (half + 1) * 512],
                                 start=(hcc == 0), stop=(hcc == HC - 1)),
                        waits=w, inc=(hcc == HC - 1))
            vps = ve(lambda E, bank=bank:
                     E.reduce_sum(out=gtmp[:, 6:7], in_=PB(bank), axis=AX.X),
                     waits=[(sTE, vt), (sVE, v_sffz), sff_add_rel], inc=True)
            sff_add_v = ve(lambda E, tt=tt:
                           E.tensor_add(out=sff[:, tt:tt + 1],
                                        in0=sff[:, tt:tt + 1],
                                        in1=gtmp[:, 6:7]),
                           waits=[(sVE, vps)], inc=True)
            sff_add_rel = (sVE, sff_add_v)
            fp = tt % 2
            vev = ve(lambda E, bank=bank, fp=fp:
                     E.tensor_copy(out=fob[fp][:], in_=PB(bank)),
                     waits=[fob_free[fp]], inc=True)
            vsq = se(lambda E, bank=bank:
                     E.activation(out=junk[:, 0:512], in_=PB(bank),
                                  func=AF.Square, accum_out=gtmp[:, 0:1]),
                     waits=[(sVE, vev), acc_free], inc=True)
            vacc = ve(lambda E, tt=tt:
                      E.tensor_add(out=sff2[:, tt:tt + 1],
                                   in0=sff2[:, tt:tt + 1],
                                   in1=gtmp[:, 0:1]),
                      waits=[(sSE, vsq)], inc=True)
            acc_free = (sVE, vacc)
            bank_free[bank] = (sSE, vsq)
            vwf = dma("scalar", fo_d[tt * 128:(tt + 1) * 128,
                                     dcol0:dcol0 + 512], fob[fp][:],
                      sFW[fp], waits=[(sVE, vev)])
            fob_free[fp] = (sFW[fp], vwf)
            hld_free[p] = (sTE, sTE.v)
        WdT_half_free[half] = (sTE, sTE.v)
    v_fw = [sFW[0].v, sFW[1].v]
    v_b3_v = sVE.v
    if phase == "B3":
        P.emit("sync", None, waits=[(sFW[0], v_fw[0]), (sFW[1], v_fw[1]),
                                    (sY1W, v_y1_written)])
        return _finish(nc, P, es)

    fin_free = [None, None]
    y1in_free = [None, None]
    yout_free = [None, None]
    junk_free = None
    for tt in range(16):
        p = tt % 2
        vlf = dma("sync", fin[p][:], fo_d[tt * 128:(tt + 1) * 128, :], sFI[p],
                  waits=[(sFW[0], v_fw[0]), (sFW[1], v_fw[1]), fin_free[p]])
        vly = dma("sync", y1in[p][:], y1_d[tt * 128:(tt + 1) * 128, :], sYI[p],
                  waits=[(sY1W, v_y1_written), y1in_free[p]])
        st = tt % 2
        c = st * 8
        smu = stats[:, c + 2:c + 3]
        svar = stats[:, c + 3:c + 4]
        slnv = stats[:, c + 4:c + 5]
        srstd = stats[:, c + 5:c + 6]
        srstdv = stats[:, c + 6:c + 7]
        snmr = stats[:, c + 7:c + 8]
        wstat = [(sVE, v_b3_v), stats_free[st]]
        vsm = ve(lambda E, tt=tt, smu=smu:
                 E.tensor_scalar_mul(out=smu, in0=sff[:, tt:tt + 1],
                                     scalar1=INV_D),
                 waits=wstat, inc=True)
        vsv = ve(lambda E, tt=tt, svar=svar:
                 E.tensor_scalar_mul(out=svar, in0=sff2[:, tt:tt + 1],
                                     scalar1=INV_D), inc=True)
        vmm = ve(lambda E, smu=smu: E.tensor_mul(out=gtmp[:, 1:2], in0=smu,
                                                 in1=smu),
                 waits=[(sVE, vsm)], inc=True)
        v1 = ve(lambda E, svar=svar:
                E.scalar_tensor_tensor(out=svar, in0=svar, scalar=EPS,
                                       in1=gtmp[:, 1:2], op0=ALU.add,
                                       op1=ALU.subtract),
                waits=[(sVE, vmm)], inc=True)
        vln4 = se(lambda E, svar=svar, slnv=slnv:
                  E.activation(out=slnv, in_=svar, func=AF.Ln),
                  waits=[(sVE, v1)], inc=True)
        v2 = se(lambda E, slnv=slnv, srstd=srstd:
                E.activation(out=srstd, in_=slnv, func=AF.Exp,
                             bias=np.float32(0.0), scale=np.float32(-0.5)),
                waits=[(sSE, vln4)], inc=True)
        ve(lambda E, srstd=srstd, srstdv=srstdv:
           E.tensor_copy(out=srstdv, in_=srstd), waits=[(sSE, v2)])
        v3 = ve(lambda E, smu=smu, srstd=srstd, snmr=snmr:
                E.scalar_tensor_tensor(out=snmr, in0=smu, scalar=-1.0,
                                       in1=srstd, op0=ALU.mult, op1=ALU.mult),
                inc=True)
        v4 = se(lambda E, p=p, snmr=snmr, srstdv=srstdv:
                E.activation(out=junk[:], in_=fin[p][:], func=AF.Identity,
                             bias=snmr, scale=srstdv),
                waits=[(sVE, v3), (sFI[p], vlf), junk_free], inc=True)
        stats_free[st] = (sSE, v4)
        fin_free[p] = (sSE, v4)
        vy2 = ve(lambda E, p=p: E.tensor_add(out=yout[p][:], in0=y1in[p][:],
                                             in1=junk[:]),
                 waits=[(sSE, v4), (sYI[p], vly), yout_free[p]], inc=True)
        junk_free = (sVE, vy2)
        y1in_free[p] = (sVE, vy2)
        vst = dma("scalar", out_d[tt * 128:(tt + 1) * 128, :], yout[p][:],
                  sOW[p], waits=[(sVE, vy2)])
        yout_free[p] = (sOW[p], vst)

    P.emit("sync", None, waits=[(sOW[0], sOW[0].v), (sOW[1], sOW[1].v)])
    return _finish(nc, P, es)


def _finish(nc, P, es):
    with nc.Block() as block:
        @block.sync
        def _(E):
            P.run("sync", E)

        @block.tensor
        def _(E):
            P.run("tensor", E)

        @block.vector
        def _(E):
            P.run("vector", E)

        @block.scalar
        def _(E):
            P.run("scalar", E)

        @block.gpsimd
        def _(E):
            P.run("gpsimd", E)

    es.close()
    return nc


def kernel(L0_attn_query, L0_attn_key, L0_attn_value, L0_attn_Wqkv,
           L0_attn_Wproj, L0_attn_bproj, L0_ffn_wq_gate, L0_ffn_s_gate,
           L0_ffn_wq_up, L0_ffn_s_up, L0_ffn_wq_down, L0_ffn_s_down):
    xq = np.ascontiguousarray(np.asarray(L0_attn_query, dtype=np.float32))
    Wqkv = np.ascontiguousarray(np.asarray(L0_attn_Wqkv, dtype=np.float32))
    Wproj = np.ascontiguousarray(np.asarray(L0_attn_Wproj, dtype=np.float32))
    bprojv = np.ascontiguousarray(np.asarray(L0_attn_bproj, dtype=np.float32))
    wqg = np.ascontiguousarray(np.asarray(L0_ffn_wq_gate, dtype=np.int32))
    wqu = np.ascontiguousarray(np.asarray(L0_ffn_wq_up, dtype=np.int32))
    wqdn = np.ascontiguousarray(np.asarray(L0_ffn_wq_down, dtype=np.int32))
    sg = np.asarray(L0_ffn_s_gate, dtype=np.float32)
    su = np.asarray(L0_ffn_s_up, dtype=np.float32)
    sd = np.asarray(L0_ffn_s_down, dtype=np.float32)

    sg2 = np.ascontiguousarray(sg.reshape(HC, 128).T)
    su2 = np.ascontiguousarray(su.reshape(HC, 128).T)
    sd2 = np.ascontiguousarray(sd.reshape(16, 128).T)
    ident = np.eye(128, dtype=ml_dtypes.bfloat16)

    nc = build_nc()
    in_maps = []
    for c in range(8):
        j, half = c // 2, c % 2
        xs = []
        for a in range(NB):
            xs.append(xq[a, half * TQ:(half + 1) * TQ])
            xs.append(xq[a, (1 - half) * TQ:(2 - half) * TQ])
        x_all = np.ascontiguousarray(np.concatenate(xs, axis=0))
        x_own_c = np.ascontiguousarray(xq[j, half * TQ:(half + 1) * TQ])
        wqkv_sl = np.ascontiguousarray(np.concatenate(
            [Wqkv[(4 * r + j) * 384:(4 * r + j + 1) * 384] for r in range(R)],
            axis=0))
        in_maps.append({
            "x_all": x_all, "x_own": x_own_c, "wqkv_s": wqkv_sl,
            "wproj": Wproj, "bproj": bprojv,
            "wq_gate": wqg, "wq_up": wqu, "wq_down": wqdn,
            "s_gate2": sg2, "s_up2": su2, "s_down2": sd2,
            "ident": ident,
        })
    res = run_bass_kernel_spmd(nc, in_maps, core_ids=list(range(8)))
    out = np.empty((NB, S, D), dtype=np.float32)
    for c in range(8):
        j, half = c // 2, c % 2
        out[j, half * TQ:(half + 1) * TQ] = res.results[c]["out"]
    return out


# version 09
# speedup vs baseline: 1.2687x; 1.0411x over previous
import numpy as np
import ml_dtypes
import concourse.bass as bass
import concourse.mybir as mybir
from concourse.bass_utils import run_bass_kernel_spmd

F32 = mybir.dt.float32
BF16 = mybir.dt.bfloat16
I32 = mybir.dt.int32
AF = mybir.ActivationFunctionType
ALU = mybir.AluOpType
AX = mybir.AxisListType

NB = 4
S = 4096
D = 2048
R = 4
HID = 5632
TQ = 2048
NORM = 1.0 / np.sqrt(128.0)
EPS = 1e-5
NSTRIP = 16
QB = 256
NQB = TQ // QB
NKC = S // 128
HC = HID // 128
INV_D = 1.0 / D
SB_BASE = 18432
SBUF_LIMIT = 229120


class Sem:
    def __init__(self, nc, es, name):
        self.name = name
        self.h = es.enter_context(nc.semaphore(name))
        self.v = 0

    def bump(self, n):
        self.v += n
        return self.v


class Prog:
    ENGS = ("sync", "tensor", "vector", "scalar", "gpsimd")

    def __init__(self):
        self.streams = {e: [] for e in self.ENGS}
        self.hwm = {e: {} for e in self.ENGS}

    def emit(self, eng, fn, waits=(), incs=()):
        ws = []
        for w in waits:
            if w is None:
                continue
            s, v = w
            if v is None:
                continue
            if self.hwm[eng].get(s.name, -1) >= v:
                continue
            self.hwm[eng][s.name] = v
            ws.append((s, v))
        self.streams[eng].append((tuple(ws), fn, tuple(incs)))

    def run(self, eng_name, E):
        for ws, fn, incs in self.streams[eng_name]:
            for s, v in ws:
                E.wait_ge(s.h, v)
            if fn is None:
                continue
            ins = fn(E)
            for s, n in incs:
                ins = ins.then_inc(s.h, n)


def build_nc(phase="full"):
    nc = bass.Bass("TRN2", target_bir_lowering=False)
    from contextlib import ExitStack
    es = ExitStack()

    x_all = nc.dram_tensor("x_all", [NB * S, D], F32, kind="ExternalInput").ap()
    x_own = nc.dram_tensor("x_own", [TQ, D], F32, kind="ExternalInput").ap()
    wqkv_s = nc.dram_tensor("wqkv_s", [R * 384, D], F32, kind="ExternalInput").ap()
    wproj = nc.dram_tensor("wproj", [D, D], F32, kind="ExternalInput").ap()
    bproj = nc.dram_tensor("bproj", [D], F32, kind="ExternalInput").ap()
    wq_gate = nc.dram_tensor("wq_gate", [HID, D], I32, kind="ExternalInput").ap()
    wq_up = nc.dram_tensor("wq_up", [HID, D], I32, kind="ExternalInput").ap()
    wq_down = nc.dram_tensor("wq_down", [D, HID], I32, kind="ExternalInput").ap()
    s_gate2 = nc.dram_tensor("s_gate2", [128, HC], F32, kind="ExternalInput").ap()
    s_up2 = nc.dram_tensor("s_up2", [128, HC], F32, kind="ExternalInput").ap()
    s_down2 = nc.dram_tensor("s_down2", [128, 16], F32, kind="ExternalInput").ap()
    ident_in = nc.dram_tensor("ident", [128, 128], BF16, kind="ExternalInput").ap()
    out_d = nc.dram_tensor("out", [TQ, D], F32, kind="ExternalOutput").ap()

    _k = lambda p: "ExternalOutput" if phase == p else "Internal"
    ctx_d = nc.dram_tensor("ctx_d", [TQ, D], BF16, kind=_k("A")).ap()
    h_d = nc.dram_tensor("h_d", [HID, TQ], BF16, kind=_k("B2")).ap()
    y1_d = nc.dram_tensor("y1_d", [TQ, D], F32, kind=_k("B1")).ap()
    fo_d = nc.dram_tensor("fo_d", [TQ, D], BF16, kind=_k("B3")).ap()

    OFF = [SB_BASE]

    def sb(name, shape, dtype):
        nbytes = int(np.prod(shape[1:])) * mybir.dt.size(dtype)
        o = (OFF[0] + 31) // 32 * 32
        h = nc.alloc_sbuf_tensor_at(name, shape, dtype, offset=o)
        OFF[0] = o + nbytes
        assert OFF[0] <= SBUF_LIMIT, f"sbuf overflow at {name}: {OFF[0]}"
        return h

    identS = sb("identS", [128, 128], BF16)
    ones128 = sb("ones128", [128, 128], BF16)
    sg2 = sb("sg2", [128, HC], F32)
    su2 = sb("su2", [128, HC], F32)
    sd2 = sb("sd2", [128, 16], F32)
    nbg = sb("nbg", [128, HC], F32)
    nbu = sb("nbu", [128, HC], F32)
    nbd = sb("nbd", [128, 16], F32)
    stats = sb("stats", [128, 16], F32)
    ssq2 = sb("ssq2", [128, 2], F32)
    maxK = sb("maxK", [128, 64], F32)
    maxQ = sb("maxQ", [128, 32], F32)
    gtmp = sb("gtmp", [128, 8], F32)
    dcol = sb("dcol", [128, 2], F32)
    rdcol = sb("rdcol", [128, 2], F32)
    sff = sb("sff", [128, 16], F32)
    sff2 = sb("sff2", [128, 16], F32)
    PERS_END = OFF[0]

    wqkvT = sb("wqkvT", [128, 16 * 1536], BF16)
    Kst = sb("Kst", [128, R * S], BF16)
    Vst = sb("Vst", [128, 32 * 520], BF16)
    Qst = sb("Qst", [128, R * TQ], BF16)
    Pt = [sb("Pt0", [128, NKC * QB], BF16), sb("Pt1", [128, NKC * QB], BF16)]
    xin = [sb("xin0", [128, 2048], F32), sb("xin1", [128, 2048], F32)]
    xln = [sb("xln0", [128, 2048], BF16), sb("xln1", [128, 2048], BF16)]
    xT = [sb("xT0", [128, 16 * 256], BF16), sb("xT1", [128, 16 * 256], BF16)]
    Ksqb = sb("Ksqb", [128, 512], BF16)
    ctxs = [sb("ctxs0", [128, 128], BF16), sb("ctxs1", [128, 128], BF16)]
    assert OFF[0] <= SBUF_LIMIT

    OFF[0] = PERS_END
    junk = sb("junk", [128, 2048], BF16)
    BJ_END = OFF[0]
    WprojT = sb("WprojT", [128, 16 * 2048], BF16)
    zT = sb("zT", [128, 16 * 2048], BF16)
    bpb = sb("bpb", [128, 2048], F32)
    B_COMMON_END = OFF[0]

    B1W_START = OFF[0]
    wpl_ = sb("wpl0", [128, 2048], F32)
    wpl = [wpl_, wpl_]
    ctxin = [sb("ctxin0", [128, 2048], BF16), sb("ctxin1", [128, 2048], BF16)]
    ctxTb = [sb("ctxT0", [128, 2048], BF16), sb("ctxT1", [128, 2048], BF16)]
    attnout = sb("attnout", [128, 2048], F32)
    xo_ = sb("xo0", [128, 2048], F32)
    xo = [xo_, xo_]
    y1t = sb("y1t", [128, 2048], F32)
    zt = [sb("zt0", [128, 2048], BF16), sb("zt1", [128, 2048], BF16)]
    ln3t = sb("ln3t", [128, 2048], BF16)
    wplb = ln3t
    assert OFF[0] <= SBUF_LIMIT

    OFF[0] = BJ_END
    wqg = [sb("wqg0", [128, 2048], I32), sb("wqg1", [128, 2048], I32)]
    wgd = sb("wgd", [128, 2048], BF16)
    wud = sb("wud", [128, 2048], BF16)
    WgT = sb("WgT", [128, 2048], BF16)
    WuT = sb("WuT", [128, 2048], BF16)
    gbuf = sb("gbuf", [128, 2048], BF16)
    tbuf = sb("tbuf", [128, 2048], BF16)
    ubuf = sb("ubuf", [128, 2048], BF16)
    abuf = sb("abuf", [128, 2048], BF16)
    hbuf = [sb("hbuf0", [128, 2048], BF16), sb("hbuf1", [128, 2048], BF16)]
    assert OFF[0] <= BJ_END + 16 * 2048 * 2, "B2 must fit inside WprojT region"
    OFF[0] = B1W_START
    wqgu = [sb("wqgu0", [128, 2048], I32), sb("wqgu1", [128, 2048], I32)]

    OFF[0] = BJ_END
    WdT = sb("WdT", [128, HC * 1024], BF16)
    wqd = [sb("wqd0", [128, 2816], I32), sb("wqd1", [128, 2816], I32)]
    wdq = sb("wdq", [128, HID], BF16)
    hld = [sb("hld0", [128, HC * 128], BF16), sb("hld1", [128, HC * 128], BF16)]
    fob = [sb("fob0", [128, 512], BF16), sb("fob1", [128, 512], BF16)]
    fin = [sb("fin0", [128, 2048], BF16), sb("fin1", [128, 2048], BF16)]
    y1in = [sb("y1in0", [128, 2048], F32), sb("y1in1", [128, 2048], F32)]
    yout = [sb("yout0", [128, 2048], F32), sb("yout1", [128, 2048], F32)]
    assert OFF[0] <= SBUF_LIMIT

    pbT = [nc.alloc_psum_tensor(f"pbT{i}", [128, 1024], F32) for i in range(2)]
    pb4 = nc.alloc_psum_tensor("pb4", [128, 512], F32)
    pb5 = nc.alloc_psum_tensor("pb5", [128, 512], F32)
    pbS = nc.alloc_psum_tensor("pbS", [128, 1024], F32)

    def PB(slot, a=0, b=512):
        if slot == 4:
            return pb4[:, a:b]
        if slot == 5:
            return pb5[:, a:b]
        if slot == 6:
            return pbS[:, a:b]
        return pbS[:, 512 + a:512 + b]

    sWA = Sem(nc, es, "sWA")
    sX = [Sem(nc, es, "sX0"), Sem(nc, es, "sX1")]
    sWPL = [Sem(nc, es, "sWPL0"), Sem(nc, es, "sWPL1")]
    sXO = Sem(nc, es, "sXO")
    sBPB = Sem(nc, es, "sBPB")
    sCI = [Sem(nc, es, "sCI0"), Sem(nc, es, "sCI1")]
    sCW = [Sem(nc, es, "sCW0"), Sem(nc, es, "sCW1")]
    sWQ = [Sem(nc, es, "sWQ0"), Sem(nc, es, "sWQ1")]
    sWU = [Sem(nc, es, "sWU0"), Sem(nc, es, "sWU1")]
    sWD = [Sem(nc, es, "sWD0"), Sem(nc, es, "sWD1")]
    sHW = [Sem(nc, es, "sHW0"), Sem(nc, es, "sHW1")]
    sHL = [Sem(nc, es, "sHL0"), Sem(nc, es, "sHL1")]
    sFW = [Sem(nc, es, "sFW0"), Sem(nc, es, "sFW1")]
    sFI = [Sem(nc, es, "sFI0"), Sem(nc, es, "sFI1")]
    sYI = [Sem(nc, es, "sYI0"), Sem(nc, es, "sYI1")]
    sY1W = Sem(nc, es, "sY1W")
    sOW = [Sem(nc, es, "sOW0"), Sem(nc, es, "sOW1")]
    sTE = Sem(nc, es, "sTE")
    sVE = Sem(nc, es, "sVE")
    sSE = Sem(nc, es, "sSE")
    sGE = Sem(nc, es, "sGE")

    P = Prog()

    def dma(eng, out, in_, sem, waits=()):
        P.emit(eng, lambda E, o=out, i=in_: E.dma_start(out=o, in_=i),
               waits=waits, incs=((sem, 16),))
        return sem.bump(16)

    def te(fn, waits=(), inc=False):
        P.emit("tensor", fn, waits=waits, incs=((sTE, 1),))
        return sTE.bump(1)

    def ve(fn, waits=(), inc=False):
        P.emit("vector", fn, waits=waits, incs=((sVE, 1),) if inc else ())
        return sVE.bump(1) if inc else None

    def se(fn, waits=(), inc=False):
        P.emit("scalar", fn, waits=waits, incs=((sSE, 1),) if inc else ())
        return sSE.bump(1) if inc else None

    def ge(fn, waits=(), inc=False):
        P.emit("gpsimd", fn, waits=waits, incs=((sGE, 1),) if inc else ())
        return sGE.bump(1) if inc else None

    def vdr():
        P.emit("vector", lambda E: E.drain())

    def sdr():
        P.emit("scalar", lambda E: E.drain())

    bank_free = [None] * 8

    trot = [0]

    def transpose_tile_set(srcbuf, dstfn, banks_unused, waits_first,
                           extra_evac_wait=None, ndc=16, koff=0):
        vt = None
        done = 0
        while done < ndc:
            g = min(8, ndc - done)
            bank = trot[0] % 2
            trot[0] += 1
            for q in range(g):
                dc = done + q
                w = []
                if done == 0 and q == 0:
                    w = list(waits_first)
                if q == 0:
                    w = w + [bank_free[bank]]
                vt = te(lambda E, dc=dc, bank=bank, q=q:
                        E.matmul(out=pbT[bank][:, q * 128:(q + 1) * 128],
                                 lhsT=srcbuf[:, koff + dc * 128: koff + (dc + 1) * 128],
                                 rhs=identS[:], start=True, stop=True),
                        waits=w)
            ew = [(sTE, vt)]
            if extra_evac_wait is not None:
                ew.append(extra_evac_wait)
            vv = ve(lambda E, dst=dstfn(done, g), bank=bank, g=g:
                    E.tensor_copy(out=dst,
                                  in_=pbT[bank][:].rearrange(
                                      "p (c t) -> p c t", t=128)[:, 0:g, :]),
                    waits=ew, inc=True)
            bank_free[bank] = (sVE, vv)
            done += g
        return vt, sVE.v

    def norm_mm(rhs_ap, waits, width=256):
        bank = trot[0] % 2
        trot[0] += 1
        vt = te(lambda E, bank=bank, width=width:
                E.matmul(out=pbT[bank][:, 0:width], lhsT=ones128[:], rhs=rhs_ap,
                         start=True, stop=True),
                waits=list(waits) + [bank_free[bank]])
        return bank, vt

    v_ident = dma("sync", identS[:], ident_in, sWA)
    v_sg = dma("sync", sg2[:], s_gate2, sWA)
    v_su = dma("sync", su2[:], s_up2, sWA)
    v_sd = dma("sync", sd2[:], s_down2, sWA)

    ve(lambda E: E.memset(ones128[:], 1.0))
    ve(lambda E: E.tensor_scalar_mul(out=nbg[:], in0=sg2[:], scalar1=-128.0),
       waits=[(sWA, 64)])
    ve(lambda E: E.tensor_scalar_mul(out=nbu[:], in0=su2[:], scalar1=-128.0))
    v_nb = ve(lambda E: E.tensor_scalar_mul(out=nbd[:], in0=sd2[:], scalar1=-128.0),
              inc=True)

    xin_free = [None, None]
    xln_free = [None, None]
    for ob in range(12):
        p = ob % 2
        vl = dma("sync", xin[p][:], wqkv_s[ob * 128:(ob + 1) * 128, :], sX[p],
                 waits=[xin_free[p]])
        vc = ve(lambda E, p=p: E.tensor_copy(out=xln[p][:], in_=xin[p][:]),
                waits=[(sX[p], vl), xln_free[p], (sWA, 64)], inc=True)
        xin_free[p] = (sVE, vc)

        def dst4(dc0, g, ob=ob):
            return wqkvT[:].rearrange("p (dc o) -> p dc o", o=1536)[
                :, dc0:dc0 + g, ob * 128:(ob + 1) * 128]
        vt, _ = transpose_tile_set(xln[p], dst4, None,
                                   waits_first=[(sVE, vc)])
        xln_free[p] = (sTE, vt)
    v_wqkvT = sVE.v

    kst_free = None
    vst_free = None
    pt_free = [None, None]
    ctxs_free = [None, None]
    stats_free = [None, None]
    xT_free = [None, None]
    exp_reg_rel = {0: None, 1: None}
    ctx_bank_rel = {4: None, 5: None}
    gbias_free = None
    ksqb_free = [None, None]

    def ln_chain(tile_idx, src_ap, dst_ap, data_ready, dst_free=None, scr=None):
        if scr is None:
            scr = junk[:]
        st = tile_idx % 2
        c = st * 8
        ssum = stats[:, c + 0:c + 1]
        ssq = stats[:, c + 1:c + 2]
        smu = stats[:, c + 2:c + 3]
        smu2 = stats[:, c + 3:c + 4]
        snm2e = stats[:, c + 4:c + 5]
        srstd = stats[:, c + 5:c + 6]
        snegr = stats[:, c + 6:c + 7]
        snmr = stats[:, c + 7:c + 8]
        wfree = [stats_free[st]] + ([dst_free] if dst_free else [])
        vq = se(lambda E: E.activation(out=scr, in_=src_ap, func=AF.Square,
                                       accum_out=ssq),
                waits=list(data_ready) + wfree, inc=True)
        vs = se(lambda E: E.activation(out=scr, in_=src_ap, func=AF.Identity,
                                       accum_out=ssum),
                waits=[(sSE, vq)], inc=True)
        v1 = se(lambda E: E.mul(out=smu, in_=ssum, mul=INV_D),
                waits=[(sSE, vs)], inc=True)
        v2 = se(lambda E: E.activation(out=smu2, in_=smu, func=AF.Square),
                waits=[(sSE, v1)], inc=True)
        v3 = se(lambda E: E.activation(out=snm2e, in_=smu2, func=AF.Identity,
                                       bias=np.float32(EPS),
                                       scale=np.float32(-1.0)),
                waits=[(sSE, v2)], inc=True)
        v4 = se(lambda E: E.activation(out=srstd, in_=ssq, func=AF.Ln,
                                       bias=snm2e, scale=np.float32(INV_D)),
                waits=[(sSE, v3)], inc=True)
        v5 = se(lambda E: E.activation(out=srstd, in_=srstd, func=AF.Exp,
                                       bias=np.float32(0.0),
                                       scale=np.float32(-0.5)),
                waits=[(sSE, v4)], inc=True)
        v6 = se(lambda E: E.mul(out=snegr, in_=srstd, mul=-1.0),
                waits=[(sSE, v5)], inc=True)
        v7 = se(lambda E: E.activation(out=snmr, in_=smu, func=AF.Identity,
                                       bias=np.float32(0.0), scale=snegr),
                waits=[(sSE, v6)], inc=True)
        v8 = se(lambda E: E.activation(out=dst_ap, in_=src_ap, func=AF.Identity,
                                       bias=snmr, scale=srstd),
                waits=[(sSE, v7)], inc=True)
        stats_free[st] = (sSE, v8)
        return v8

    a_chain = {}
    a_xtr = {}

    def a1_chain(a, n, k):
        tp = (n * 2 + k) % 2
        row0 = a * S + n * 256
        vl = dma("sync", xin[tp][:],
                 x_all[row0 + k * 128: row0 + (k + 1) * 128, :], sX[tp],
                 waits=[xin_free[tp]])
        v_ln = ln_chain(n * 2 + k, xin[tp][:], xln[tp][:],
                        [(sX[tp], vl)], dst_free=xln_free[tp],
                        scr=xln[tp][:])
        xin_free[tp] = (sSE, v_ln)
        a_chain[(n, k)] = v_ln

    def a1_trans(n, k):
        tp = (n * 2 + k) % 2
        p = n % 2

        def dst4(dc0, g, p=p, k=k):
            return xT[p][:].rearrange("p (dc t) -> p dc t", t=256)[
                :, dc0:dc0 + g, k * 128:(k + 1) * 128]
        vt, _ = transpose_tile_set(xln[tp], dst4, None,
                                   waits_first=[(sSE, a_chain[(n, k)])],
                                   extra_evac_wait=xT_free[p])
        xln_free[tp] = (sTE, vt)
        if k == 1:
            a_xtr[n] = sVE.v

    for a in range(NB):
        v_vmset = ve(lambda E: E.memset(Vst[:], 1.0), waits=[vst_free], inc=True)
        for k in range(2):
            a1_chain(a, 0, k)
            a1_trans(0, k)
        for n in range(NSTRIP):
            p = n % 2
            xT_ready = a_xtr[n]
            if n + 1 < NSTRIP:
                a1_chain(a, n + 1, 0)
                a1_chain(a, n + 1, 1)

            for r in range(R):
                bank = 4 + r
                for dc in range(16):
                    w = []
                    if dc == 0:
                        w = [(sVE, xT_ready), bank_free[bank]]
                    vt = te(lambda E, r=r, dc=dc, bank=bank, p=p:
                            E.matmul(out=PB(bank, 0, 256),
                                     lhsT=wqkvT[:, dc * 1536 + r * 384 + 128:
                                                dc * 1536 + r * 384 + 256],
                                     rhs=xT[p][:, dc * 256:(dc + 1) * 256],
                                     start=(dc == 0), stop=(dc == 15)),
                            waits=w, inc=(dc == 15))
                vek = ve(lambda E, r=r, n=n, bank=bank:
                         E.tensor_copy(out=Kst[:, r * S + n * 256: r * S + (n + 1) * 256],
                                       in_=PB(bank, 0, 256)),
                         waits=[(sTE, vt), kst_free], inc=True)
                bank_free[bank] = (sVE, vek)
                if n % 2 == 1:
                    vsq = ve(lambda E, r=r, n=n:
                             E.tensor_mul(out=Ksqb[:],
                                          in0=Kst[:, r * S + (n - 1) * 256:
                                                 r * S + (n + 1) * 256],
                                          in1=Kst[:, r * S + (n - 1) * 256:
                                                 r * S + (n + 1) * 256]),
                             waits=[(sVE, vek), ksqb_free[0]], inc=True)
                    nbank, vn = norm_mm(Ksqb[:], [(sVE, vsq)], 512)
                    vm = ve(lambda E, r=r, n=n, nbank=nbank:
                            E.reduce_max(out=maxK[:, r * 8 + n // 2:
                                               r * 8 + n // 2 + 1],
                                         in_=pbT[nbank][:, 0:512], axis=AX.X),
                            waits=[(sTE, vn)], inc=True)
                    bank_free[nbank] = (sVE, vm)
                    ksqb_free[0] = (sTE, vn)
            if n + 1 < NSTRIP:
                a1_trans(n + 1, 0)

            if n < 8:
                for r in range(R):
                    bank = 4 + r
                    for dc in range(16):
                        w = []
                        if dc == 0:
                            w = [bank_free[bank]]
                        vt = te(lambda E, r=r, dc=dc, bank=bank, p=p:
                                E.matmul(out=PB(bank, 0, 256),
                                         lhsT=wqkvT[:, dc * 1536 + r * 384:
                                                    dc * 1536 + r * 384 + 128],
                                         rhs=xT[p][:, dc * 256:(dc + 1) * 256],
                                         start=(dc == 0), stop=(dc == 15)),
                                waits=w, inc=(dc == 15))
                    veq = ve(lambda E, r=r, n=n, bank=bank:
                             E.tensor_copy(out=Qst[:, r * TQ + n * 256:
                                                  r * TQ + (n + 1) * 256],
                                           in_=PB(bank, 0, 256)),
                             waits=[(sTE, vt), kst_free], inc=True)
                    bank_free[bank] = (sVE, veq)
                    if n % 2 == 1:
                        vsq = ve(lambda E, r=r, n=n:
                                 E.tensor_mul(out=Ksqb[:],
                                              in0=Qst[:, r * TQ + (n - 1) * 256:
                                                     r * TQ + (n + 1) * 256],
                                              in1=Qst[:, r * TQ + (n - 1) * 256:
                                                     r * TQ + (n + 1) * 256]),
                                 waits=[(sVE, veq), ksqb_free[1]], inc=True)
                        nbank, vn = norm_mm(Ksqb[:], [(sVE, vsq)], 512)
                        vm = ve(lambda E, r=r, n=n, nbank=nbank:
                                E.reduce_max(out=maxQ[:, r * 4 + n // 2:
                                                   r * 4 + n // 2 + 1],
                                             in_=pbT[nbank][:, 0:512],
                                             axis=AX.X),
                                waits=[(sTE, vn)], inc=True)
                        bank_free[nbank] = (sVE, vm)
                        ksqb_free[1] = (sTE, vn)
            if n + 1 < NSTRIP:
                a1_trans(n + 1, 1)

            for k in range(2):
                bank = 4 + k
                for r in range(R):
                    for dc in range(16):
                        w = []
                        if r == 0 and dc == 0:
                            w = [(sVE, v_vmset), bank_free[bank]]
                        vt = te(lambda E, k=k, r=r, dc=dc, bank=bank, p=p:
                                E.matmul(out=PB(bank, r * 128, (r + 1) * 128),
                                         lhsT=xT[p][:, dc * 256 + k * 128:
                                                   dc * 256 + k * 128 + 128],
                                         rhs=wqkvT[:, dc * 1536 + r * 384 + 256:
                                                   dc * 1536 + r * 384 + 384],
                                         start=(dc == 0), stop=(dc == 15)),
                                waits=w, inc=(r == 3 and dc == 15))
                tc = n * 2 + k
                dstv = Vst[:].rearrange("p (tc r x) -> p tc r x", r=4, x=130)[
                    :, tc, :, 0:128]
                vev = ve(lambda E, dstv=dstv, bank=bank:
                         E.tensor_copy(out=dstv,
                                       in_=PB(bank).rearrange(
                                           "p (r x) -> p r x", x=128)),
                         waits=[(sTE, vt)], inc=True)
                bank_free[bank] = (sVE, vev)
            xT_free[p] = (sTE, sTE.v)

        vst_ready = sVE.v
        v_norm_done = sVE.v
        for r in range(R):
            i_block = 4 * a + r
            vk1 = ve(lambda E, r=r: E.reduce_max(out=gtmp[:, 0:1],
                                                 in_=maxK[:, r * 8:(r + 1) * 8],
                                                 axis=AX.X),
                     waits=[gbias_free, (sVE, v_norm_done)], inc=True)
            vk2 = ve(lambda E, r=r: E.reduce_max(out=gtmp[:, 1:2],
                                                 in_=maxQ[:, r * 4:(r + 1) * 4],
                                                 axis=AX.X), inc=True)
            v1 = ve(lambda E: E.tensor_mul(out=gtmp[:, 2:3], in0=gtmp[:, 0:1],
                                           in1=gtmp[:, 1:2]),
                    waits=[(sVE, vk2)], inc=True)
            vln_g = se(lambda E: E.activation(out=gtmp[:, 3:4], in_=gtmp[:, 2:3],
                                              func=AF.Ln),
                       waits=[(sVE, v1)], inc=True)
            v2 = se(lambda E: E.activation(out=gtmp[:, 4:5], in_=gtmp[:, 3:4],
                                           func=AF.Exp, bias=np.float32(0.0),
                                           scale=np.float32(0.5)),
                    waits=[(sSE, vln_g)], inc=True)
            vbias = ve(lambda E: E.tensor_scalar_mul(out=gtmp[:, 5:6],
                                                     in0=gtmp[:, 4:5],
                                                     scalar1=-float(NORM)),
                       waits=[(sSE, v2)], inc=True)

            for qb in range(NQB):
                ptp = qb % 2
                for m2 in range(8):
                    reg = m2 % 2
                    rten = pbS if reg == 0 else pbT[1]
                    rslot = 6 if reg == 0 else 1
                    for kco in range(4):
                        kc = 4 * m2 + kco
                        w = [exp_reg_rel[reg]]
                        if m2 in (0, 1) and kco == 0:
                            w += [bank_free[rslot]]
                            if reg == 0:
                                w += [bank_free[7]]
                        if kc == 0:
                            w += [(sVE, vst_ready), (sVE, vbias)]
                        vt = te(lambda E, r=r, kc=kc, qb=qb, rten=rten:
                                E.matmul(out=rten[:, (kc % 4) * 256:
                                                  (kc % 4) * 256 + 256],
                                         lhsT=Kst[:, r * S + kc * 128:
                                                r * S + (kc + 1) * 128],
                                         rhs=Qst[:, r * TQ + qb * 256:
                                                r * TQ + (qb + 1) * 256],
                                         start=True, stop=True),
                                waits=w)
                    wx = [(sTE, vt)]
                    if m2 == 0:
                        wx += [pt_free[ptp]]
                    vxp = se(lambda E, m2=m2, rten=rten, ptp=ptp:
                             E.activation(out=Pt[ptp][:, m2 * 1024:
                                                     (m2 + 1) * 1024],
                                          in_=rten[:], func=AF.Exp,
                                          bias=gtmp[:, 5:6],
                                          scale=np.float32(NORM)),
                             waits=wx, inc=True)
                    exp_reg_rel[reg] = (sSE, vxp)
                exp_done = sSE.v
                for qt in range(2):
                    bank = 4 + qt
                    for kc in range(NKC):
                        w = []
                        if kc == 0:
                            w = [(sSE, exp_done), ctx_bank_rel[bank]]
                        vt = te(lambda E, r=r, qt=qt, kc=kc, bank=bank, ptp=ptp:
                                E.matmul(out=PB(bank, 0, 129),
                                         lhsT=Pt[ptp][:, kc * 256 + qt * 128:
                                                     kc * 256 + qt * 128 + 128],
                                         rhs=Vst[:, kc * 520 + r * 130:
                                                 kc * 520 + r * 130 + 129],
                                         start=(kc == 0), stop=(kc == NKC - 1)),
                                waits=w, inc=(kc == NKC - 1))
                    vd = ve(lambda E, bank=bank, qt=qt:
                            E.tensor_copy(out=dcol[:, qt:qt + 1],
                                          in_=PB(bank, 128, 129)),
                            waits=[(sTE, vt)], inc=True)
                    vr = ve(lambda E, qt=qt:
                            E.reciprocal(out=rdcol[:, qt:qt + 1],
                                         in_=dcol[:, qt:qt + 1]),
                            waits=[(sVE, vd)], inc=True)
                    tile = qb * 2 + qt
                    vnorm = se(lambda E, bank=bank, qt=qt:
                               E.activation(out=ctxs[qt][:],
                                            in_=PB(bank, 0, 128),
                                            func=AF.Identity,
                                            bias=np.float32(0.0),
                                            scale=rdcol[:, qt:qt + 1]),
                               waits=[(sVE, vr), ctxs_free[qt]], inc=True)
                    ctx_bank_rel[bank] = (sSE, vnorm)
                    vctx = dma("scalar",
                               ctx_d[tile * 128:(tile + 1) * 128,
                                     i_block * 128:(i_block + 1) * 128],
                               ctxs[qt][:], sCW[qt], waits=[(sSE, vnorm)])
                    ctxs_free[qt] = (sCW[qt], vctx)
                pt_free[ptp] = (sTE, sTE.v)
            gbias_free = (sSE, sSE.v)
        kst_free = (sTE, sTE.v)
        vst_free = (sTE, sTE.v)
        bank_free[4] = ctx_bank_rel[4]
        bank_free[5] = ctx_bank_rel[5]
        bank_free[6] = exp_reg_rel[0]
        bank_free[7] = exp_reg_rel[0]
        bank_free[1] = exp_reg_rel[1]

    if phase == "A":
        P.emit("sync", None, waits=[(sCW[0], sCW[0].v), (sCW[1], sCW[1].v)])
        return _finish(nc, P, es)
    barrier = [(sTE, sTE.v), (sVE, sVE.v), (sSE, sSE.v), (sGE, sGE.v),
               (sCW[0], sCW[0].v), (sCW[1], sCW[1].v)]

    rep_src = bass.AP(bproj.tensor, 0, [[0, 128], [1, 2048]])
    v_bpb = dma("sync", bpb[:], rep_src, sBPB, waits=barrier)

    zoff = zT.manual_sbuf_range[0]
    wplA = [nc.alloc_sbuf_tensor_at("wplA0", [128, 4 * 2048], F32, offset=zoff),
            nc.alloc_sbuf_tensor_at("wplA1", [128, 4 * 2048], F32,
                                    offset=zoff + 4 * 2048 * 4)]
    wplA_free = [None, None]
    wplb_free = None
    for blk in range(4):
        bp = blk % 2
        wsrc = bass.AP(wproj.tensor, blk * 512 * 2048,
                       [[2048, 128], [128 * 2048, 4], [1, 2048]])
        wdst = bass.AP(wplA[bp], 0, [[8192, 128], [2048, 4], [1, 2048]])
        vl = dma("sync", wdst, wsrc, sWPL[bp],
                 waits=barrier + [wplA_free[bp]])
        for t in range(4):
            ob = blk * 4 + t
            vc = ve(lambda E, bp=bp, t=t:
                    E.tensor_copy(out=wplb[:],
                                  in_=wplA[bp][:, t * 2048:(t + 1) * 2048]),
                    waits=barrier + [(sWPL[bp], vl), wplb_free], inc=True)

            def dst4(dc0, g, ob=ob):
                return WprojT[:].rearrange("p (dc o) -> p dc o", o=2048)[
                    :, dc0:dc0 + g, ob * 128:(ob + 1) * 128]
            vt, _ = transpose_tile_set(wplb, dst4, None,
                                       waits_first=barrier + [(sVE, vc)])
            wplb_free = (sTE, vt)
        wplA_free[bp] = (sVE, sVE.v)
    v_wprojT = sVE.v
    b0_ln3_guard = wplb_free

    all_ctx0 = sCW[0].v
    all_ctx1 = sCW[1].v
    ctxin_free = [None, None]
    ctxT_free = [None, None]
    xo_free = None
    attnout_free = None
    y1t_free = []
    zt_free = [None, None]
    zt_pending = []
    ln3_free = b0_ln3_guard
    vT_ready = {}

    def b1_stage1(tt, extra):
        p = tt % 2
        vl = dma("sync", ctxin[p][:], ctx_d[tt * 128:(tt + 1) * 128, :],
                 sCI[p], waits=extra + [ctxin_free[p]])

        def dst4(dc0, g, p=p):
            return ctxTb[p][:].rearrange("p (dc t) -> p dc t", t=128)[
                :, dc0:dc0 + g, :]
        vt, _ = transpose_tile_set(ctxin[p], dst4, None,
                                   waits_first=[(sCI[p], vl)],
                                   extra_evac_wait=ctxT_free[p])
        ctxin_free[p] = (sTE, vt)
        vT_ready[tt] = sVE.v

    b1_stage1(0, barrier + [(sCW[0], all_ctx0), (sCW[1], all_ctx1)])
    for tt in range(16):
        p = tt % 2
        if tt + 1 < 16:
            b1_stage1(tt + 1, [])
        vxo = dma("sync", xo[p][:], x_own[tt * 128:(tt + 1) * 128, :], sXO,
                  waits=barrier + [xo_free])
        for dob in range(4):
            bank = 4 + dob
            for dc in range(16):
                w = []
                if dc == 0:
                    w = [(sVE, vT_ready[tt]), (sVE, v_wprojT), bank_free[bank]]
                vt = te(lambda E, p=p, dob=dob, dc=dc, bank=bank:
                        E.matmul(out=PB(bank),
                                 lhsT=ctxTb[p][:, dc * 128:(dc + 1) * 128],
                                 rhs=WprojT[:, dc * 2048 + dob * 512:
                                           dc * 2048 + (dob + 1) * 512],
                                 start=(dc == 0), stop=(dc == 15)),
                        waits=w, inc=(dc == 15))
            vev = ve(lambda E, dob=dob, bank=bank:
                     E.tensor_tensor(out=attnout[:, dob * 512:(dob + 1) * 512],
                                     in0=PB(bank),
                                     in1=bpb[:, dob * 512:(dob + 1) * 512],
                                     op=ALU.add),
                     waits=[(sTE, vt), (sBPB, v_bpb), attnout_free], inc=True)
            bank_free[bank] = (sVE, vev)
        v_ao = sVE.v
        ctxT_free[p] = (sTE, sTE.v)
        v_ln3 = ln_chain(tt, attnout[:], ln3t[:], [(sVE, v_ao)],
                         dst_free=ln3_free)
        attnout_free = (sSE, v_ln3)
        vy1 = ve(lambda E, p=p: E.tensor_add(out=y1t[:], in0=xo[p][:],
                                             in1=ln3t[:]),
                 waits=[(sSE, v_ln3), (sXO, vxo)] + y1t_free, inc=True)
        ln3_free = (sVE, vy1)
        xo_free = (sVE, vy1)
        vw1 = dma("scalar", y1_d[tt * 128:(tt + 1) * 128, :], y1t[:], sY1W,
                  waits=[(sVE, vy1)])
        v_ln2 = ln_chain(tt, y1t[:], zt[tt % 2][:], [(sVE, vy1)],
                         dst_free=zt_free[tt % 2])
        y1t_free = [(sY1W, vw1), (sSE, v_ln2)]
        zt_pending.append((tt, v_ln2))
        if len(zt_pending) == 2 or tt == 15:
            for ptt, pv in (zt_pending if tt == 15 else zt_pending[:1]):
                def dst4(dc0, g, ptt=ptt):
                    return zT[:].rearrange("p (dc t) -> p dc t", t=2048)[
                        :, dc0:dc0 + g, ptt * 128:(ptt + 1) * 128]
                vt, _ = transpose_tile_set(zt[ptt % 2], dst4, None,
                                           waits_first=[(sSE, pv)])
                zt_free[ptt % 2] = (sTE, vt)
            zt_pending = zt_pending[1:] if tt != 15 else []
    v_zT = sVE.v
    v_y1_written = sY1W.v
    if phase == "B1":
        P.emit("sync", None, waits=[(sY1W, v_y1_written)])
        return _finish(nc, P, es)
    b2barrier = [(sTE, sTE.v), (sVE, sVE.v), (sSE, sSE.v), (sGE, sGE.v),
                 (sY1W, v_y1_written)]

    wqg_free = [None, None]
    wgd_free = None
    wud_free = None
    WgT_free = None
    WuT_free = None
    gbuf_free = None
    ubuf_free = None
    tbuf_free = None
    abuf_free = None
    hbuf_free = [None, None]
    wqgu_free = [None, None]
    dqv = {}

    def b2_dq(hc, extra):
        nonlocal wgd_free, wud_free
        p = hc % 2
        vg = dma("sync", wqg[p][:], wq_gate[hc * 128:(hc + 1) * 128, :],
                 sWQ[p], waits=extra + [wqg_free[p]])
        vu = dma("sync", wqgu[p][:], wq_up[hc * 128:(hc + 1) * 128, :],
                 sWU[p], waits=extra + [wqgu_free[p]])
        vdq_g = se(lambda E, p=p, hc=hc:
                   E.activation(out=wgd[:], in_=wqg[p][:], func=AF.Identity,
                                bias=nbg[:, hc:hc + 1], scale=sg2[:, hc:hc + 1]),
                   waits=extra + [(sWQ[p], vg), (sVE, v_nb), wgd_free],
                   inc=True)
        wqg_free[p] = (sSE, vdq_g)
        vdq_u = se(lambda E, p=p, hc=hc:
                   E.activation(out=wud[:], in_=wqgu[p][:], func=AF.Identity,
                                bias=nbu[:, hc:hc + 1], scale=su2[:, hc:hc + 1]),
                   waits=[(sWU[p], vu), wud_free], inc=True)
        wqgu_free[p] = (sSE, vdq_u)
        dqv[hc] = (vdq_g, vdq_u)

    b2_dq(0, b2barrier)
    for hc in range(HC):
        vdq_g, vdq_u = dqv[hc]

        def dstg(dc0, g):
            return WgT[:].rearrange("p (dc t) -> p dc t", t=128)[
                :, dc0:dc0 + g, :]
        vtg, _ = transpose_tile_set(wgd, dstg, None,
                                    waits_first=[(sSE, vdq_g), WgT_free])
        wgd_free = (sTE, vtg)
        vWgT = sVE.v

        def dstu(dc0, g):
            return WuT[:].rearrange("p (dc t) -> p dc t", t=128)[
                :, dc0:dc0 + g, :]
        vtu, _ = transpose_tile_set(wud, dstu, None,
                                    waits_first=[(sSE, vdq_u), WuT_free])
        wud_free = (sTE, vtu)
        vWuT = sVE.v
        if hc + 1 < HC:
            b2_dq(hc + 1, [])

        for tb in range(4):
            bank = 4 + tb
            for dc in range(16):
                w = []
                if dc == 0:
                    w = [(sVE, vWgT), (sVE, v_zT), bank_free[bank]]
                vt = te(lambda E, dc=dc, tb=tb, bank=bank:
                        E.matmul(out=PB(bank),
                                 lhsT=WgT[:, dc * 128:(dc + 1) * 128],
                                 rhs=zT[:, dc * 2048 + tb * 512:
                                        dc * 2048 + (tb + 1) * 512],
                                 start=(dc == 0), stop=(dc == 15)),
                        waits=w, inc=(dc == 15))
            vev = ve(lambda E, tb=tb, bank=bank:
                     E.tensor_copy(out=gbuf[:, tb * 512:(tb + 1) * 512],
                                   in_=PB(bank)),
                     waits=[(sTE, vt), gbuf_free], inc=True)
            bank_free[bank] = (sVE, vev)
        WgT_free = (sTE, vt)
        v_g = sVE.v
        for tb in range(4):
            bank = 4 + tb
            for dc in range(16):
                w = []
                if dc == 0:
                    w = [(sVE, vWuT), bank_free[bank]]
                vt = te(lambda E, dc=dc, tb=tb, bank=bank:
                        E.matmul(out=PB(bank),
                                 lhsT=WuT[:, dc * 128:(dc + 1) * 128],
                                 rhs=zT[:, dc * 2048 + tb * 512:
                                        dc * 2048 + (tb + 1) * 512],
                                 start=(dc == 0), stop=(dc == 15)),
                        waits=w, inc=(dc == 15))
            vev = ve(lambda E, tb=tb, bank=bank:
                     E.tensor_copy(out=ubuf[:, tb * 512:(tb + 1) * 512],
                                   in_=PB(bank)),
                     waits=[(sTE, vt), ubuf_free], inc=True)
            bank_free[bank] = (sVE, vev)
        WuT_free = (sTE, vt)
        v_u = sVE.v
        v_t = se(lambda E: E.activation(out=tbuf[:], in_=gbuf[:], func=AF.Tanh,
                                        bias=np.float32(0.0),
                                        scale=np.float32(0.5)),
                 waits=[(sVE, v_g), tbuf_free], inc=True)
        va = ve(lambda E: E.tensor_mul(out=abuf[:], in0=gbuf[:], in1=tbuf[:]),
                waits=[(sSE, v_t), abuf_free], inc=True)
        vhp = ve(lambda E: E.tensor_add(out=abuf[:], in0=abuf[:], in1=gbuf[:]),
                 waits=[(sVE, va)], inc=True)
        gbuf_free = (sVE, vhp)
        tbuf_free = (sVE, vhp)
        vh = ve(lambda E, p=hc % 2: E.scalar_tensor_tensor(out=hbuf[p][:],
                                                      in0=abuf[:], scalar=0.5,
                                                      in1=ubuf[:],
                                                      op0=ALU.mult,
                                                      op1=ALU.mult),
                waits=[(sVE, v_u), (sVE, vhp), hbuf_free[hc % 2]], inc=True)
        abuf_free = (sVE, vh)
        ubuf_free = (sVE, vh)
        vwh = dma("scalar", h_d[hc * 128:(hc + 1) * 128, :], hbuf[hc % 2][:],
                  sHW[hc % 2], waits=[(sVE, vh)])
        hbuf_free[hc % 2] = (sHW[hc % 2], vwh)
    v_hw = [sHW[0].v, sHW[1].v]
    if phase == "B2":
        P.emit("sync", None, waits=[(sHW[0], v_hw[0]), (sHW[1], v_hw[1]),
                                    (sY1W, v_y1_written)])
        return _finish(nc, P, es)
    b3barrier = [(sTE, sTE.v), (sVE, sVE.v), (sSE, sSE.v),
                 (sHW[0], v_hw[0]), (sHW[1], v_hw[1]), (sY1W, v_y1_written)]

    ve(lambda E: E.memset(sff[:], 0.0), waits=b3barrier)
    v_sffz = ve(lambda E: E.memset(sff2[:], 0.0), inc=True)
    wqd_free = [None, None]
    wdq_free_box = [None]
    WdT_half_free = [None, None]
    sff_add_rel = None
    hld_free = [None, None]
    fob_free = [None, None]
    acc_free = None
    wdt_ready = {}

    def b3_build(u, extra):
        half = u % 2
        for dt in range(4):
            row0 = u * 512 + dt * 128
            sidx = u * 4 + dt
            vl0 = dma("sync", wqd[0][:], wq_down[row0:row0 + 128, 0:2816],
                      sWD[0], waits=extra + [wqd_free[0]])
            vl1 = dma("sync", wqd[1][:], wq_down[row0:row0 + 128, 2816:5632],
                      sWD[1], waits=extra + [wqd_free[1]])
            vdq0 = se(lambda E, sidx=sidx:
                      E.activation(out=wdq[:, 0:2816], in_=wqd[0][:],
                                   func=AF.Identity, bias=nbd[:, sidx:sidx + 1],
                                   scale=sd2[:, sidx:sidx + 1]),
                      waits=extra + [(sWD[0], vl0), wdq_free_box[0]],
                      inc=True)
            vdq1 = se(lambda E, sidx=sidx:
                      E.activation(out=wdq[:, 2816:5632], in_=wqd[1][:],
                                   func=AF.Identity, bias=nbd[:, sidx:sidx + 1],
                                   scale=sd2[:, sidx:sidx + 1]),
                      waits=[(sWD[1], vl1)], inc=True)
            wqd_free[0] = (sSE, vdq0)
            wqd_free[1] = (sSE, vdq1)

            def dstw(dc0, g, dt=dt, half=half):
                return WdT[:].rearrange("p (hcx x) -> p hcx x", x=1024)[
                    :, dc0:dc0 + g, half * 512 + dt * 128:
                    half * 512 + (dt + 1) * 128]
            vt, _ = transpose_tile_set(wdq, dstw, None,
                                       waits_first=[(sSE, vdq1),
                                                    WdT_half_free[half]],
                                       ndc=HC)
            wdq_free_box[0] = (sTE, vt)
        wdt_ready[u] = sVE.v

    b3_build(0, b3barrier)
    for u in range(4):
        half = u % 2
        if u + 1 < 4:
            b3_build(u + 1, [])
        for tt in range(16):
            p = tt % 2
            hsrc = bass.AP(h_d.tensor, tt * 128,
                           [[TQ, 128], [128 * TQ, HC], [1, 128]])
            hdst = bass.AP(hld[p], 0, [[HC * 128, 128], [128, HC], [1, 128]])
            vlh = dma("sync", hdst, hsrc, sHL[p],
                      waits=[(sHW[0], v_hw[0]), (sHW[1], v_hw[1]),
                             hld_free[p]])
            bank = 4 + ((u * 16 + tt) % 4)
            dcol0 = u * 512
            for hcc in range(HC):
                w = []
                if hcc == 0:
                    w = [(sHL[p], vlh), (sVE, wdt_ready[u]), bank_free[bank]]
                vt = te(lambda E, p=p, hcc=hcc, half=half, bank=bank:
                        E.matmul(out=PB(bank),
                                 lhsT=hld[p][:, hcc * 128:(hcc + 1) * 128],
                                 rhs=WdT[:, hcc * 1024 + half * 512:
                                        hcc * 1024 + (half + 1) * 512],
                                 start=(hcc == 0), stop=(hcc == HC - 1)),
                        waits=w, inc=(hcc == HC - 1))
            vps = ve(lambda E, bank=bank:
                     E.reduce_sum(out=gtmp[:, 6:7], in_=PB(bank), axis=AX.X),
                     waits=[(sTE, vt), (sVE, v_sffz), sff_add_rel], inc=True)
            sff_add_v = ve(lambda E, tt=tt:
                           E.tensor_add(out=sff[:, tt:tt + 1],
                                        in0=sff[:, tt:tt + 1],
                                        in1=gtmp[:, 6:7]),
                           waits=[(sVE, vps)], inc=True)
            sff_add_rel = (sVE, sff_add_v)
            fp = tt % 2
            vev = ve(lambda E, bank=bank, fp=fp:
                     E.tensor_copy(out=fob[fp][:], in_=PB(bank)),
                     waits=[fob_free[fp]], inc=True)
            vsq = se(lambda E, bank=bank:
                     E.activation(out=junk[:, 0:512], in_=PB(bank),
                                  func=AF.Square, accum_out=gtmp[:, 0:1]),
                     waits=[(sVE, vev), acc_free], inc=True)
            vacc = ve(lambda E, tt=tt:
                      E.tensor_add(out=sff2[:, tt:tt + 1],
                                   in0=sff2[:, tt:tt + 1],
                                   in1=gtmp[:, 0:1]),
                      waits=[(sSE, vsq)], inc=True)
            acc_free = (sVE, vacc)
            bank_free[bank] = (sSE, vsq)
            vwf = dma("scalar", fo_d[tt * 128:(tt + 1) * 128,
                                     dcol0:dcol0 + 512], fob[fp][:],
                      sFW[fp], waits=[(sVE, vev)])
            fob_free[fp] = (sFW[fp], vwf)
            hld_free[p] = (sTE, sTE.v)
        WdT_half_free[half] = (sTE, sTE.v)
    v_fw = [sFW[0].v, sFW[1].v]
    v_b3_v = sVE.v
    if phase == "B3":
        P.emit("sync", None, waits=[(sFW[0], v_fw[0]), (sFW[1], v_fw[1]),
                                    (sY1W, v_y1_written)])
        return _finish(nc, P, es)

    fin_free = [None, None]
    y1in_free = [None, None]
    yout_free = [None, None]
    junk_free = None
    for tt in range(16):
        p = tt % 2
        vlf = dma("sync", fin[p][:], fo_d[tt * 128:(tt + 1) * 128, :], sFI[p],
                  waits=[(sFW[0], v_fw[0]), (sFW[1], v_fw[1]), fin_free[p]])
        vly = dma("sync", y1in[p][:], y1_d[tt * 128:(tt + 1) * 128, :], sYI[p],
                  waits=[(sY1W, v_y1_written), y1in_free[p]])
        st = tt % 2
        c = st * 8
        smu = stats[:, c + 2:c + 3]
        svar = stats[:, c + 3:c + 4]
        slnv = stats[:, c + 4:c + 5]
        srstd = stats[:, c + 5:c + 6]
        srstdv = stats[:, c + 6:c + 7]
        snmr = stats[:, c + 7:c + 8]
        wstat = [(sVE, v_b3_v), stats_free[st]]
        vsm = ve(lambda E, tt=tt, smu=smu:
                 E.tensor_scalar_mul(out=smu, in0=sff[:, tt:tt + 1],
                                     scalar1=INV_D),
                 waits=wstat, inc=True)
        vsv = ve(lambda E, tt=tt, svar=svar:
                 E.tensor_scalar_mul(out=svar, in0=sff2[:, tt:tt + 1],
                                     scalar1=INV_D), inc=True)
        vmm = ve(lambda E, smu=smu: E.tensor_mul(out=gtmp[:, 1:2], in0=smu,
                                                 in1=smu),
                 waits=[(sVE, vsm)], inc=True)
        v1 = ve(lambda E, svar=svar:
                E.scalar_tensor_tensor(out=svar, in0=svar, scalar=EPS,
                                       in1=gtmp[:, 1:2], op0=ALU.add,
                                       op1=ALU.subtract),
                waits=[(sVE, vmm)], inc=True)
        vln4 = se(lambda E, svar=svar, slnv=slnv:
                  E.activation(out=slnv, in_=svar, func=AF.Ln),
                  waits=[(sVE, v1)], inc=True)
        v2 = se(lambda E, slnv=slnv, srstd=srstd:
                E.activation(out=srstd, in_=slnv, func=AF.Exp,
                             bias=np.float32(0.0), scale=np.float32(-0.5)),
                waits=[(sSE, vln4)], inc=True)
        ve(lambda E, srstd=srstd, srstdv=srstdv:
           E.tensor_copy(out=srstdv, in_=srstd), waits=[(sSE, v2)])
        v3 = ve(lambda E, smu=smu, srstd=srstd, snmr=snmr:
                E.scalar_tensor_tensor(out=snmr, in0=smu, scalar=-1.0,
                                       in1=srstd, op0=ALU.mult, op1=ALU.mult),
                inc=True)
        v4 = se(lambda E, p=p, snmr=snmr, srstdv=srstdv:
                E.activation(out=junk[:], in_=fin[p][:], func=AF.Identity,
                             bias=snmr, scale=srstdv),
                waits=[(sVE, v3), (sFI[p], vlf), junk_free], inc=True)
        stats_free[st] = (sSE, v4)
        fin_free[p] = (sSE, v4)
        vy2 = ve(lambda E, p=p: E.tensor_add(out=yout[p][:], in0=y1in[p][:],
                                             in1=junk[:]),
                 waits=[(sSE, v4), (sYI[p], vly), yout_free[p]], inc=True)
        junk_free = (sVE, vy2)
        y1in_free[p] = (sVE, vy2)
        vst = dma("scalar", out_d[tt * 128:(tt + 1) * 128, :], yout[p][:],
                  sOW[p], waits=[(sVE, vy2)])
        yout_free[p] = (sOW[p], vst)

    P.emit("sync", None, waits=[(sOW[0], sOW[0].v), (sOW[1], sOW[1].v)])
    return _finish(nc, P, es)


def _finish(nc, P, es):
    with nc.Block() as block:
        @block.sync
        def _(E):
            P.run("sync", E)

        @block.tensor
        def _(E):
            P.run("tensor", E)

        @block.vector
        def _(E):
            P.run("vector", E)

        @block.scalar
        def _(E):
            P.run("scalar", E)

        @block.gpsimd
        def _(E):
            P.run("gpsimd", E)

    es.close()
    return nc


def kernel(L0_attn_query, L0_attn_key, L0_attn_value, L0_attn_Wqkv,
           L0_attn_Wproj, L0_attn_bproj, L0_ffn_wq_gate, L0_ffn_s_gate,
           L0_ffn_wq_up, L0_ffn_s_up, L0_ffn_wq_down, L0_ffn_s_down):
    xq = np.ascontiguousarray(np.asarray(L0_attn_query, dtype=np.float32))
    Wqkv = np.ascontiguousarray(np.asarray(L0_attn_Wqkv, dtype=np.float32))
    Wproj = np.ascontiguousarray(np.asarray(L0_attn_Wproj, dtype=np.float32))
    bprojv = np.ascontiguousarray(np.asarray(L0_attn_bproj, dtype=np.float32))
    wqg = np.ascontiguousarray(np.asarray(L0_ffn_wq_gate, dtype=np.int32))
    wqu = np.ascontiguousarray(np.asarray(L0_ffn_wq_up, dtype=np.int32))
    wqdn = np.ascontiguousarray(np.asarray(L0_ffn_wq_down, dtype=np.int32))
    sg = np.asarray(L0_ffn_s_gate, dtype=np.float32)
    su = np.asarray(L0_ffn_s_up, dtype=np.float32)
    sd = np.asarray(L0_ffn_s_down, dtype=np.float32)

    sg2 = np.ascontiguousarray(sg.reshape(HC, 128).T)
    su2 = np.ascontiguousarray(su.reshape(HC, 128).T)
    sd2 = np.ascontiguousarray(sd.reshape(16, 128).T)
    ident = np.eye(128, dtype=ml_dtypes.bfloat16)

    nc = build_nc()
    in_maps = []
    for c in range(8):
        j, half = c // 2, c % 2
        xs = []
        for a in range(NB):
            xs.append(xq[a, half * TQ:(half + 1) * TQ])
            xs.append(xq[a, (1 - half) * TQ:(2 - half) * TQ])
        x_all = np.ascontiguousarray(np.concatenate(xs, axis=0))
        x_own_c = np.ascontiguousarray(xq[j, half * TQ:(half + 1) * TQ])
        wqkv_sl = np.ascontiguousarray(np.concatenate(
            [Wqkv[(4 * r + j) * 384:(4 * r + j + 1) * 384] for r in range(R)],
            axis=0))
        in_maps.append({
            "x_all": x_all, "x_own": x_own_c, "wqkv_s": wqkv_sl,
            "wproj": Wproj, "bproj": bprojv,
            "wq_gate": wqg, "wq_up": wqu, "wq_down": wqdn,
            "s_gate2": sg2, "s_up2": su2, "s_down2": sd2,
            "ident": ident,
        })
    res = run_bass_kernel_spmd(nc, in_maps, core_ids=list(range(8)))
    out = np.empty((NB, S, D), dtype=np.float32)
    for c in range(8):
        j, half = c // 2, c % 2
        out[j, half * TQ:(half + 1) * TQ] = res.results[c]["out"]
    return out


# version 10
# speedup vs baseline: 1.2727x; 1.0032x over previous
import numpy as np
import ml_dtypes
import concourse.bass as bass
import concourse.mybir as mybir
from concourse.bass_utils import run_bass_kernel_spmd

F32 = mybir.dt.float32
BF16 = mybir.dt.bfloat16
I32 = mybir.dt.int32
AF = mybir.ActivationFunctionType
ALU = mybir.AluOpType
AX = mybir.AxisListType

NB = 4
S = 4096
D = 2048
R = 4
HID = 5632
TQ = 2048
NORM = 1.0 / np.sqrt(128.0)
EPS = 1e-5
NSTRIP = 16
QB = 256
NQB = TQ // QB
NKC = S // 128
HC = HID // 128
INV_D = 1.0 / D
SB_BASE = 18432
SBUF_LIMIT = 229120


class Sem:
    def __init__(self, nc, es, name):
        self.name = name
        self.h = es.enter_context(nc.semaphore(name))
        self.v = 0

    def bump(self, n):
        self.v += n
        return self.v


class Prog:
    ENGS = ("sync", "tensor", "vector", "scalar", "gpsimd")

    def __init__(self):
        self.streams = {e: [] for e in self.ENGS}
        self.hwm = {e: {} for e in self.ENGS}

    def emit(self, eng, fn, waits=(), incs=()):
        ws = []
        for w in waits:
            if w is None:
                continue
            s, v = w
            if v is None:
                continue
            if self.hwm[eng].get(s.name, -1) >= v:
                continue
            self.hwm[eng][s.name] = v
            ws.append((s, v))
        self.streams[eng].append((tuple(ws), fn, tuple(incs)))

    def run(self, eng_name, E):
        for ws, fn, incs in self.streams[eng_name]:
            for s, v in ws:
                E.wait_ge(s.h, v)
            if fn is None:
                continue
            ins = fn(E)
            for s, n in incs:
                ins = ins.then_inc(s.h, n)


def build_nc(phase="full"):
    nc = bass.Bass("TRN2", target_bir_lowering=False)
    from contextlib import ExitStack
    es = ExitStack()

    x_all = nc.dram_tensor("x_all", [NB * S, D], F32, kind="ExternalInput").ap()
    x_own = nc.dram_tensor("x_own", [TQ, D], F32, kind="ExternalInput").ap()
    wqkv_s = nc.dram_tensor("wqkv_s", [R * 384, D], F32, kind="ExternalInput").ap()
    wproj = nc.dram_tensor("wproj", [D, D], F32, kind="ExternalInput").ap()
    bproj = nc.dram_tensor("bproj", [D], F32, kind="ExternalInput").ap()
    wq_gate = nc.dram_tensor("wq_gate", [HID, D], I32, kind="ExternalInput").ap()
    wq_up = nc.dram_tensor("wq_up", [HID, D], I32, kind="ExternalInput").ap()
    wq_down = nc.dram_tensor("wq_down", [D, HID], I32, kind="ExternalInput").ap()
    s_gate2 = nc.dram_tensor("s_gate2", [128, HC], F32, kind="ExternalInput").ap()
    s_up2 = nc.dram_tensor("s_up2", [128, HC], F32, kind="ExternalInput").ap()
    s_down2 = nc.dram_tensor("s_down2", [128, 16], F32, kind="ExternalInput").ap()
    ident_in = nc.dram_tensor("ident", [128, 128], BF16, kind="ExternalInput").ap()
    out_d = nc.dram_tensor("out", [TQ, D], F32, kind="ExternalOutput").ap()

    _k = lambda p: "ExternalOutput" if phase == p else "Internal"
    ctx_d = nc.dram_tensor("ctx_d", [TQ, D], BF16, kind=_k("A")).ap()
    h_d = nc.dram_tensor("h_d", [HID, TQ], BF16, kind=_k("B2")).ap()
    y1_d = nc.dram_tensor("y1_d", [TQ, D], F32, kind=_k("B1")).ap()
    fo_d = nc.dram_tensor("fo_d", [TQ, D], BF16, kind=_k("B3")).ap()

    OFF = [SB_BASE]

    def sb(name, shape, dtype):
        nbytes = int(np.prod(shape[1:])) * mybir.dt.size(dtype)
        o = (OFF[0] + 31) // 32 * 32
        h = nc.alloc_sbuf_tensor_at(name, shape, dtype, offset=o)
        OFF[0] = o + nbytes
        assert OFF[0] <= SBUF_LIMIT, f"sbuf overflow at {name}: {OFF[0]}"
        return h

    identS = sb("identS", [128, 128], BF16)
    ones128 = sb("ones128", [128, 128], BF16)
    sg2 = sb("sg2", [128, HC], F32)
    su2 = sb("su2", [128, HC], F32)
    sd2 = sb("sd2", [128, 16], F32)
    nbg = sb("nbg", [128, HC], F32)
    nbu = sb("nbu", [128, HC], F32)
    nbd = sb("nbd", [128, 16], F32)
    stats = sb("stats", [128, 16], F32)
    ssq2 = sb("ssq2", [128, 2], F32)
    maxK = sb("maxK", [128, 64], F32)
    maxQ = sb("maxQ", [128, 32], F32)
    gtmp = sb("gtmp", [128, 8], F32)
    dcol = sb("dcol", [128, 2], F32)
    rdcol = sb("rdcol", [128, 2], F32)
    sff = sb("sff", [128, 16], F32)
    sff2 = sb("sff2", [128, 16], F32)
    PERS_END = OFF[0]

    wqkvT = sb("wqkvT", [128, 16 * 1536], BF16)
    Kst = sb("Kst", [128, R * S], BF16)
    Vst = sb("Vst", [128, 32 * 520], BF16)
    Qst = sb("Qst", [128, R * TQ], BF16)
    Pt = [sb("Pt0", [128, NKC * QB], BF16), sb("Pt1", [128, NKC * QB], BF16)]
    xin = [sb("xin0", [128, 2048], F32), sb("xin1", [128, 2048], F32)]
    xln = [sb("xln0", [128, 2048], BF16), sb("xln1", [128, 2048], BF16)]
    xT = [sb("xT0", [128, 16 * 256], BF16), sb("xT1", [128, 16 * 256], BF16)]
    Ksqb = sb("Ksqb", [128, 512], BF16)
    ctxs = [sb("ctxs0", [128, 128], BF16), sb("ctxs1", [128, 128], BF16)]
    assert OFF[0] <= SBUF_LIMIT

    OFF[0] = PERS_END
    junk = sb("junk", [128, 2048], BF16)
    BJ_END = OFF[0]
    WprojT = sb("WprojT", [128, 16 * 2048], BF16)
    zT = sb("zT", [128, 16 * 2048], BF16)
    bpb = sb("bpb", [128, 2048], F32)
    B_COMMON_END = OFF[0]

    B1W_START = OFF[0]
    wpl_ = sb("wpl0", [128, 2048], F32)
    wpl = [wpl_, wpl_]
    ctxin = [sb("ctxin0", [128, 2048], BF16), sb("ctxin1", [128, 2048], BF16)]
    ctxTb = [sb("ctxT0", [128, 2048], BF16), sb("ctxT1", [128, 2048], BF16)]
    attnout = sb("attnout", [128, 2048], F32)
    xo_ = sb("xo0", [128, 2048], F32)
    xo = [xo_, xo_]
    y1t = sb("y1t", [128, 2048], F32)
    zt = [sb("zt0", [128, 2048], BF16), sb("zt1", [128, 2048], BF16)]
    ln3t = sb("ln3t", [128, 2048], BF16)
    wplb = ln3t
    assert OFF[0] <= SBUF_LIMIT

    OFF[0] = BJ_END
    wqg = [sb("wqg0", [128, 2048], I32), sb("wqg1", [128, 2048], I32)]
    wgd = sb("wgd", [128, 2048], BF16)
    wud = sb("wud", [128, 2048], BF16)
    WgT = sb("WgT", [128, 2048], BF16)
    WuT = sb("WuT", [128, 2048], BF16)
    gbuf = sb("gbuf", [128, 2048], BF16)
    tbuf = sb("tbuf", [128, 2048], BF16)
    ubuf = sb("ubuf", [128, 2048], BF16)
    abuf = sb("abuf", [128, 2048], BF16)
    hbuf = [sb("hbuf0", [128, 2048], BF16), sb("hbuf1", [128, 2048], BF16)]
    assert OFF[0] <= BJ_END + 16 * 2048 * 2, "B2 must fit inside WprojT region"
    OFF[0] = B1W_START
    wqgu = [sb("wqgu0", [128, 2048], I32), sb("wqgu1", [128, 2048], I32)]

    OFF[0] = BJ_END
    WdT = sb("WdT", [128, HC * 1024], BF16)
    wqd = [sb("wqd0", [128, 2816], I32), sb("wqd1", [128, 2816], I32)]
    wdq = sb("wdq", [128, HID], BF16)
    hld = [sb("hld0", [128, HC * 128], BF16), sb("hld1", [128, HC * 128], BF16)]
    fob = [sb("fob0", [128, 512], BF16), sb("fob1", [128, 512], BF16)]
    fin = [sb("fin0", [128, 2048], BF16), sb("fin1", [128, 2048], BF16)]
    y1in = [sb("y1in0", [128, 2048], F32), sb("y1in1", [128, 2048], F32)]
    yout = [sb("yout0", [128, 2048], F32), sb("yout1", [128, 2048], F32)]
    assert OFF[0] <= SBUF_LIMIT

    pbT = [nc.alloc_psum_tensor(f"pbT{i}", [128, 1024], F32) for i in range(2)]
    pb4 = nc.alloc_psum_tensor("pb4", [128, 512], F32)
    pb5 = nc.alloc_psum_tensor("pb5", [128, 512], F32)
    pbS = nc.alloc_psum_tensor("pbS", [128, 1024], F32)

    def PB(slot, a=0, b=512):
        if slot == 4:
            return pb4[:, a:b]
        if slot == 5:
            return pb5[:, a:b]
        if slot == 6:
            return pbS[:, a:b]
        return pbS[:, 512 + a:512 + b]

    sWA = Sem(nc, es, "sWA")
    sX = [Sem(nc, es, "sX0"), Sem(nc, es, "sX1")]
    sWPL = [Sem(nc, es, "sWPL0"), Sem(nc, es, "sWPL1")]
    sXO = Sem(nc, es, "sXO")
    sBPB = Sem(nc, es, "sBPB")
    sCI = [Sem(nc, es, "sCI0"), Sem(nc, es, "sCI1")]
    sCW = [Sem(nc, es, "sCW0"), Sem(nc, es, "sCW1")]
    sWQ = [Sem(nc, es, "sWQ0"), Sem(nc, es, "sWQ1")]
    sWU = [Sem(nc, es, "sWU0"), Sem(nc, es, "sWU1")]
    sWD = [Sem(nc, es, "sWD0"), Sem(nc, es, "sWD1")]
    sHW = [Sem(nc, es, "sHW0"), Sem(nc, es, "sHW1")]
    sHL = [Sem(nc, es, "sHL0"), Sem(nc, es, "sHL1")]
    sFW = [Sem(nc, es, "sFW0"), Sem(nc, es, "sFW1")]
    sFI = [Sem(nc, es, "sFI0"), Sem(nc, es, "sFI1")]
    sYI = [Sem(nc, es, "sYI0"), Sem(nc, es, "sYI1")]
    sY1W = Sem(nc, es, "sY1W")
    sOW = [Sem(nc, es, "sOW0"), Sem(nc, es, "sOW1")]
    sTE = Sem(nc, es, "sTE")
    sVE = Sem(nc, es, "sVE")
    sSE = Sem(nc, es, "sSE")
    sGE = Sem(nc, es, "sGE")

    P = Prog()

    def dma(eng, out, in_, sem, waits=()):
        P.emit(eng, lambda E, o=out, i=in_: E.dma_start(out=o, in_=i),
               waits=waits, incs=((sem, 16),))
        return sem.bump(16)

    def te(fn, waits=(), inc=False):
        P.emit("tensor", fn, waits=waits, incs=((sTE, 1),))
        return sTE.bump(1)

    def ve(fn, waits=(), inc=False):
        P.emit("vector", fn, waits=waits, incs=((sVE, 1),) if inc else ())
        return sVE.bump(1) if inc else None

    def se(fn, waits=(), inc=False):
        P.emit("scalar", fn, waits=waits, incs=((sSE, 1),) if inc else ())
        return sSE.bump(1) if inc else None

    def ge(fn, waits=(), inc=False):
        P.emit("gpsimd", fn, waits=waits, incs=((sGE, 1),) if inc else ())
        return sGE.bump(1) if inc else None

    def vdr():
        P.emit("vector", lambda E: E.drain())

    def sdr():
        P.emit("scalar", lambda E: E.drain())

    bank_free = [None] * 8

    trot = [0]

    def transpose_tile_set(srcbuf, dstfn, banks_unused, waits_first,
                           extra_evac_wait=None, ndc=16, koff=0):
        vt = None
        done = 0
        while done < ndc:
            g = min(8, ndc - done)
            bank = trot[0] % 2
            trot[0] += 1
            for q in range(g):
                dc = done + q
                w = []
                if done == 0 and q == 0:
                    w = list(waits_first)
                if q == 0:
                    w = w + [bank_free[bank]]
                vt = te(lambda E, dc=dc, bank=bank, q=q:
                        E.matmul(out=pbT[bank][:, q * 128:(q + 1) * 128],
                                 lhsT=srcbuf[:, koff + dc * 128: koff + (dc + 1) * 128],
                                 rhs=identS[:], start=True, stop=True),
                        waits=w)
            ew = [(sTE, vt)]
            if extra_evac_wait is not None:
                ew.append(extra_evac_wait)
            vv = ve(lambda E, dst=dstfn(done, g), bank=bank, g=g:
                    E.tensor_copy(out=dst,
                                  in_=pbT[bank][:].rearrange(
                                      "p (c t) -> p c t", t=128)[:, 0:g, :]),
                    waits=ew, inc=True)
            bank_free[bank] = (sVE, vv)
            done += g
        return vt, sVE.v

    def norm_mm(rhs_ap, waits, width=256):
        bank = trot[0] % 2
        trot[0] += 1
        vt = te(lambda E, bank=bank, width=width:
                E.matmul(out=pbT[bank][:, 0:width], lhsT=ones128[:], rhs=rhs_ap,
                         start=True, stop=True),
                waits=list(waits) + [bank_free[bank]])
        return bank, vt

    v_ident = dma("sync", identS[:], ident_in, sWA)
    v_sg = dma("sync", sg2[:], s_gate2, sWA)
    v_su = dma("sync", su2[:], s_up2, sWA)
    v_sd = dma("sync", sd2[:], s_down2, sWA)

    ve(lambda E: E.memset(ones128[:], 1.0))
    ve(lambda E: E.tensor_scalar_mul(out=nbg[:], in0=sg2[:], scalar1=-128.0),
       waits=[(sWA, 64)])
    ve(lambda E: E.tensor_scalar_mul(out=nbu[:], in0=su2[:], scalar1=-128.0))
    v_nb = ve(lambda E: E.tensor_scalar_mul(out=nbd[:], in0=sd2[:], scalar1=-128.0),
              inc=True)

    xin_free = [None, None]
    xln_free = [None, None]
    for ob in range(12):
        p = ob % 2
        vl = dma("sync", xin[p][:], wqkv_s[ob * 128:(ob + 1) * 128, :], sX[p],
                 waits=[xin_free[p]])
        vc = ve(lambda E, p=p: E.tensor_copy(out=xln[p][:], in_=xin[p][:]),
                waits=[(sX[p], vl), xln_free[p], (sWA, 64)], inc=True)
        xin_free[p] = (sVE, vc)

        def dst4(dc0, g, ob=ob):
            return wqkvT[:].rearrange("p (dc o) -> p dc o", o=1536)[
                :, dc0:dc0 + g, ob * 128:(ob + 1) * 128]
        vt, _ = transpose_tile_set(xln[p], dst4, None,
                                   waits_first=[(sVE, vc)])
        xln_free[p] = (sTE, vt)
    v_wqkvT = sVE.v

    kst_free = None
    vst_free = None
    pt_free = [None, None]
    ctxs_free = [None, None]
    stats_free = [None, None]
    xT_free = [None, None]
    exp_reg_rel = {0: None, 1: None}
    ctx_bank_rel = {4: None, 5: None}
    gbias_free = None
    ksqb_free = [None, None]

    def ln_chain(tile_idx, src_ap, dst_ap, data_ready, dst_free=None, scr=None):
        if scr is None:
            scr = junk[:]
        st = tile_idx % 2
        c = st * 8
        ssum = stats[:, c + 0:c + 1]
        ssq = stats[:, c + 1:c + 2]
        smu = stats[:, c + 2:c + 3]
        smu2 = stats[:, c + 3:c + 4]
        snm2e = stats[:, c + 4:c + 5]
        srstd = stats[:, c + 5:c + 6]
        snegr = stats[:, c + 6:c + 7]
        snmr = stats[:, c + 7:c + 8]
        wfree = [stats_free[st]] + ([dst_free] if dst_free else [])
        vq = se(lambda E: E.activation(out=scr, in_=src_ap, func=AF.Square,
                                       accum_out=ssq),
                waits=list(data_ready) + wfree, inc=True)
        vs = se(lambda E: E.activation(out=scr, in_=src_ap, func=AF.Identity,
                                       accum_out=ssum),
                waits=[(sSE, vq)], inc=True)
        v1 = se(lambda E: E.mul(out=smu, in_=ssum, mul=INV_D),
                waits=[(sSE, vs)], inc=True)
        v6 = se(lambda E: E.mul(out=snegr, in_=smu, mul=-1.0),
                waits=[(sSE, v1)], inc=True)
        v2 = se(lambda E: E.activation(out=smu2, in_=smu, func=AF.Square),
                waits=[(sSE, v6)], inc=True)
        v3 = se(lambda E: E.activation(out=snm2e, in_=smu2, func=AF.Identity,
                                       bias=np.float32(EPS),
                                       scale=np.float32(-1.0)),
                waits=[(sSE, v2)], inc=True)
        v4 = se(lambda E: E.activation(out=srstd, in_=ssq, func=AF.Ln,
                                       bias=snm2e, scale=np.float32(INV_D)),
                waits=[(sSE, v3)], inc=True)
        v5 = se(lambda E: E.activation(out=srstd, in_=srstd, func=AF.Exp,
                                       bias=np.float32(0.0),
                                       scale=np.float32(-0.5)),
                waits=[(sSE, v4)], inc=True)
        v7 = se(lambda E: E.activation(out=snmr, in_=snegr, func=AF.Identity,
                                       bias=np.float32(0.0), scale=srstd),
                waits=[(sSE, v5)], inc=True)
        v8 = se(lambda E: E.activation(out=dst_ap, in_=src_ap, func=AF.Identity,
                                       bias=snmr, scale=srstd),
                waits=[(sSE, v7)], inc=True)
        stats_free[st] = (sSE, v8)
        return v8

    a_chain = {}
    a_xtr = {}

    def a1_chain(a, n, k):
        tp = (n * 2 + k) % 2
        row0 = a * S + n * 256
        vl = dma("sync", xin[tp][:],
                 x_all[row0 + k * 128: row0 + (k + 1) * 128, :], sX[tp],
                 waits=[xin_free[tp]])
        v_ln = ln_chain(n * 2 + k, xin[tp][:], xln[tp][:],
                        [(sX[tp], vl)], dst_free=xln_free[tp],
                        scr=xln[tp][:])
        xin_free[tp] = (sSE, v_ln)
        a_chain[(n, k)] = v_ln

    def a1_trans(n, k):
        tp = (n * 2 + k) % 2
        p = n % 2

        def dst4(dc0, g, p=p, k=k):
            return xT[p][:].rearrange("p (dc t) -> p dc t", t=256)[
                :, dc0:dc0 + g, k * 128:(k + 1) * 128]
        vt, _ = transpose_tile_set(xln[tp], dst4, None,
                                   waits_first=[(sSE, a_chain[(n, k)])],
                                   extra_evac_wait=xT_free[p])
        xln_free[tp] = (sTE, vt)
        if k == 1:
            a_xtr[n] = sVE.v

    for a in range(NB):
        v_vmset = ve(lambda E: E.memset(Vst[:], 1.0), waits=[vst_free], inc=True)
        for k in range(2):
            a1_chain(a, 0, k)
            a1_trans(0, k)
        for n in range(NSTRIP):
            p = n % 2
            xT_ready = a_xtr[n]
            if n + 1 < NSTRIP:
                a1_chain(a, n + 1, 0)
                a1_chain(a, n + 1, 1)

            for r in range(R):
                bank = 4 + r
                for dc in range(16):
                    w = []
                    if dc == 0:
                        w = [(sVE, xT_ready), bank_free[bank]]
                    vt = te(lambda E, r=r, dc=dc, bank=bank, p=p:
                            E.matmul(out=PB(bank, 0, 256),
                                     lhsT=wqkvT[:, dc * 1536 + r * 384 + 128:
                                                dc * 1536 + r * 384 + 256],
                                     rhs=xT[p][:, dc * 256:(dc + 1) * 256],
                                     start=(dc == 0), stop=(dc == 15)),
                            waits=w, inc=(dc == 15))
                vek = ve(lambda E, r=r, n=n, bank=bank:
                         E.tensor_copy(out=Kst[:, r * S + n * 256: r * S + (n + 1) * 256],
                                       in_=PB(bank, 0, 256)),
                         waits=[(sTE, vt), kst_free], inc=True)
                bank_free[bank] = (sVE, vek)
                if n % 2 == 1:
                    vsq = ve(lambda E, r=r, n=n:
                             E.tensor_mul(out=Ksqb[:],
                                          in0=Kst[:, r * S + (n - 1) * 256:
                                                 r * S + (n + 1) * 256],
                                          in1=Kst[:, r * S + (n - 1) * 256:
                                                 r * S + (n + 1) * 256]),
                             waits=[(sVE, vek), ksqb_free[0]], inc=True)
                    nbank, vn = norm_mm(Ksqb[:], [(sVE, vsq)], 512)
                    vm = ve(lambda E, r=r, n=n, nbank=nbank:
                            E.reduce_max(out=maxK[:, r * 8 + n // 2:
                                               r * 8 + n // 2 + 1],
                                         in_=pbT[nbank][:, 0:512], axis=AX.X),
                            waits=[(sTE, vn)], inc=True)
                    bank_free[nbank] = (sVE, vm)
                    ksqb_free[0] = (sTE, vn)
            if n + 1 < NSTRIP:
                a1_trans(n + 1, 0)

            if n < 8:
                for r in range(R):
                    bank = 4 + r
                    for dc in range(16):
                        w = []
                        if dc == 0:
                            w = [bank_free[bank]]
                        vt = te(lambda E, r=r, dc=dc, bank=bank, p=p:
                                E.matmul(out=PB(bank, 0, 256),
                                         lhsT=wqkvT[:, dc * 1536 + r * 384:
                                                    dc * 1536 + r * 384 + 128],
                                         rhs=xT[p][:, dc * 256:(dc + 1) * 256],
                                         start=(dc == 0), stop=(dc == 15)),
                                waits=w, inc=(dc == 15))
                    veq = ve(lambda E, r=r, n=n, bank=bank:
                             E.tensor_copy(out=Qst[:, r * TQ + n * 256:
                                                  r * TQ + (n + 1) * 256],
                                           in_=PB(bank, 0, 256)),
                             waits=[(sTE, vt), kst_free], inc=True)
                    bank_free[bank] = (sVE, veq)
                    if n % 2 == 1:
                        vsq = ve(lambda E, r=r, n=n:
                                 E.tensor_mul(out=Ksqb[:],
                                              in0=Qst[:, r * TQ + (n - 1) * 256:
                                                     r * TQ + (n + 1) * 256],
                                              in1=Qst[:, r * TQ + (n - 1) * 256:
                                                     r * TQ + (n + 1) * 256]),
                                 waits=[(sVE, veq), ksqb_free[1]], inc=True)
                        nbank, vn = norm_mm(Ksqb[:], [(sVE, vsq)], 512)
                        vm = ve(lambda E, r=r, n=n, nbank=nbank:
                                E.reduce_max(out=maxQ[:, r * 4 + n // 2:
                                                   r * 4 + n // 2 + 1],
                                             in_=pbT[nbank][:, 0:512],
                                             axis=AX.X),
                                waits=[(sTE, vn)], inc=True)
                        bank_free[nbank] = (sVE, vm)
                        ksqb_free[1] = (sTE, vn)
            if n + 1 < NSTRIP:
                a1_trans(n + 1, 1)

            for k in range(2):
                bank = 4 + k
                for r in range(R):
                    for dc in range(16):
                        w = []
                        if r == 0 and dc == 0:
                            w = [(sVE, v_vmset), bank_free[bank]]
                        vt = te(lambda E, k=k, r=r, dc=dc, bank=bank, p=p:
                                E.matmul(out=PB(bank, r * 128, (r + 1) * 128),
                                         lhsT=xT[p][:, dc * 256 + k * 128:
                                                   dc * 256 + k * 128 + 128],
                                         rhs=wqkvT[:, dc * 1536 + r * 384 + 256:
                                                   dc * 1536 + r * 384 + 384],
                                         start=(dc == 0), stop=(dc == 15)),
                                waits=w, inc=(r == 3 and dc == 15))
                tc = n * 2 + k
                dstv = Vst[:].rearrange("p (tc r x) -> p tc r x", r=4, x=130)[
                    :, tc, :, 0:128]
                vev = ve(lambda E, dstv=dstv, bank=bank:
                         E.tensor_copy(out=dstv,
                                       in_=PB(bank).rearrange(
                                           "p (r x) -> p r x", x=128)),
                         waits=[(sTE, vt)], inc=True)
                bank_free[bank] = (sVE, vev)
            xT_free[p] = (sTE, sTE.v)

        vst_ready = sVE.v
        v_norm_done = sVE.v
        for r in range(R):
            i_block = 4 * a + r
            vk1 = ve(lambda E, r=r: E.reduce_max(out=gtmp[:, 0:1],
                                                 in_=maxK[:, r * 8:(r + 1) * 8],
                                                 axis=AX.X),
                     waits=[gbias_free, (sVE, v_norm_done)], inc=True)
            vk2 = ve(lambda E, r=r: E.reduce_max(out=gtmp[:, 1:2],
                                                 in_=maxQ[:, r * 4:(r + 1) * 4],
                                                 axis=AX.X), inc=True)
            v1 = ve(lambda E: E.tensor_mul(out=gtmp[:, 2:3], in0=gtmp[:, 0:1],
                                           in1=gtmp[:, 1:2]),
                    waits=[(sVE, vk2)], inc=True)
            vln_g = se(lambda E: E.activation(out=gtmp[:, 3:4], in_=gtmp[:, 2:3],
                                              func=AF.Ln),
                       waits=[(sVE, v1)], inc=True)
            v2 = se(lambda E: E.activation(out=gtmp[:, 4:5], in_=gtmp[:, 3:4],
                                           func=AF.Exp, bias=np.float32(0.0),
                                           scale=np.float32(0.5)),
                    waits=[(sSE, vln_g)], inc=True)
            vbias = ve(lambda E: E.tensor_scalar_mul(out=gtmp[:, 5:6],
                                                     in0=gtmp[:, 4:5],
                                                     scalar1=-float(NORM)),
                       waits=[(sSE, v2)], inc=True)

            for qb in range(NQB):
                ptp = qb % 2
                for m2 in range(8):
                    reg = m2 % 2
                    rten = pbS if reg == 0 else pbT[1]
                    rslot = 6 if reg == 0 else 1
                    for kco in range(4):
                        kc = 4 * m2 + kco
                        w = [exp_reg_rel[reg]]
                        if m2 in (0, 1) and kco == 0:
                            w += [bank_free[rslot]]
                            if reg == 0:
                                w += [bank_free[7]]
                        if kc == 0:
                            w += [(sVE, vst_ready), (sVE, vbias)]
                        vt = te(lambda E, r=r, kc=kc, qb=qb, rten=rten:
                                E.matmul(out=rten[:, (kc % 4) * 256:
                                                  (kc % 4) * 256 + 256],
                                         lhsT=Kst[:, r * S + kc * 128:
                                                r * S + (kc + 1) * 128],
                                         rhs=Qst[:, r * TQ + qb * 256:
                                                r * TQ + (qb + 1) * 256],
                                         start=True, stop=True),
                                waits=w)
                    wx = [(sTE, vt)]
                    if m2 == 0:
                        wx += [pt_free[ptp]]
                    vxp = se(lambda E, m2=m2, rten=rten, ptp=ptp:
                             E.activation(out=Pt[ptp][:, m2 * 1024:
                                                     (m2 + 1) * 1024],
                                          in_=rten[:], func=AF.Exp,
                                          bias=gtmp[:, 5:6],
                                          scale=np.float32(NORM)),
                             waits=wx, inc=True)
                    exp_reg_rel[reg] = (sSE, vxp)
                exp_done = sSE.v
                for qt in range(2):
                    bank = 4 + qt
                    for kc in range(NKC):
                        w = []
                        if kc == 0:
                            w = [(sSE, exp_done), ctx_bank_rel[bank]]
                        vt = te(lambda E, r=r, qt=qt, kc=kc, bank=bank, ptp=ptp:
                                E.matmul(out=PB(bank, 0, 129),
                                         lhsT=Pt[ptp][:, kc * 256 + qt * 128:
                                                     kc * 256 + qt * 128 + 128],
                                         rhs=Vst[:, kc * 520 + r * 130:
                                                 kc * 520 + r * 130 + 129],
                                         start=(kc == 0), stop=(kc == NKC - 1)),
                                waits=w, inc=(kc == NKC - 1))
                    vd = ve(lambda E, bank=bank, qt=qt:
                            E.tensor_copy(out=dcol[:, qt:qt + 1],
                                          in_=PB(bank, 128, 129)),
                            waits=[(sTE, vt)], inc=True)
                    vr = ve(lambda E, qt=qt:
                            E.reciprocal(out=rdcol[:, qt:qt + 1],
                                         in_=dcol[:, qt:qt + 1]),
                            waits=[(sVE, vd)], inc=True)
                    tile = qb * 2 + qt
                    vnorm = se(lambda E, bank=bank, qt=qt:
                               E.activation(out=ctxs[qt][:],
                                            in_=PB(bank, 0, 128),
                                            func=AF.Identity,
                                            bias=np.float32(0.0),
                                            scale=rdcol[:, qt:qt + 1]),
                               waits=[(sVE, vr), ctxs_free[qt]], inc=True)
                    ctx_bank_rel[bank] = (sSE, vnorm)
                    vctx = dma("scalar",
                               ctx_d[tile * 128:(tile + 1) * 128,
                                     i_block * 128:(i_block + 1) * 128],
                               ctxs[qt][:], sCW[qt], waits=[(sSE, vnorm)])
                    ctxs_free[qt] = (sCW[qt], vctx)
                pt_free[ptp] = (sTE, sTE.v)
            gbias_free = (sSE, sSE.v)
        kst_free = (sTE, sTE.v)
        vst_free = (sTE, sTE.v)
        bank_free[4] = ctx_bank_rel[4]
        bank_free[5] = ctx_bank_rel[5]
        bank_free[6] = exp_reg_rel[0]
        bank_free[7] = exp_reg_rel[0]
        bank_free[1] = exp_reg_rel[1]

    if phase == "A":
        P.emit("sync", None, waits=[(sCW[0], sCW[0].v), (sCW[1], sCW[1].v)])
        return _finish(nc, P, es)
    barrier = [(sTE, sTE.v), (sVE, sVE.v), (sSE, sSE.v), (sGE, sGE.v),
               (sCW[0], sCW[0].v), (sCW[1], sCW[1].v)]

    rep_src = bass.AP(bproj.tensor, 0, [[0, 128], [1, 2048]])
    v_bpb = dma("sync", bpb[:], rep_src, sBPB, waits=barrier)

    zoff = zT.manual_sbuf_range[0]
    wplA = [nc.alloc_sbuf_tensor_at("wplA0", [128, 4 * 2048], F32, offset=zoff),
            nc.alloc_sbuf_tensor_at("wplA1", [128, 4 * 2048], F32,
                                    offset=zoff + 4 * 2048 * 4)]
    wplA_free = [None, None]
    wplb_free = None
    for blk in range(4):
        bp = blk % 2
        wsrc = bass.AP(wproj.tensor, blk * 512 * 2048,
                       [[2048, 128], [128 * 2048, 4], [1, 2048]])
        wdst = bass.AP(wplA[bp], 0, [[8192, 128], [2048, 4], [1, 2048]])
        vl = dma("sync", wdst, wsrc, sWPL[bp],
                 waits=barrier + [wplA_free[bp]])
        for t in range(4):
            ob = blk * 4 + t
            vc = ve(lambda E, bp=bp, t=t:
                    E.tensor_copy(out=wplb[:],
                                  in_=wplA[bp][:, t * 2048:(t + 1) * 2048]),
                    waits=barrier + [(sWPL[bp], vl), wplb_free], inc=True)

            def dst4(dc0, g, ob=ob):
                return WprojT[:].rearrange("p (dc o) -> p dc o", o=2048)[
                    :, dc0:dc0 + g, ob * 128:(ob + 1) * 128]
            vt, _ = transpose_tile_set(wplb, dst4, None,
                                       waits_first=barrier + [(sVE, vc)])
            wplb_free = (sTE, vt)
        wplA_free[bp] = (sVE, sVE.v)
    v_wprojT = sVE.v
    b0_ln3_guard = wplb_free

    all_ctx0 = sCW[0].v
    all_ctx1 = sCW[1].v
    ctxin_free = [None, None]
    ctxT_free = [None, None]
    xo_free = None
    attnout_free = None
    y1t_free = []
    zt_free = [None, None]
    zt_pending = []
    ln3_free = b0_ln3_guard
    vT_ready = {}

    def b1_stage1(tt, extra):
        p = tt % 2
        vl = dma("sync", ctxin[p][:], ctx_d[tt * 128:(tt + 1) * 128, :],
                 sCI[p], waits=extra + [ctxin_free[p]])

        def dst4(dc0, g, p=p):
            return ctxTb[p][:].rearrange("p (dc t) -> p dc t", t=128)[
                :, dc0:dc0 + g, :]
        vt, _ = transpose_tile_set(ctxin[p], dst4, None,
                                   waits_first=[(sCI[p], vl)],
                                   extra_evac_wait=ctxT_free[p])
        ctxin_free[p] = (sTE, vt)
        vT_ready[tt] = sVE.v

    b1_stage1(0, barrier + [(sCW[0], all_ctx0), (sCW[1], all_ctx1)])
    for tt in range(16):
        p = tt % 2
        if tt + 1 < 16:
            b1_stage1(tt + 1, [])
        vxo = dma("sync", xo[p][:], x_own[tt * 128:(tt + 1) * 128, :], sXO,
                  waits=barrier + [xo_free])
        for dob in range(4):
            bank = 4 + dob
            for dc in range(16):
                w = []
                if dc == 0:
                    w = [(sVE, vT_ready[tt]), (sVE, v_wprojT), bank_free[bank]]
                vt = te(lambda E, p=p, dob=dob, dc=dc, bank=bank:
                        E.matmul(out=PB(bank),
                                 lhsT=ctxTb[p][:, dc * 128:(dc + 1) * 128],
                                 rhs=WprojT[:, dc * 2048 + dob * 512:
                                           dc * 2048 + (dob + 1) * 512],
                                 start=(dc == 0), stop=(dc == 15)),
                        waits=w, inc=(dc == 15))
            vev = ve(lambda E, dob=dob, bank=bank:
                     E.tensor_tensor(out=attnout[:, dob * 512:(dob + 1) * 512],
                                     in0=PB(bank),
                                     in1=bpb[:, dob * 512:(dob + 1) * 512],
                                     op=ALU.add),
                     waits=[(sTE, vt), (sBPB, v_bpb), attnout_free], inc=True)
            bank_free[bank] = (sVE, vev)
        v_ao = sVE.v
        ctxT_free[p] = (sTE, sTE.v)
        v_ln3 = ln_chain(tt, attnout[:], ln3t[:], [(sVE, v_ao)],
                         dst_free=ln3_free)
        attnout_free = (sSE, v_ln3)
        vy1 = ve(lambda E, p=p: E.tensor_add(out=y1t[:], in0=xo[p][:],
                                             in1=ln3t[:]),
                 waits=[(sSE, v_ln3), (sXO, vxo)] + y1t_free, inc=True)
        ln3_free = (sVE, vy1)
        xo_free = (sVE, vy1)
        vw1 = dma("scalar", y1_d[tt * 128:(tt + 1) * 128, :], y1t[:], sY1W,
                  waits=[(sVE, vy1)])
        v_ln2 = ln_chain(tt, y1t[:], zt[tt % 2][:], [(sVE, vy1)],
                         dst_free=zt_free[tt % 2])
        y1t_free = [(sY1W, vw1), (sSE, v_ln2)]
        zt_pending.append((tt, v_ln2))
        if len(zt_pending) == 2 or tt == 15:
            for ptt, pv in (zt_pending if tt == 15 else zt_pending[:1]):
                def dst4(dc0, g, ptt=ptt):
                    return zT[:].rearrange("p (dc t) -> p dc t", t=2048)[
                        :, dc0:dc0 + g, ptt * 128:(ptt + 1) * 128]
                vt, _ = transpose_tile_set(zt[ptt % 2], dst4, None,
                                           waits_first=[(sSE, pv)])
                zt_free[ptt % 2] = (sTE, vt)
            zt_pending = zt_pending[1:] if tt != 15 else []
    v_zT = sVE.v
    v_y1_written = sY1W.v
    if phase == "B1":
        P.emit("sync", None, waits=[(sY1W, v_y1_written)])
        return _finish(nc, P, es)
    b2barrier = [(sTE, sTE.v), (sVE, sVE.v), (sSE, sSE.v), (sGE, sGE.v),
                 (sY1W, v_y1_written)]

    wqg_free = [None, None]
    wgd_free = None
    wud_free = None
    WgT_free = None
    WuT_free = None
    gbuf_free = None
    ubuf_free = None
    tbuf_free = None
    abuf_free = None
    hbuf_free = [None, None]
    wqgu_free = [None, None]
    dqv = {}

    def b2_dq(hc, extra):
        nonlocal wgd_free, wud_free
        p = hc % 2
        vg = dma("sync", wqg[p][:], wq_gate[hc * 128:(hc + 1) * 128, :],
                 sWQ[p], waits=extra + [wqg_free[p]])
        vu = dma("sync", wqgu[p][:], wq_up[hc * 128:(hc + 1) * 128, :],
                 sWU[p], waits=extra + [wqgu_free[p]])
        vdq_g = se(lambda E, p=p, hc=hc:
                   E.activation(out=wgd[:], in_=wqg[p][:], func=AF.Identity,
                                bias=nbg[:, hc:hc + 1], scale=sg2[:, hc:hc + 1]),
                   waits=extra + [(sWQ[p], vg), (sVE, v_nb), wgd_free],
                   inc=True)
        wqg_free[p] = (sSE, vdq_g)
        vdq_u = se(lambda E, p=p, hc=hc:
                   E.activation(out=wud[:], in_=wqgu[p][:], func=AF.Identity,
                                bias=nbu[:, hc:hc + 1], scale=su2[:, hc:hc + 1]),
                   waits=[(sWU[p], vu), wud_free], inc=True)
        wqgu_free[p] = (sSE, vdq_u)
        dqv[hc] = (vdq_g, vdq_u)

    b2_dq(0, b2barrier)
    for hc in range(HC):
        vdq_g, vdq_u = dqv[hc]

        def dstg(dc0, g):
            return WgT[:].rearrange("p (dc t) -> p dc t", t=128)[
                :, dc0:dc0 + g, :]
        vtg, _ = transpose_tile_set(wgd, dstg, None,
                                    waits_first=[(sSE, vdq_g), WgT_free])
        wgd_free = (sTE, vtg)
        vWgT = sVE.v

        def dstu(dc0, g):
            return WuT[:].rearrange("p (dc t) -> p dc t", t=128)[
                :, dc0:dc0 + g, :]
        vtu, _ = transpose_tile_set(wud, dstu, None,
                                    waits_first=[(sSE, vdq_u), WuT_free])
        wud_free = (sTE, vtu)
        vWuT = sVE.v
        if hc + 1 < HC:
            b2_dq(hc + 1, [])

        for tb in range(4):
            bank = 4 + tb
            for dc in range(16):
                w = []
                if dc == 0:
                    w = [(sVE, vWgT), (sVE, v_zT), bank_free[bank]]
                vt = te(lambda E, dc=dc, tb=tb, bank=bank:
                        E.matmul(out=PB(bank),
                                 lhsT=WgT[:, dc * 128:(dc + 1) * 128],
                                 rhs=zT[:, dc * 2048 + tb * 512:
                                        dc * 2048 + (tb + 1) * 512],
                                 start=(dc == 0), stop=(dc == 15)),
                        waits=w, inc=(dc == 15))
            vev = ve(lambda E, tb=tb, bank=bank:
                     E.tensor_copy(out=gbuf[:, tb * 512:(tb + 1) * 512],
                                   in_=PB(bank)),
                     waits=[(sTE, vt), gbuf_free], inc=True)
            bank_free[bank] = (sVE, vev)
        WgT_free = (sTE, vt)
        v_g = sVE.v
        for tb in range(4):
            bank = 4 + tb
            for dc in range(16):
                w = []
                if dc == 0:
                    w = [(sVE, vWuT), bank_free[bank]]
                vt = te(lambda E, dc=dc, tb=tb, bank=bank:
                        E.matmul(out=PB(bank),
                                 lhsT=WuT[:, dc * 128:(dc + 1) * 128],
                                 rhs=zT[:, dc * 2048 + tb * 512:
                                        dc * 2048 + (tb + 1) * 512],
                                 start=(dc == 0), stop=(dc == 15)),
                        waits=w, inc=(dc == 15))
            vev = ve(lambda E, tb=tb, bank=bank:
                     E.tensor_copy(out=ubuf[:, tb * 512:(tb + 1) * 512],
                                   in_=PB(bank)),
                     waits=[(sTE, vt), ubuf_free], inc=True)
            bank_free[bank] = (sVE, vev)
        WuT_free = (sTE, vt)
        v_u = sVE.v
        v_t = se(lambda E: E.activation(out=tbuf[:], in_=gbuf[:], func=AF.Tanh,
                                        bias=np.float32(0.0),
                                        scale=np.float32(0.5)),
                 waits=[(sVE, v_g), tbuf_free], inc=True)
        va = ve(lambda E: E.tensor_mul(out=abuf[:], in0=gbuf[:], in1=tbuf[:]),
                waits=[(sSE, v_t), abuf_free], inc=True)
        vhp = ve(lambda E: E.tensor_add(out=abuf[:], in0=abuf[:], in1=gbuf[:]),
                 waits=[(sVE, va)], inc=True)
        gbuf_free = (sVE, vhp)
        tbuf_free = (sVE, vhp)
        vh = ve(lambda E, p=hc % 2: E.scalar_tensor_tensor(out=hbuf[p][:],
                                                      in0=abuf[:], scalar=0.5,
                                                      in1=ubuf[:],
                                                      op0=ALU.mult,
                                                      op1=ALU.mult),
                waits=[(sVE, v_u), (sVE, vhp), hbuf_free[hc % 2]], inc=True)
        abuf_free = (sVE, vh)
        ubuf_free = (sVE, vh)
        vwh = dma("scalar", h_d[hc * 128:(hc + 1) * 128, :], hbuf[hc % 2][:],
                  sHW[hc % 2], waits=[(sVE, vh)])
        hbuf_free[hc % 2] = (sHW[hc % 2], vwh)
    v_hw = [sHW[0].v, sHW[1].v]
    if phase == "B2":
        P.emit("sync", None, waits=[(sHW[0], v_hw[0]), (sHW[1], v_hw[1]),
                                    (sY1W, v_y1_written)])
        return _finish(nc, P, es)
    b3barrier = [(sTE, sTE.v), (sVE, sVE.v), (sSE, sSE.v),
                 (sHW[0], v_hw[0]), (sHW[1], v_hw[1]), (sY1W, v_y1_written)]

    ve(lambda E: E.memset(sff[:], 0.0), waits=b3barrier)
    v_sffz = ve(lambda E: E.memset(sff2[:], 0.0), inc=True)
    wqd_free = [None, None]
    wdq_free_box = [None]
    WdT_half_free = [None, None]
    sff_add_rel = None
    hld_free = [None, None]
    fob_free = [None, None]
    acc_free = None
    wdt_ready = {}

    def b3_build(u, extra):
        half = u % 2
        for dt in range(4):
            row0 = u * 512 + dt * 128
            sidx = u * 4 + dt
            vl0 = dma("sync", wqd[0][:], wq_down[row0:row0 + 128, 0:2816],
                      sWD[0], waits=extra + [wqd_free[0]])
            vl1 = dma("sync", wqd[1][:], wq_down[row0:row0 + 128, 2816:5632],
                      sWD[1], waits=extra + [wqd_free[1]])
            vdq0 = se(lambda E, sidx=sidx:
                      E.activation(out=wdq[:, 0:2816], in_=wqd[0][:],
                                   func=AF.Identity, bias=nbd[:, sidx:sidx + 1],
                                   scale=sd2[:, sidx:sidx + 1]),
                      waits=extra + [(sWD[0], vl0), wdq_free_box[0]],
                      inc=True)
            vdq1 = se(lambda E, sidx=sidx:
                      E.activation(out=wdq[:, 2816:5632], in_=wqd[1][:],
                                   func=AF.Identity, bias=nbd[:, sidx:sidx + 1],
                                   scale=sd2[:, sidx:sidx + 1]),
                      waits=[(sWD[1], vl1)], inc=True)
            wqd_free[0] = (sSE, vdq0)
            wqd_free[1] = (sSE, vdq1)

            def dstw(dc0, g, dt=dt, half=half):
                return WdT[:].rearrange("p (hcx x) -> p hcx x", x=1024)[
                    :, dc0:dc0 + g, half * 512 + dt * 128:
                    half * 512 + (dt + 1) * 128]
            vt, _ = transpose_tile_set(wdq, dstw, None,
                                       waits_first=[(sSE, vdq1),
                                                    WdT_half_free[half]],
                                       ndc=HC)
            wdq_free_box[0] = (sTE, vt)
        wdt_ready[u] = sVE.v

    b3_build(0, b3barrier)
    for u in range(4):
        half = u % 2
        if u + 1 < 4:
            b3_build(u + 1, [])
        for tt in range(16):
            p = tt % 2
            hsrc = bass.AP(h_d.tensor, tt * 128,
                           [[TQ, 128], [128 * TQ, HC], [1, 128]])
            hdst = bass.AP(hld[p], 0, [[HC * 128, 128], [128, HC], [1, 128]])
            vlh = dma("sync", hdst, hsrc, sHL[p],
                      waits=[(sHW[0], v_hw[0]), (sHW[1], v_hw[1]),
                             hld_free[p]])
            bank = 4 + ((u * 16 + tt) % 4)
            dcol0 = u * 512
            for hcc in range(HC):
                w = []
                if hcc == 0:
                    w = [(sHL[p], vlh), (sVE, wdt_ready[u]), bank_free[bank]]
                vt = te(lambda E, p=p, hcc=hcc, half=half, bank=bank:
                        E.matmul(out=PB(bank),
                                 lhsT=hld[p][:, hcc * 128:(hcc + 1) * 128],
                                 rhs=WdT[:, hcc * 1024 + half * 512:
                                        hcc * 1024 + (half + 1) * 512],
                                 start=(hcc == 0), stop=(hcc == HC - 1)),
                        waits=w, inc=(hcc == HC - 1))
            vps = ve(lambda E, bank=bank:
                     E.reduce_sum(out=gtmp[:, 6:7], in_=PB(bank), axis=AX.X),
                     waits=[(sTE, vt), (sVE, v_sffz), sff_add_rel], inc=True)
            sff_add_v = ve(lambda E, tt=tt:
                           E.tensor_add(out=sff[:, tt:tt + 1],
                                        in0=sff[:, tt:tt + 1],
                                        in1=gtmp[:, 6:7]),
                           waits=[(sVE, vps)], inc=True)
            sff_add_rel = (sVE, sff_add_v)
            fp = tt % 2
            vev = ve(lambda E, bank=bank, fp=fp:
                     E.tensor_copy(out=fob[fp][:], in_=PB(bank)),
                     waits=[fob_free[fp]], inc=True)
            vsq = se(lambda E, bank=bank:
                     E.activation(out=junk[:, 0:512], in_=PB(bank),
                                  func=AF.Square, accum_out=gtmp[:, 0:1]),
                     waits=[(sVE, vev), acc_free], inc=True)
            vacc = ve(lambda E, tt=tt:
                      E.tensor_add(out=sff2[:, tt:tt + 1],
                                   in0=sff2[:, tt:tt + 1],
                                   in1=gtmp[:, 0:1]),
                      waits=[(sSE, vsq)], inc=True)
            acc_free = (sVE, vacc)
            bank_free[bank] = (sSE, vsq)
            vwf = dma("scalar", fo_d[tt * 128:(tt + 1) * 128,
                                     dcol0:dcol0 + 512], fob[fp][:],
                      sFW[fp], waits=[(sVE, vev)])
            fob_free[fp] = (sFW[fp], vwf)
            hld_free[p] = (sTE, sTE.v)
        WdT_half_free[half] = (sTE, sTE.v)
    v_fw = [sFW[0].v, sFW[1].v]
    v_b3_v = sVE.v
    if phase == "B3":
        P.emit("sync", None, waits=[(sFW[0], v_fw[0]), (sFW[1], v_fw[1]),
                                    (sY1W, v_y1_written)])
        return _finish(nc, P, es)

    fin_free = [None, None]
    y1in_free = [None, None]
    yout_free = [None, None]
    junk_free = None
    for tt in range(16):
        p = tt % 2
        vlf = dma("sync", fin[p][:], fo_d[tt * 128:(tt + 1) * 128, :], sFI[p],
                  waits=[(sFW[0], v_fw[0]), (sFW[1], v_fw[1]), fin_free[p]])
        vly = dma("sync", y1in[p][:], y1_d[tt * 128:(tt + 1) * 128, :], sYI[p],
                  waits=[(sY1W, v_y1_written), y1in_free[p]])
        st = tt % 2
        c = st * 8
        smu = stats[:, c + 2:c + 3]
        svar = stats[:, c + 3:c + 4]
        slnv = stats[:, c + 4:c + 5]
        srstd = stats[:, c + 5:c + 6]
        srstdv = stats[:, c + 6:c + 7]
        snmr = stats[:, c + 7:c + 8]
        wstat = [(sVE, v_b3_v), stats_free[st]]
        vsm = ve(lambda E, tt=tt, smu=smu:
                 E.tensor_scalar_mul(out=smu, in0=sff[:, tt:tt + 1],
                                     scalar1=INV_D),
                 waits=wstat, inc=True)
        vsv = ve(lambda E, tt=tt, svar=svar:
                 E.tensor_scalar_mul(out=svar, in0=sff2[:, tt:tt + 1],
                                     scalar1=INV_D), inc=True)
        vmm = ve(lambda E, smu=smu: E.tensor_mul(out=gtmp[:, 1:2], in0=smu,
                                                 in1=smu),
                 waits=[(sVE, vsm)], inc=True)
        v1 = ve(lambda E, svar=svar:
                E.scalar_tensor_tensor(out=svar, in0=svar, scalar=EPS,
                                       in1=gtmp[:, 1:2], op0=ALU.add,
                                       op1=ALU.subtract),
                waits=[(sVE, vmm)], inc=True)
        vln4 = se(lambda E, svar=svar, slnv=slnv:
                  E.activation(out=slnv, in_=svar, func=AF.Ln),
                  waits=[(sVE, v1)], inc=True)
        v2 = se(lambda E, slnv=slnv, srstd=srstd:
                E.activation(out=srstd, in_=slnv, func=AF.Exp,
                             bias=np.float32(0.0), scale=np.float32(-0.5)),
                waits=[(sSE, vln4)], inc=True)
        ve(lambda E, srstd=srstd, srstdv=srstdv:
           E.tensor_copy(out=srstdv, in_=srstd), waits=[(sSE, v2)])
        v3 = ve(lambda E, smu=smu, srstd=srstd, snmr=snmr:
                E.scalar_tensor_tensor(out=snmr, in0=smu, scalar=-1.0,
                                       in1=srstd, op0=ALU.mult, op1=ALU.mult),
                inc=True)
        v4 = se(lambda E, p=p, snmr=snmr, srstdv=srstdv:
                E.activation(out=junk[:], in_=fin[p][:], func=AF.Identity,
                             bias=snmr, scale=srstdv),
                waits=[(sVE, v3), (sFI[p], vlf), junk_free], inc=True)
        stats_free[st] = (sSE, v4)
        fin_free[p] = (sSE, v4)
        vy2 = ve(lambda E, p=p: E.tensor_add(out=yout[p][:], in0=y1in[p][:],
                                             in1=junk[:]),
                 waits=[(sSE, v4), (sYI[p], vly), yout_free[p]], inc=True)
        junk_free = (sVE, vy2)
        y1in_free[p] = (sVE, vy2)
        vst = dma("scalar", out_d[tt * 128:(tt + 1) * 128, :], yout[p][:],
                  sOW[p], waits=[(sVE, vy2)])
        yout_free[p] = (sOW[p], vst)

    P.emit("sync", None, waits=[(sOW[0], sOW[0].v), (sOW[1], sOW[1].v)])
    return _finish(nc, P, es)


def _finish(nc, P, es):
    with nc.Block() as block:
        @block.sync
        def _(E):
            P.run("sync", E)

        @block.tensor
        def _(E):
            P.run("tensor", E)

        @block.vector
        def _(E):
            P.run("vector", E)

        @block.scalar
        def _(E):
            P.run("scalar", E)

        @block.gpsimd
        def _(E):
            P.run("gpsimd", E)

    es.close()
    return nc


def kernel(L0_attn_query, L0_attn_key, L0_attn_value, L0_attn_Wqkv,
           L0_attn_Wproj, L0_attn_bproj, L0_ffn_wq_gate, L0_ffn_s_gate,
           L0_ffn_wq_up, L0_ffn_s_up, L0_ffn_wq_down, L0_ffn_s_down):
    xq = np.ascontiguousarray(np.asarray(L0_attn_query, dtype=np.float32))
    Wqkv = np.ascontiguousarray(np.asarray(L0_attn_Wqkv, dtype=np.float32))
    Wproj = np.ascontiguousarray(np.asarray(L0_attn_Wproj, dtype=np.float32))
    bprojv = np.ascontiguousarray(np.asarray(L0_attn_bproj, dtype=np.float32))
    wqg = np.ascontiguousarray(np.asarray(L0_ffn_wq_gate, dtype=np.int32))
    wqu = np.ascontiguousarray(np.asarray(L0_ffn_wq_up, dtype=np.int32))
    wqdn = np.ascontiguousarray(np.asarray(L0_ffn_wq_down, dtype=np.int32))
    sg = np.asarray(L0_ffn_s_gate, dtype=np.float32)
    su = np.asarray(L0_ffn_s_up, dtype=np.float32)
    sd = np.asarray(L0_ffn_s_down, dtype=np.float32)

    sg2 = np.ascontiguousarray(sg.reshape(HC, 128).T)
    su2 = np.ascontiguousarray(su.reshape(HC, 128).T)
    sd2 = np.ascontiguousarray(sd.reshape(16, 128).T)
    ident = np.eye(128, dtype=ml_dtypes.bfloat16)

    nc = build_nc()
    in_maps = []
    for c in range(8):
        j, half = c // 2, c % 2
        xs = []
        for a in range(NB):
            xs.append(xq[a, half * TQ:(half + 1) * TQ])
            xs.append(xq[a, (1 - half) * TQ:(2 - half) * TQ])
        x_all = np.ascontiguousarray(np.concatenate(xs, axis=0))
        x_own_c = np.ascontiguousarray(xq[j, half * TQ:(half + 1) * TQ])
        wqkv_sl = np.ascontiguousarray(np.concatenate(
            [Wqkv[(4 * r + j) * 384:(4 * r + j + 1) * 384] for r in range(R)],
            axis=0))
        in_maps.append({
            "x_all": x_all, "x_own": x_own_c, "wqkv_s": wqkv_sl,
            "wproj": Wproj, "bproj": bprojv,
            "wq_gate": wqg, "wq_up": wqu, "wq_down": wqdn,
            "s_gate2": sg2, "s_up2": su2, "s_down2": sd2,
            "ident": ident,
        })
    res = run_bass_kernel_spmd(nc, in_maps, core_ids=list(range(8)))
    out = np.empty((NB, S, D), dtype=np.float32)
    for c in range(8):
        j, half = c // 2, c % 2
        out[j, half * TQ:(half + 1) * TQ] = res.results[c]["out"]
    return out
